# version 07
# baseline (speedup 1.0000x reference)
import numpy as np
from contextlib import ExitStack

import concourse.bass as bass
import concourse.mybir as mybir
from concourse.bass_utils import run_bass_kernel_spmd

F32 = mybir.dt.float32
BF16 = mybir.dt.bfloat16
AF = mybir.ActivationFunctionType
ALU = mybir.AluOpType
AX = mybir.AxisListType

P = 128
D = 2048
NCD = D // P
SEQ = 4096
NT_FULL = SEQ // P
TOWN = 2048
NT_OWN = TOWN // P
EPS = 1e-5
SM_SCALE = 1.0 / float(np.sqrt(2048.0))
NEG = -1e30
INV_D = 1.0 / D
E = 16
I_EXP = 1408
NI_EXP = I_EXP // P
I_SH = 5632
NI_SH = I_SH // P

ALL_PHASES = ("p0", "p1", "p2", "p3", "p5")


class Ctx:
    pass


def build_nc(phases=ALL_PHASES, dbg=()):
    nc = bass.Bass("TRN2", target_bir_lowering=False)
    c = Ctx()
    c.nc = nc

    def dram(name, shape, dtype, kind="Internal"):
        k = "ExternalOutput" if (kind == "Internal" and name in dbg) else kind
        return nc.dram_tensor(name, shape, dtype, kind=k)

    c.x_in = dram("x", [SEQ, D], F32, kind="ExternalInput")
    c.qkv_in = dram("qkv", [D, 3 * D], F32, kind="ExternalInput")
    c.qkvh_in = dram("qkvh", [32, P, NCD, P], F32, kind="ExternalInput")
    c.wo_in = dram("wo", [D, D], F32, kind="ExternalInput")
    c.gamma_in = dram("gamma", [1, D], F32, kind="ExternalInput")
    c.beta_in = dram("beta", [1, D], F32, kind="ExternalInput")
    c.router_in = dram("router", [P, NCD * (E + 1)], F32, kind="ExternalInput")
    c.w13h_in = dram("w13h", [E, 22, P, NCD, P], F32, kind="ExternalInput")
    c.w2h_in = dram("w2h", [E, P, NI_EXP, D], F32, kind="ExternalInput")
    c.sguph_in = dram("sguph", [88, P, NCD, P], F32, kind="ExternalInput")
    c.sdh_in = dram("sdh", [4, P, NI_EXP, D], F32, kind="ExternalInput")
    c.csel_in = dram("csel", [1, 2], F32, kind="ExternalInput")

    c.out = dram("out", [TOWN, D], F32, kind="ExternalOutput")

    c.xn_s = dram("xn_s", [NCD, P, SEQ], BF16)
    c.r_s = dram("r_s", [TOWN, D], F32)
    c.q_s = dram("q_s", [NCD, P, SEQ], BF16)
    c.k_s = dram("k_s", [NCD, P, SEQ], BF16)
    c.v_s = dram("v_s", [SEQ, D], BF16)
    c.ao_s = dram("ao_s", [NCD, P, TOWN], BF16)
    c.x2_s = dram("x2_s", [NCD, P, TOWN], BF16)
    c.w13c = dram("w13c", [E, 22, P, NCD, P], BF16)
    c.sgupc = dram("sgupc", [88, P, NCD, P], BF16)

    st = ExitStack()
    c.st = st
    with st:
        c.ident_bf = st.enter_context(nc.sbuf_tensor("ident_bf", [P, P], BF16))
        c.ident_f32 = st.enter_context(nc.sbuf_tensor("ident_f32", [P, P], F32))
        c.cvec = st.enter_context(nc.sbuf_tensor("cvec", [P, 2], F32))
        c.mask_m = st.enter_context(nc.sbuf_tensor("mask_m", [P, 1024], F32))
        c.ones_row = st.enter_context(nc.sbuf_tensor("ones_row", [1, P], F32))
        c.epsv = st.enter_context(nc.sbuf_tensor("epsv", [P, 1], F32))

        c._semn = 0
        c.conv_count = 0
        c.convsem = st.enter_context(nc.semaphore("convsem"))

        c.coef_all = st.enter_context(nc.sbuf_tensor("coef_all", [P, 16 * E], F32))
        c.sg_all = st.enter_context(nc.sbuf_tensor("sg_all", [P, 16], F32))
        c.sgv = st.enter_context(nc.sbuf_tensor("sgv", [P, 16], F32))

        st01 = ExitStack()
        c.gamma_b = st01.enter_context(nc.sbuf_tensor("gamma_b", [P, D], F32))
        c.beta_b = st01.enter_context(nc.sbuf_tensor("beta_b", [P, D], F32))
        c.gb_row = st01.enter_context(nc.sbuf_tensor("gb_row", [1, 2 * D + 2], F32))
        emit_preamble(c)
        if "p0" in phases:
            emit_p0(c)
        st01.close()
        if "p1" in phases:
            emit_p1(c)
        if "p2" in phases:
            emit_p2(c)
        if "p3" in phases:
            emit_p3(c)
        if "p5" in phases:
            emit_p5(c)

    return nc


def emit_conv(c, g_, n):
    for _ in range(n):
        u = c.conv_count
        if u >= 20:
            return
        if u > 0:
            g_.wait_ge(c.convsem, 16 * u)
        if u < 16:
            srcv = c.w13h_in[u].rearrange("a p c j -> (a p) (c j)")
            dstv = c.w13c[u].rearrange("a p c j -> (a p) (c j)")
        else:
            q = u - 16
            srcv = c.sguph_in[q * 22:(q + 1) * 22].rearrange(
                "a p c j -> (a p) (c j)")
            dstv = c.sgupc[q * 22:(q + 1) * 22].rearrange(
                "a p c j -> (a p) (c j)")
        g_.dma_start(dstv, srcv).then_inc(c.convsem, 16)
        c.conv_count += 1


def mksems(c, names):
    d = {}
    for n in names:
        c._semn += 1
        d[n] = c.st.enter_context(c.nc.semaphore(f"{n}_{c._semn}"))
    return d


def emit_preamble(c):
    nc = c.nc
    sm = mksems(c, ["pm", "pld", "ppe", "pdv"])
    with (
        nc.psum_tensor("ps_bc", [P, D], F32) as ps_bc,
        nc.psum_tensor("ps_bc2", [P, 2], F32) as ps_bc2,
        nc.Block() as blk,
    ):
        @blk.gpsimd
        def _(g):
            g.memset(c.ident_bf[:], 0.0).then_inc(sm["pm"], 1)
            g.wait_ge(sm["pm"], 1)
            g.affine_select(out=c.ident_bf[:], in_=c.ident_bf[:],
                            compare_op=ALU.not_equal, fill=1.0, base=0,
                            pattern=[[-1, P]], channel_multiplier=1)
            g.memset(c.ident_f32[:], 0.0).then_inc(sm["pm"], 1)
            g.wait_ge(sm["pm"], 2)
            g.affine_select(out=c.ident_f32[:], in_=c.ident_f32[:],
                            compare_op=ALU.not_equal, fill=1.0, base=0,
                            pattern=[[-1, P]], channel_multiplier=1)
            g.memset(c.mask_m[:], 0.0).then_inc(sm["pm"], 1)
            g.wait_ge(sm["pm"], 3)
            g.affine_select(out=c.mask_m[:], in_=c.mask_m[:],
                            compare_op=ALU.is_ge, fill=NEG, base=512,
                            pattern=[[-1, 1024]], channel_multiplier=1)
            g.memset(c.ones_row[:], 1.0)
            g.memset(c.epsv[:], EPS).then_inc(sm["pm"], 1)

        @blk.sync
        def _(sy):
            sy.dma_start(c.gb_row[:, 0:D], c.gamma_in[:]).then_inc(sm["pld"], 16)
            sy.wait_ge(sm["pld"], 16)
            sy.dma_start(c.gb_row[:, D:2 * D], c.beta_in[:]).then_inc(sm["pld"], 16)
            sy.wait_ge(sm["pld"], 32)
            sy.dma_start(c.gb_row[:, 2 * D:2 * D + 2],
                         c.csel_in[:]).then_inc(sm["pld"], 16)

        @blk.tensor
        def _(te):
            te.wait_ge(sm["pld"], 48)
            te.wait_ge(sm["pm"], 4)
            for j in range(4):
                ins = te.matmul(ps_bc[:, j * 512:(j + 1) * 512], c.ones_row[:],
                                c.gb_row[:, j * 512:(j + 1) * 512],
                                start=True, stop=True)
            ins.then_inc(sm["ppe"], 1)
            te.wait_ge(sm["pdv"], 1)
            for j in range(4):
                te.matmul(ps_bc[:, j * 512:(j + 1) * 512], c.ones_row[:],
                          c.gb_row[:, D + j * 512:D + (j + 1) * 512],
                          start=True, stop=True)
            te.matmul(ps_bc2[:], c.ones_row[:],
                      c.gb_row[:, 2 * D:2 * D + 2],
                      start=True, stop=True).then_inc(sm["ppe"], 1)

        @blk.vector
        def _(v):
            v.wait_ge(sm["ppe"], 1)
            v.tensor_copy(c.gamma_b[:], ps_bc[:]).then_inc(sm["pdv"], 1)
            v.wait_ge(sm["ppe"], 2)
            v.tensor_copy(c.beta_b[:], ps_bc[:]).then_inc(sm["pdv"], 1)
            v.tensor_copy(c.cvec[:], ps_bc2[:]).then_inc(sm["pdv"], 1)


def emit_p0(c):
    nc = c.nc
    sm = mksems(c, ["ld0", "ld1", "rs0", "rs1", "xs0", "xs1",
                    "sa", "sd", "sp"])
    NDV = 16
    NACT = 6

    def dv(m, j):
        return m * NDV + j

    def act(m, j):
        return m * NACT + j

    S1, SSQ, MU, MUSQ, VAR, LNV, RSTD = 0, 1, 2, 3, 4, 5, 6
    S1B, SSQB, MU2, MUSQ2, VAR2, LNV2, RSTD2 = 7, 8, 9, 10, 11, 12, 13

    with (
        nc.sbuf_tensor("p0_xt", [P, 2 * D], F32) as xt,
        nc.sbuf_tensor("p0_scrap", [P, D], F32) as scrap,
        nc.sbuf_tensor("p0_sm", [P, 32], F32) as smv,
        nc.sbuf_tensor("p0_ln1", [P, 2 * D], F32) as ln1,
        nc.sbuf_tensor("p0_rb", [P, 2 * D], F32) as rb,
        nc.sbuf_tensor("p0_xnf", [P, 2 * D], F32) as xnf,
        nc.sbuf_tensor("p0_xnb", [P, 2 * D], BF16) as xnb,
        nc.sbuf_tensor("p0_stage", [P, 2 * 2048], BF16) as stage,
        nc.psum_tensor("p0_psT", [P, 2 * D], BF16) as psT,
        nc.Block() as blk,
    ):
        def smc(idx, s):
            return smv[:, 2 * idx + s:2 * idx + s + 1]

        @blk.sync
        def _(sy):
            for m in range(NT_FULL):
                s = m % 2
                ldm = sm["ld0"] if s == 0 else sm["ld1"]
                if m >= 2:
                    sy.wait_ge(sm["sd"], dv(m - 2, 7))
                    sy.wait_ge(sm["sa"], act(m - 2, 1))
                    sy.wait_ge(ldm, 16 * (m // 2))
                sy.dma_start(xt[:, s * D:(s + 1) * D],
                             c.x_in[m * P:(m + 1) * P, :]).then_inc(ldm, 16)

        @blk.scalar
        def _(a):
            for m in range(NT_FULL):
                s = m % 2
                ldm = sm["ld0"] if s == 0 else sm["ld1"]
                a.wait_ge(ldm, 16 * (m // 2 + 1))
                if m >= 2:
                    a.wait_ge(sm["sd"], dv(m - 2, 4))
                a.activation(scrap[:], xt[:, s * D:(s + 1) * D], AF.Square,
                             accum_out=smc(SSQ, s)).then_inc(sm["sa"], 1)
                a.wait_ge(sm["sd"], dv(m, 4))
                a.activation(smc(LNV, s), smc(VAR, s), AF.Ln,
                             bias=c.epsv[:]).then_inc(sm["sa"], 1)
                a.wait_ge(sm["sa"], act(m, 2))
                a.activation(smc(RSTD, s), smc(LNV, s), AF.Exp,
                             scale=-0.5).then_inc(sm["sa"], 1)
                a.wait_ge(sm["sd"], dv(m, 5))
                a.activation(scrap[:], ln1[:, s * D:(s + 1) * D], AF.Square,
                             accum_out=smc(SSQB, s)).then_inc(sm["sa"], 1)
                a.wait_ge(sm["sd"], dv(m, 11))
                a.activation(smc(LNV2, s), smc(VAR2, s), AF.Ln,
                             bias=c.epsv[:]).then_inc(sm["sa"], 1)
                a.wait_ge(sm["sa"], act(m, 5))
                a.activation(smc(RSTD2, s), smc(LNV2, s), AF.Exp,
                             scale=-0.5).then_inc(sm["sa"], 1)

        @blk.vector
        def _(v):
            for m in range(NT_FULL):
                s = m % 2
                ldm = sm["ld0"] if s == 0 else sm["ld1"]
                rsm = sm["rs0"] if s == 0 else sm["rs1"]
                xts = xt[:, s * D:(s + 1) * D]
                ln1s = ln1[:, s * D:(s + 1) * D]
                v.wait_ge(ldm, 16 * (m // 2 + 1))
                v.reduce_sum(smc(S1, s), xts, axis=AX.X).then_inc(sm["sd"], 1)
                v.wait_ge(sm["sd"], dv(m, 1))
                v.tensor_single_scalar(smc(MU, s), smc(S1, s), INV_D,
                                       ALU.mult).then_inc(sm["sd"], 1)
                v.wait_ge(sm["sd"], dv(m, 2))
                v.tensor_mul(smc(MUSQ, s), smc(MU, s),
                             smc(MU, s)).then_inc(sm["sd"], 1)
                v.wait_ge(sm["sa"], act(m, 1))
                v.wait_ge(sm["sd"], dv(m, 3))
                if m >= 2:
                    v.wait_ge(sm["sa"], act(m - 2, 2))
                v.tensor_scalar(smc(VAR, s), smc(SSQ, s), INV_D, smc(MUSQ, s),
                                ALU.mult, ALU.subtract).then_inc(sm["sd"], 1)
                v.wait_ge(sm["sa"], act(m, 3))
                if m >= 2:
                    v.wait_ge(sm["sa"], act(m - 2, 4))
                v.tensor_scalar(ln1s, xts, smc(MU, s), smc(RSTD, s),
                                ALU.subtract, ALU.mult).then_inc(sm["sd"], 1)
                v.wait_ge(sm["sd"], dv(m, 5))
                if m >= 2:
                    v.wait_ge(rsm, 16 * (m // 2))
                v.tensor_add(rb[:, s * D:(s + 1) * D], xts,
                             ln1s).then_inc(sm["sd"], 1)
                v.wait_ge(sm["sd"], dv(m, 6))
                v.tensor_scalar_mul(rb[:, s * D:(s + 1) * D],
                                    rb[:, s * D:(s + 1) * D],
                                    c.cvec[:, s:s + 1]).then_inc(sm["sd"], 1)
                v.reduce_sum(smc(S1B, s), ln1s, axis=AX.X).then_inc(sm["sd"], 1)
                v.wait_ge(sm["sd"], dv(m, 8))
                v.tensor_single_scalar(smc(MU2, s), smc(S1B, s), INV_D,
                                       ALU.mult).then_inc(sm["sd"], 1)
                v.wait_ge(sm["sd"], dv(m, 9))
                v.tensor_mul(smc(MUSQ2, s), smc(MU2, s),
                             smc(MU2, s)).then_inc(sm["sd"], 1)
                v.wait_ge(sm["sa"], act(m, 4))
                v.wait_ge(sm["sd"], dv(m, 10))
                if m >= 2:
                    v.wait_ge(sm["sa"], act(m - 2, 5))
                v.tensor_scalar(smc(VAR2, s), smc(SSQB, s), INV_D,
                                smc(MUSQ2, s), ALU.mult,
                                ALU.subtract).then_inc(sm["sd"], 1)
                v.wait_ge(sm["sa"], act(m, 6))
                v.tensor_scalar(xnf[:, s * D:(s + 1) * D], ln1s, smc(MU2, s),
                                smc(RSTD2, s), ALU.subtract,
                                ALU.mult).then_inc(sm["sd"], 1)
                v.wait_ge(sm["sd"], dv(m, 12))
                v.tensor_mul(xnf[:, s * D:(s + 1) * D],
                             xnf[:, s * D:(s + 1) * D],
                             c.gamma_b[:]).then_inc(sm["sd"], 1)
                v.wait_ge(sm["sd"], dv(m, 13))
                if m >= 2:
                    v.wait_ge(sm["sp"], (m - 2) * 2 + 2)
                v.tensor_add(xnb[:, s * D:(s + 1) * D],
                             xnf[:, s * D:(s + 1) * D],
                             c.beta_b[:]).then_inc(sm["sd"], 1)
                sg = m % 2
                xsm = sm["xs0"] if sg == 0 else sm["xs1"]
                col0 = sg * 2048
                if m >= 2:
                    v.wait_ge(xsm, 16 * (m // 2))
                v.wait_ge(sm["sp"], m * 2 + 1)
                v.tensor_copy(stage[:, col0:col0 + 1024],
                              psT[:, (m % 2) * D:(m % 2) * D + 1024]
                              ).then_inc(sm["sd"], 1)
                v.wait_ge(sm["sp"], m * 2 + 2)
                v.tensor_copy(stage[:, col0 + 1024:col0 + 2048],
                              psT[:, (m % 2) * D + 1024:(m % 2) * D + 2048]
                              ).then_inc(sm["sd"], 1)

        @blk.tensor
        def _(te):
            for m in range(NT_FULL):
                s = m % 2
                te.wait_ge(sm["sd"], dv(m, 14))
                if m >= 2:
                    te.wait_ge(sm["sd"], dv(m - 2, 16))
                for j in range(16):
                    ins = te.transpose(
                        psT[:, (m % 2) * D + j * P:(m % 2) * D + (j + 1) * P],
                        xnb[:, s * D + j * P:s * D + (j + 1) * P],
                        c.ident_bf[:])
                    if j == 7 or j == 15:
                        ins.then_inc(sm["sp"], 1)

        @blk.gpsimd
        def _(g_):
            for m in range(NT_FULL):
                if m % 4 == 3:
                    emit_conv(c, g_, 1)
                s = m % 2
                rsm = sm["rs0"] if s == 0 else sm["rs1"]
                g_.wait_ge(sm["sd"], dv(m, 7))
                g_.wait_ge(rsm, 16 * (m // 2))
                if s == 1:
                    g_.wait_ge(sm["rs0"], 16 * (m // 2 + 1))
                row0 = (m // 2) * P
                if s == 0:
                    g_.dma_start(c.r_s[row0:row0 + P, :],
                                 rb[:, 0:D]).then_inc(rsm, 16)
                else:
                    g_.dma_start(c.r_s[row0:row0 + P, :], rb[:, D:2 * D],
                                 accum_op=ALU.add).then_inc(rsm, 16)
                sg = m % 2
                xsm = sm["xs0"] if sg == 0 else sm["xs1"]
                g_.wait_ge(sm["sd"], dv(m, 16))
                g_.wait_ge(xsm, 16 * (m // 2))
                g_.dma_start(
                    c.xn_s[:, :, m * P:(m + 1) * P].rearrange("ch p j -> p ch j"),
                    stage[:, sg * 2048:(sg + 1) * 2048].rearrange(
                        "p (ch j) -> p ch j", ch=16, j=P),
                ).then_inc(xsm, 16)
            g_.wait_ge(sm["rs0"], 16 * 16)
            g_.wait_ge(sm["rs1"], 16 * 16)
            g_.wait_ge(sm["xs0"], 16 * 16)
            g_.wait_ge(sm["xs1"], 16 * 16)


def emit_p1(c):
    nc = c.nc

    sm = mksems(c, ["aw00", "aw01", "aw10", "aw11", "aw20", "aw21",
                    "aw30", "aw31", "ax0", "ax1", "ax2", "amm", "aev",
                    "ast0", "ast1"])

    def wsem(cc):
        return sm[f"aw{cc % 4}{(cc // 4) % 2}"]

    NT = 8

    with (
        nc.sbuf_tensor("p1_wqk", [P, 32 * D], BF16) as wqk,
        nc.sbuf_tensor("p1_xns", [P, 3 * 8192], BF16) as xns,
        nc.sbuf_tensor("p1_kst", [P, 2 * 2048], BF16) as kst,
        nc.psum_tensor("p1_ps", [P, 8 * 512], F32) as ps,
        nc.Block() as blk,
    ):
        @blk.gpsimd
        def _(g):
            for cc in range(32):
                if cc % 6 == 5:
                    emit_conv(c, g, 1)
                if cc >= 8:
                    g.wait_ge(wsem(cc), 16 * (cc // 8))
                g.dma_start(
                    wqk[:, cc * D:(cc + 1) * D],
                    c.qkvh_in[cc].rearrange("p dk j -> p (dk j)"),
                ).then_inc(wsem(cc), 16)
            emit_conv(c, g, 2)

        @blk.sync
        def _(sy):
            for t in range(NT):
                xs = t % 3
                xsm = sm[f"ax{xs}"]
                if t >= 3:
                    sy.wait_ge(xsm, 16 * (t // 3))
                    sy.wait_ge(sm["amm"], 32 * (t - 2))
                sy.dma_start(
                    xns[:, xs * 8192:(xs + 1) * 8192]
                    .rearrange("p (ch j) -> p ch j", ch=NCD, j=512),
                    c.xn_s[:, :, t * 512:(t + 1) * 512]
                    .rearrange("ch p j -> p ch j"),
                ).then_inc(xsm, 16)

        @blk.tensor
        def _(te):
            for t in range(NT):
                xs = t % 3
                xsm = sm[f"ax{xs}"]
                te.wait_ge(xsm, 16 * (t // 3 + 1))
                for cc in range(32):
                    if t == 0:
                        te.wait_ge(wsem(cc), 16 * (cc // 8 + 1))
                    i = t * 32 + cc
                    b = i % 8
                    if i >= 8:
                        te.wait_ge(sm["aev"], i - 7)
                    for dk in range(NCD):
                        ins = te.matmul(
                            ps[:, b * 512:(b + 1) * 512],
                            wqk[:, cc * D + dk * P:cc * D + (dk + 1) * P],
                            xns[:, xs * 8192 + dk * 512:
                                xs * 8192 + (dk + 1) * 512],
                            start=(dk == 0), stop=(dk == NCD - 1))
                    ins.then_inc(sm["amm"], 1)

        @blk.vector
        def _(v):
            for i in range(NT * 32):
                grp = i // 4
                si = grp % 2
                stm = sm["ast0"] if si == 0 else sm["ast1"]
                v.wait_ge(sm["amm"], i + 1)
                if i % 4 == 0 and grp >= 2:
                    v.wait_ge(stm, 16 * (grp // 2))
                b = i % 8
                v.tensor_copy(kst[:, si * 2048 + (i % 4) * 512:
                                  si * 2048 + (i % 4 + 1) * 512],
                              ps[:, b * 512:(b + 1) * 512]).then_inc(sm["aev"], 1)

        @blk.scalar
        def _(a):
            for t in range(NT):
                for cc4 in range(8):
                    grp = t * 8 + cc4
                    si = grp % 2
                    stm = sm["ast0"] if si == 0 else sm["ast1"]
                    a.wait_ge(sm["aev"], grp * 4 + 4)
                    a.wait_ge(stm, 16 * (grp // 2))
                    cc0 = cc4 * 4
                    dst = (c.q_s if cc0 < NCD else c.k_s)[
                        (cc0 % NCD):(cc0 % NCD) + 4, :,
                        t * 512:(t + 1) * 512].rearrange("ch p j -> p ch j")
                    a.dma_start(dst,
                                kst[:, si * 2048:(si + 1) * 2048].rearrange(
                                    "p (ch j) -> p ch j", ch=4, j=512)
                                ).then_inc(stm, 16)
            a.wait_ge(sm["ast0"], 16 * 32)
            a.wait_ge(sm["ast1"], 16 * 32)

    sm2 = mksems(c, ["vw", "vx0", "vx1", "vmm", "vev", "vst0", "vst1"])
    with (
        nc.sbuf_tensor("p1_wv", [P, NCD * D], BF16) as wv,
        nc.sbuf_tensor("p1_xnsb", [P, 2 * 8192], BF16) as xns,
        nc.sbuf_tensor("p1_vst", [P, 2 * D], BF16) as vst,
        nc.psum_tensor("p1_pv", [P, 2 * D], F32) as pv,
        nc.Block() as blk,
    ):
        @blk.gpsimd
        def _(g):
            g.dma_start(
                wv[:].rearrange("p (dk j) -> p dk j", dk=NCD, j=D),
                c.qkv_in[:, 2 * D:3 * D]
                .rearrange("(dk p) j -> p dk j", dk=NCD, p=P),
            ).then_inc(sm2["vw"], 16)
            emit_conv(c, g, 2)

        @blk.sync
        def _(sy):
            for t in range(NT):
                xs = t % 2
                xsm = sm2["vx0"] if xs == 0 else sm2["vx1"]
                if t >= 2:
                    sy.wait_ge(xsm, 16 * (t // 2))
                    sy.wait_ge(sm2["vmm"], 4 * (t - 1))
                sy.dma_start(
                    xns[:, xs * 8192:(xs + 1) * 8192]
                    .rearrange("p (ch j) -> p ch j", ch=NCD, j=512),
                    c.xn_s[:, :, t * 512:(t + 1) * 512]
                    .rearrange("ch p j -> p ch j"),
                ).then_inc(xsm, 16)

        @blk.tensor
        def _(te):
            te.wait_ge(sm2["vw"], 16)
            for t in range(NT):
                xs = t % 2
                xsm = sm2["vx0"] if xs == 0 else sm2["vx1"]
                te.wait_ge(xsm, 16 * (t // 2 + 1))
                for tc in range(4):
                    i = t * 4 + tc
                    pslot = i % 2
                    if i >= 2:
                        te.wait_ge(sm2["vev"], i - 1)
                    for n in range(4):
                        for dk in range(NCD):
                            ins = te.matmul(
                                pv[:, pslot * D + n * 512:pslot * D + (n + 1) * 512],
                                xns[:, xs * 8192 + dk * 512 + tc * P:
                                    xs * 8192 + dk * 512 + (tc + 1) * P],
                                wv[:, dk * D + n * 512:dk * D + (n + 1) * 512],
                                start=(dk == 0), stop=(dk == NCD - 1))
                    ins.then_inc(sm2["vmm"], 1)

        @blk.vector
        def _(v):
            for i in range(NT * 4):
                pslot = i % 2
                v.wait_ge(sm2["vmm"], i + 1)
                stm = sm2["vst0"] if pslot == 0 else sm2["vst1"]
                if i >= 2:
                    v.wait_ge(stm, 16 * (i // 2))
                v.tensor_copy(vst[:, pslot * D:(pslot + 1) * D],
                              pv[:, pslot * D:(pslot + 1) * D]
                              ).then_inc(sm2["vev"], 1)

        @blk.scalar
        def _(a):
            for i in range(NT * 4):
                pslot = i % 2
                stm = sm2["vst0"] if pslot == 0 else sm2["vst1"]
                a.wait_ge(sm2["vev"], i + 1)
                a.wait_ge(stm, 16 * (i // 2))
                a.dma_start(c.v_s[i * P:(i + 1) * P, :],
                            vst[:, pslot * D:(pslot + 1) * D]
                            ).then_inc(stm, 16)
            a.wait_ge(sm2["vst0"], 16 * 16)
            a.wait_ge(sm2["vst1"], 16 * 16)


def emit_p2(c):
    nc = c.nc
    sm = mksems(c, ["qg0", "qg1", "kb0", "kb1", "kb2", "vb0", "vb1", "vb2",
                    "amm", "aev", "ssm", "sact", "spk", "pev",
                    "bmm", "bev", "ao0", "ao1"])

    NG = 16

    def nblk(g):
        return (g + 2) // 2

    cumA = [0] * (NG + 1)
    cumB = [0] * (NG + 1)
    for g in range(NG):
        cumA[g + 1] = cumA[g] + nblk(g) * 2
        cumB[g + 1] = cumB[g] + nblk(g) * 8
    klist = [(g, j) for g in range(NG) for j in range(nblk(g))]
    NKL = len(klist)

    pk_of_tile = {}
    aopk_of_tile = {}
    pk = 0
    for g in range(NG):
        for q in range(2):
            m = 2 * g + q
            npk = (m + 1 + 7) // 8
            pk_of_tile[m] = (pk, npk)
            pk += npk
        for q in range(2):
            m = 2 * g + q
            aopk_of_tile[m] = pk
            pk += 2

    def ammv(g, j, q):
        return cumA[g] + j * 2 + q + 1

    def bmmv(g, jb, q, dvN):
        return cumB[g] + jb * 8 + q * 4 + dvN + 1

    def ssm_max(m):
        return 8 * (m // 2) + 2 * (m % 2) + 1

    def ssm_bias(m):
        return 8 * (m // 2) + 2 * (m % 2) + 2

    def ssm_recip(m):
        return 8 * (m // 2) + 4 + (m % 2) + 1

    def ssm_aosc(m):
        return 8 * (m // 2) + 6 + (m % 2) + 1

    with (
        nc.sbuf_tensor("p2_qg", [P, 2 * NCD * 256], BF16) as qg,
        nc.sbuf_tensor("p2_kb", [P, 3 * 8192], BF16) as kb,
        nc.sbuf_tensor("p2_vb", [P, 3 * 8192], BF16) as vb,
        nc.sbuf_tensor("p2_strip", [P, 2 * SEQ], F32) as strip,
        nc.sbuf_tensor("p2_pb", [P, SEQ], BF16) as pb,
        nc.sbuf_tensor("p2_pt", [P, 2 * SEQ], BF16) as pt,
        nc.sbuf_tensor("p2_aoacc", [P, 2 * D], F32) as aoacc,
        nc.sbuf_tensor("p2_aotm", [P, 2 * D], BF16) as aotm,
        nc.sbuf_tensor("p2_aost", [P, 2 * D], BF16) as aost,
        nc.sbuf_tensor("p2_linv", [P, 4], F32) as linv,
        nc.sbuf_tensor("p2_smm", [P, 8], F32) as smm,
        nc.psum_tensor("p2_psA", [P, 6 * 512], F32) as psA,
        nc.psum_tensor("p2_psT", [P, 2 * 1024], BF16) as psT,
        nc.Block() as blk,
    ):
        @blk.sync
        def _(sy):
            kcnt = 0
            vcnt = 0
            for g in range(NG):
                qs = g % 2
                qsm = sm["qg0"] if qs == 0 else sm["qg1"]
                if g >= 2:
                    sy.wait_ge(qsm, 16 * (g // 2))
                    sy.wait_ge(sm["amm"], cumA[g - 1])
                sy.dma_start(
                    qg[:, qs * NCD * 256:(qs + 1) * NCD * 256]
                    .rearrange("p (ch j) -> p ch j", ch=NCD, j=256),
                    c.q_s[:, :, g * 256:(g + 1) * 256]
                    .rearrange("ch p j -> p ch j"),
                ).then_inc(qsm, 16)
                for j in range(nblk(g)):
                    ks = kcnt % 3
                    ksm = sm[f"kb{ks}"]
                    if kcnt >= 3:
                        sy.wait_ge(ksm, 16 * (kcnt // 3))
                        gp, jp = klist[kcnt - 3]
                        sy.wait_ge(sm["amm"], cumA[gp] + jp * 2 + 2)
                    sy.dma_start(
                        kb[:, ks * 8192:(ks + 1) * 8192]
                        .rearrange("p (ch j) -> p ch j", ch=NCD, j=512),
                        c.k_s[:, :, j * 512:(j + 1) * 512]
                        .rearrange("ch p j -> p ch j"),
                    ).then_inc(ksm, 16)
                    kcnt += 1
                for jb in range(nblk(g)):
                    vs = vcnt % 3
                    vsm = sm[f"vb{vs}"]
                    if vcnt >= 3:
                        sy.wait_ge(vsm, 16 * (vcnt // 3))
                        gp, jp = klist[vcnt - 3]
                        sy.wait_ge(sm["bmm"], cumB[gp] + jp * 8 + 8)
                    sy.dma_start(
                        vb[:, vs * 8192:(vs + 1) * 8192]
                        .rearrange("p (tc d) -> p tc d", tc=4, d=D),
                        c.v_s[jb * 512:(jb + 1) * 512, :]
                        .rearrange("(tc p) d -> p tc d", tc=4, p=P),
                    ).then_inc(vsm, 16)
                    vcnt += 1

        @blk.gpsimd
        def _(g_):
            for g in range(NG):
                if g in (3, 6, 9, 12):
                    emit_conv(c, g_, 1)
                for q in range(2):
                    m = 2 * g + q
                    s = m % 2
                    aosm = sm["ao0"] if s == 0 else sm["ao1"]
                    g_.wait_ge(sm["pev"], aopk_of_tile[m] + 2)
                    g_.wait_ge(aosm, 16 * (m // 2))
                    if s == 1:
                        g_.wait_ge(sm["ao0"], 16 * (m // 2 + 1))
                    k = m // 2
                    dst = c.ao_s[:, :, k * P:(k + 1) * P].rearrange(
                        "ch p j -> p ch j")
                    srcv = aost[:, s * D:(s + 1) * D].rearrange(
                        "p (ch j) -> p ch j", ch=NCD, j=P)
                    if s == 0:
                        g_.dma_start(dst, srcv).then_inc(aosm, 16)
                    else:
                        g_.dma_start(dst, srcv,
                                     accum_op=ALU.add).then_inc(aosm, 16)
            g_.wait_ge(sm["ao0"], 16 * 16)
            g_.wait_ge(sm["ao1"], 16 * 16)

        @blk.tensor
        def _(te):
            kcnt = 0
            vcnt = 0
            for g in range(NG):
                qs = g % 2
                qsm = sm["qg0"] if qs == 0 else sm["qg1"]
                te.wait_ge(qsm, 16 * (g // 2 + 1))
                for j in range(nblk(g)):
                    ks = kcnt % 3
                    ksm = sm[f"kb{ks}"]
                    te.wait_ge(ksm, 16 * (kcnt // 3 + 1))
                    kcnt += 1
                    for q in range(2):
                        i = cumA[g] + j * 2 + q
                        b = i % 6
                        if i >= 6:
                            te.wait_ge(sm["aev"], i - 5)
                        for dk in range(NCD):
                            ins = te.matmul(
                                psA[:, b * 512:(b + 1) * 512],
                                qg[:, qs * NCD * 256 + dk * 256 + q * P:
                                   qs * NCD * 256 + dk * 256 + (q + 1) * P],
                                kb[:, ks * 8192 + dk * 512:
                                   ks * 8192 + (dk + 1) * 512],
                                start=(dk == 0), stop=(dk == NCD - 1))
                        ins.then_inc(sm["amm"], 1)
                for q in range(2):
                    m = 2 * g + q
                    pk0, npk = pk_of_tile[m]
                    te.wait_ge(sm["sact"], m + 1)
                    ntk = m + 1
                    for pkk in range(npk):
                        slot = (pk0 + pkk) % 2
                        if pk0 + pkk >= 2:
                            te.wait_ge(sm["pev"], pk0 + pkk - 1)
                        lo = pkk * 8
                        hi = min(lo + 8, ntk)
                        for tk in range(lo, hi):
                            ins = te.transpose(
                                psT[:, slot * 1024 + (tk - lo) * P:
                                    slot * 1024 + (tk - lo + 1) * P],
                                pb[:, tk * P:(tk + 1) * P],
                                c.ident_bf[:])
                        ins.then_inc(sm["spk"], 1)
                for jb in range(nblk(g)):
                    vs = vcnt % 3
                    vsm = sm[f"vb{vs}"]
                    te.wait_ge(vsm, 16 * (vcnt // 3 + 1))
                    vcnt += 1
                    for q in range(2):
                        m = 2 * g + q
                        pk0, npk = pk_of_tile[m]
                        te.wait_ge(sm["pev"], pk0 + npk)
                        for dvN in range(4):
                            i = cumB[g] + jb * 8 + q * 4 + dvN
                            b = i % 6
                            if i >= 6:
                                te.wait_ge(sm["bev"], i - 5)
                            tks = [tk for tk in range(jb * 4, jb * 4 + 4)
                                   if tk <= m]
                            for nn, tk in enumerate(tks):
                                ins = te.matmul(
                                    psA[:, b * 512:(b + 1) * 512],
                                    pt[:, q * SEQ + tk * P:
                                       q * SEQ + (tk + 1) * P],
                                    vb[:, vs * 8192 + (tk - jb * 4) * D +
                                       dvN * 512:
                                       vs * 8192 + (tk - jb * 4) * D +
                                       (dvN + 1) * 512],
                                    start=(nn == 0), stop=(nn == len(tks) - 1))
                            ins.then_inc(sm["bmm"], 1)
                for q in range(2):
                    m = 2 * g + q
                    te.wait_ge(sm["ssm"], ssm_aosc(m))
                    for half in range(2):
                        pkk = aopk_of_tile[m] + half
                        slot = pkk % 2
                        if pkk >= 2:
                            te.wait_ge(sm["pev"], pkk - 1)
                        for jj in range(8):
                            ch = half * 8 + jj
                            ins = te.transpose(
                                psT[:, slot * 1024 + jj * P:
                                    slot * 1024 + (jj + 1) * P],
                                aotm[:, (m % 2) * D + ch * P:
                                     (m % 2) * D + (ch + 1) * P],
                                c.ident_bf[:])
                        ins.then_inc(sm["spk"], 1)

        @blk.vector
        def _(v):
            for g in range(NG):
                for j in range(nblk(g)):
                    for q in range(2):
                        i = cumA[g] + j * 2 + q
                        b = i % 6
                        v.wait_ge(sm["amm"], i + 1)
                        m = 2 * g + q
                        if m >= 2:
                            v.wait_ge(sm["sact"], m - 2 + 1)
                        dst = strip[:, q * SEQ + j * 512:q * SEQ + (j + 1) * 512]
                        if j == nblk(g) - 1:
                            s0 = 512 - 128 * (m % 4)
                            v.tensor_add(dst, psA[:, b * 512:(b + 1) * 512],
                                         c.mask_m[:, s0:s0 + 512]
                                         ).then_inc(sm["aev"], 1)
                        else:
                            v.tensor_copy(dst, psA[:, b * 512:(b + 1) * 512]
                                          ).then_inc(sm["aev"], 1)
                for q in range(2):
                    m = 2 * g + q
                    L = 512 * nblk(g)
                    v.wait_ge(sm["aev"], cumA[g] + (nblk(g) - 1) * 2 + q + 1)
                    v.reduce_max(smm[:, q:q + 1], strip[:, q * SEQ:q * SEQ + L],
                                 axis=AX.X).then_inc(sm["ssm"], 1)
                    v.wait_ge(sm["ssm"], ssm_max(m))
                    if m >= 2:
                        v.wait_ge(sm["sact"], m - 2 + 1)
                    v.tensor_single_scalar(smm[:, 4 + q:5 + q], smm[:, q:q + 1],
                                           -SM_SCALE, ALU.mult
                                           ).then_inc(sm["ssm"], 1)
                for q in range(2):
                    m = 2 * g + q
                    pk0, npk = pk_of_tile[m]
                    ntk = m + 1
                    for pkk in range(npk):
                        gi = pk0 + pkk
                        slot = gi % 2
                        v.wait_ge(sm["spk"], gi + 1)
                        lo = pkk * 8
                        hi = min(lo + 8, ntk)
                        if m >= 2:
                            v.wait_ge(sm["bmm"],
                                      bmmv(g - 1, nblk(g - 1) - 1, q, 3))
                        v.tensor_copy(pt[:, q * SEQ + lo * P:q * SEQ + hi * P],
                                      psT[:, slot * 1024:
                                          slot * 1024 + (hi - lo) * P]
                                      ).then_inc(sm["pev"], 1)
                    v.wait_ge(sm["sact"], m + 1)
                    v.reciprocal(linv[:, (g % 2) * 2 + q:(g % 2) * 2 + q + 1],
                                 smm[:, 6 + q:7 + q]
                                 ).then_inc(sm["ssm"], 1)
                for jb in range(nblk(g)):
                    for q in range(2):
                        for dvN in range(4):
                            i = cumB[g] + jb * 8 + q * 4 + dvN
                            b = i % 6
                            v.wait_ge(sm["bmm"], i + 1)
                            dst = aoacc[:, q * D + dvN * 512:
                                        q * D + (dvN + 1) * 512]
                            if jb == 0:
                                v.tensor_copy(dst, psA[:, b * 512:(b + 1) * 512]
                                              ).then_inc(sm["bev"], 1)
                            else:
                                v.tensor_add(dst, dst,
                                             psA[:, b * 512:(b + 1) * 512]
                                             ).then_inc(sm["bev"], 1)
                for q in range(2):
                    m = 2 * g + q
                    v.wait_ge(sm["bev"], bmmv(g, nblk(g) - 1, q, 3))
                    if m >= 2:
                        v.wait_ge(sm["spk"], aopk_of_tile[m - 2] + 2)
                    v.tensor_scalar(aotm[:, (m % 2) * D:(m % 2 + 1) * D],
                                    aoacc[:, q * D:(q + 1) * D],
                                    linv[:, (g % 2) * 2 + q:(g % 2) * 2 + q + 1],
                                    c.cvec[:, m % 2:m % 2 + 1],
                                    ALU.mult, ALU.mult).then_inc(sm["ssm"], 1)
                for q in range(2):
                    m = 2 * g + q
                    s = m % 2
                    aosm = sm["ao0"] if s == 0 else sm["ao1"]
                    for half in range(2):
                        pkk = aopk_of_tile[m] + half
                        slot = pkk % 2
                        v.wait_ge(sm["spk"], pkk + 1)
                        if half == 0:
                            v.wait_ge(aosm, 16 * (m // 2))
                        v.tensor_copy(aost[:, s * D + half * 1024:
                                           s * D + (half + 1) * 1024],
                                      psT[:, slot * 1024:(slot + 1) * 1024]
                                      ).then_inc(sm["pev"], 1)

        @blk.scalar
        def _(a):
            for g in range(NG):
                for q in range(2):
                    m = 2 * g + q
                    L = 512 * nblk(g)
                    a.wait_ge(sm["ssm"], ssm_bias(m))
                    if m >= 2:
                        a.wait_ge(sm["ssm"], ssm_recip(m - 2))
                    if m >= 1:
                        pk0p, npkp = pk_of_tile[m - 1]
                        a.wait_ge(sm["spk"], pk0p + npkp)
                    a.activation(pb[:, 0:L],
                                 strip[:, q * SEQ:q * SEQ + L], AF.Exp,
                                 scale=SM_SCALE, bias=smm[:, 4 + q:5 + q],
                                 accum_out=smm[:, 6 + q:7 + q]
                                 ).then_inc(sm["sact"], 1)


def emit_p3(c):
    nc = c.nc
    sm = mksems(c, ["wo", "rt", "ao0", "ao1", "rr0", "rr1", "omm", "tp",
                    "rmm", "sa", "sd", "x10", "x11", "x20", "x21"])

    NDV3 = 35
    NACT3 = 4

    def sd(t, j):
        return t * NDV3 + j

    def sa_(t, j):
        return t * NACT3 + j

    (S1, MU, MUSQ, VAR, LNV, RSTD, SSQ, MX, NEGC, SE, RSE) = range(11)

    with (
        nc.sbuf_tensor("p3_wo", [P, NCD * D], BF16) as wo,
        nc.sbuf_tensor("p3_rsb", [P, NCD * 17], F32) as rsb,
        nc.sbuf_tensor("p3_aob", [P, 2 * D], BF16) as aob,
        nc.sbuf_tensor("p3_rb", [P, 2 * D], F32) as rb,
        nc.sbuf_tensor("p3_x1", [P, 2 * D], F32) as x1b,
        nc.sbuf_tensor("p3_xn2", [P, 2 * D], F32) as xn2,
        nc.sbuf_tensor("p3_xfm", [P, 2 * D], F32) as xfm,
        nc.sbuf_tensor("p3_x2st", [P, 2 * D], BF16) as x2st,
        nc.sbuf_tensor("p3_scrap", [P, D], F32) as scrap,
        nc.sbuf_tensor("p3_sm", [P, 32], F32) as smv,
        nc.sbuf_tensor("p3_lgt", [P, 2 * 32], F32) as lgt,
        nc.sbuf_tensor("p3_prob", [P, 2 * 16], F32) as prob,
        nc.sbuf_tensor("p3_pn", [P, 2 * 16], F32) as pnb,
        nc.sbuf_tensor("p3_w4", [P, 2 * 16], F32) as w4,
        nc.sbuf_tensor("p3_msk", [P, 2 * 16], F32) as mskb,
        nc.psum_tensor("p3_pso", [P, D], F32) as pso,
        nc.psum_tensor("p3_psT", [P, 2 * 512], F32) as psT3,
        nc.psum_tensor("p3_psr", [P, 17], F32) as psr,
        nc.Block() as blk,
    ):
        def smc(idx, s):
            return smv[:, 2 * idx + s:2 * idx + s + 1]

        @blk.gpsimd
        def _(g_):
            g_.dma_start(
                wo[:].rearrange("p (dk j) -> p dk j", dk=NCD, j=D),
                c.wo_in.rearrange("(dk p) j -> p dk j", dk=NCD, p=P),
            ).then_inc(sm["wo"], 16)
            for t in range(NT_OWN):
                s = t % 2
                x1m = sm["x10"] if s == 0 else sm["x11"]
                x2m = sm["x20"] if s == 0 else sm["x21"]
                g_.wait_ge(sm["sd"], sd(t, 1))
                g_.wait_ge(x1m, 16 * (t // 2))
                g_.dma_start(c.out[t * P:(t + 1) * P, :],
                             x1b[:, s * D:(s + 1) * D]).then_inc(x1m, 16)
                g_.wait_ge(sm["sd"], sd(t, 14))
                g_.wait_ge(x2m, 16 * (t // 2))
                g_.dma_start(
                    c.x2_s[:, :, t * P:(t + 1) * P].rearrange("ch p j -> p ch j"),
                    x2st[:, s * D:(s + 1) * D].rearrange(
                        "p (ch j) -> p ch j", ch=NCD, j=P),
                ).then_inc(x2m, 16)
            for n in ("x10", "x11", "x20", "x21"):
                g_.wait_ge(sm[n], 16 * 8)

        @blk.sync
        def _(sy):
            sy.dma_start(rsb[:], c.router_in[:]).then_inc(sm["rt"], 16)
            for t in range(NT_OWN):
                s = t % 2
                aom = sm["ao0"] if s == 0 else sm["ao1"]
                rrm = sm["rr0"] if s == 0 else sm["rr1"]
                if t >= 2:
                    sy.wait_ge(aom, 16 * (t // 2))
                    sy.wait_ge(sm["omm"], t - 1)
                sy.dma_start(
                    aob[:, s * D:(s + 1) * D].rearrange(
                        "p (ch j) -> p ch j", ch=NCD, j=P),
                    c.ao_s[:, :, t * P:(t + 1) * P].rearrange("ch p j -> p ch j"),
                ).then_inc(aom, 16)
                if t >= 2:
                    sy.wait_ge(rrm, 16 * (t // 2))
                    sy.wait_ge(sm["sd"], sd(t - 2, 1))
                sy.dma_start(rb[:, s * D:(s + 1) * D],
                             c.r_s[t * P:(t + 1) * P, :]).then_inc(rrm, 16)

        @blk.tensor
        def _(te):
            te.wait_ge(sm["wo"], 16)
            te.wait_ge(sm["rt"], 16)
            for t in range(NT_OWN):
                s = t % 2
                aom = sm["ao0"] if s == 0 else sm["ao1"]
                te.wait_ge(aom, 16 * (t // 2 + 1))
                if t >= 1:
                    te.wait_ge(sm["sd"], sd(t - 1, 1))
                for n in range(4):
                    for dk in range(NCD):
                        ins = te.matmul(
                            pso[:, n * 512:(n + 1) * 512],
                            aob[:, s * D + dk * P:s * D + (dk + 1) * P],
                            wo[:, dk * D + n * 512:dk * D + (n + 1) * 512],
                            start=(dk == 0), stop=(dk == NCD - 1))
                ins.then_inc(sm["omm"], 1)
                te.wait_ge(sm["sd"], sd(t, 6))
                for pkk in range(4):
                    gi = t * 4 + pkk
                    slot = gi % 2
                    if gi >= 2:
                        tp_, pp_ = divmod(gi - 2, 4)
                        te.wait_ge(sm["sd"], sd(tp_, 8 + 2 * pp_))
                    for jj in range(4):
                        ch = pkk * 4 + jj
                        ins = te.transpose(
                            psT3[:, slot * 512 + jj * P:slot * 512 + (jj + 1) * P],
                            xn2[:, s * D + ch * P:s * D + (ch + 1) * P],
                            c.ident_f32[:])
                    ins.then_inc(sm["tp"], 1)
                te.wait_ge(sm["sd"], sd(t, 14))
                if t >= 2:
                    te.wait_ge(sm["sd"], sd(t - 2, 15))
                for dk in range(NCD):
                    ins = te.matmul(psr[:], xfm[:, s * D + dk * P:
                                               s * D + (dk + 1) * P],
                                    rsb[:, dk * 17:(dk + 1) * 17],
                                    start=(dk == 0), stop=(dk == NCD - 1))
                ins.then_inc(sm["rmm"], 1)

        @blk.scalar
        def _(a):
            for t in range(NT_OWN):
                s = t % 2
                a.wait_ge(sm["sd"], sd(t, 1))
                if t >= 2:
                    a.wait_ge(sm["sd"], sd(t - 2, 5))
                a.activation(scrap[:], x1b[:, s * D:(s + 1) * D], AF.Square,
                             accum_out=smc(SSQ, s)).then_inc(sm["sa"], 1)
                a.wait_ge(sm["sd"], sd(t, 5))
                a.activation(smc(LNV, s), smc(VAR, s), AF.Ln,
                             bias=c.epsv[:]).then_inc(sm["sa"], 1)
                a.wait_ge(sm["sa"], sa_(t, 2))
                a.activation(smc(RSTD, s), smc(LNV, s), AF.Exp,
                             scale=-0.5).then_inc(sm["sa"], 1)
                a.wait_ge(sm["sd"], sd(t, 17))
                if t >= 2:
                    a.wait_ge(sm["sd"], sd(t - 2, 18))
                a.activation(prob[:, s * 16:(s + 1) * 16],
                             lgt[:, s * 32:s * 32 + 16], AF.Exp,
                             bias=smc(NEGC, s),
                             accum_out=smc(SE, s)).then_inc(sm["sa"], 1)
            a.wait_ge(sm["sd"], NT_OWN * NDV3)
            a.activation(c.sgv[:], c.sg_all[:], AF.Sigmoid).then_inc(sm["sa"], 1)

        @blk.vector
        def _(v):
            for t in range(NT_OWN):
                s = t % 2
                rrm = sm["rr0"] if s == 0 else sm["rr1"]
                x1m = sm["x10"] if s == 0 else sm["x11"]
                x2m = sm["x20"] if s == 0 else sm["x21"]
                v.wait_ge(sm["omm"], t + 1)
                v.wait_ge(rrm, 16 * (t // 2 + 1))
                if t >= 2:
                    v.wait_ge(sm["sa"], sa_(t - 2, 1))
                    v.wait_ge(x1m, 16 * (t // 2))
                v.tensor_add(x1b[:, s * D:(s + 1) * D], pso[:],
                             rb[:, s * D:(s + 1) * D]).then_inc(sm["sd"], 1)
                v.wait_ge(sm["sd"], sd(t, 1))
                v.reduce_sum(smc(S1, s), x1b[:, s * D:(s + 1) * D],
                             axis=AX.X).then_inc(sm["sd"], 1)
                v.wait_ge(sm["sd"], sd(t, 2))
                v.tensor_single_scalar(smc(MU, s), smc(S1, s), INV_D,
                                       ALU.mult).then_inc(sm["sd"], 1)
                v.wait_ge(sm["sd"], sd(t, 3))
                v.tensor_mul(smc(MUSQ, s), smc(MU, s),
                             smc(MU, s)).then_inc(sm["sd"], 1)
                v.wait_ge(sm["sa"], sa_(t, 1))
                v.wait_ge(sm["sd"], sd(t, 4))
                if t >= 2:
                    v.wait_ge(sm["sa"], sa_(t - 2, 2))
                v.tensor_scalar(smc(VAR, s), smc(SSQ, s), INV_D, smc(MUSQ, s),
                                ALU.mult, ALU.subtract).then_inc(sm["sd"], 1)
                v.wait_ge(sm["sa"], sa_(t, 3))
                if t >= 1:
                    v.wait_ge(sm["tp"], t * 4)
                v.tensor_scalar(xn2[:, s * D:(s + 1) * D],
                                x1b[:, s * D:(s + 1) * D], smc(MU, s),
                                smc(RSTD, s), ALU.subtract,
                                ALU.mult).then_inc(sm["sd"], 1)
                for pkk in range(4):
                    gi = t * 4 + pkk
                    slot = gi % 2
                    v.wait_ge(sm["tp"], gi + 1)
                    if pkk == 0:
                        if t >= 1:
                            v.wait_ge(sm["rmm"], t)
                        v.wait_ge(x2m, 16 * (t // 2))
                    v.tensor_copy(xfm[:, s * D + pkk * 512:
                                      s * D + (pkk + 1) * 512],
                                  psT3[:, slot * 512:(slot + 1) * 512]
                                  ).then_inc(sm["sd"], 1)
                    v.tensor_copy(x2st[:, s * D + pkk * 512:
                                       s * D + (pkk + 1) * 512],
                                  psT3[:, slot * 512:(slot + 1) * 512]
                                  ).then_inc(sm["sd"], 1)
                v.wait_ge(sm["rmm"], t + 1)
                v.tensor_copy(lgt[:, s * 32:s * 32 + 17],
                              psr[:]).then_inc(sm["sd"], 1)
                v.wait_ge(sm["sd"], sd(t, 15))
                v.reduce_max(smc(MX, s), lgt[:, s * 32:s * 32 + 16],
                             axis=AX.X).then_inc(sm["sd"], 1)
                v.wait_ge(sm["sd"], sd(t, 16))
                if t >= 2:
                    v.wait_ge(sm["sa"], sa_(t - 2, 4))
                v.tensor_single_scalar(smc(NEGC, s), smc(MX, s), -1.0,
                                       ALU.mult).then_inc(sm["sd"], 1)
                v.wait_ge(sm["sa"], sa_(t, 4))
                v.reciprocal(smc(RSE, s), smc(SE, s)).then_inc(sm["sd"], 1)
                v.wait_ge(sm["sd"], sd(t, 18))
                v.tensor_scalar_mul(pnb[:, s * 16:(s + 1) * 16],
                                    prob[:, s * 16:(s + 1) * 16],
                                    smc(RSE, s)).then_inc(sm["sd"], 1)
                v.wait_ge(sm["sd"], sd(t, 19))
                v.tensor_copy(w4[:, s * 16:(s + 1) * 16],
                              pnb[:, s * 16:(s + 1) * 16]).then_inc(sm["sd"], 1)
                for r4 in range(4):
                    v.wait_ge(sm["sd"], sd(t, 20 + r4 * 3))
                    v.reduce_max(smc(MX, s), w4[:, s * 16:(s + 1) * 16],
                                 axis=AX.X).then_inc(sm["sd"], 1)
                    v.wait_ge(sm["sd"], sd(t, 21 + r4 * 3))
                    v.tensor_single_scalar(mskb[:, s * 16:(s + 1) * 16],
                                           w4[:, s * 16:(s + 1) * 16],
                                           smc(MX, s),
                                           ALU.is_ge).then_inc(sm["sd"], 1)
                    v.wait_ge(sm["sd"], sd(t, 22 + r4 * 3))
                    v.scalar_tensor_tensor(w4[:, s * 16:(s + 1) * 16],
                                           mskb[:, s * 16:(s + 1) * 16], -10.0,
                                           w4[:, s * 16:(s + 1) * 16],
                                           ALU.mult, ALU.add
                                           ).then_inc(sm["sd"], 1)
                v.wait_ge(sm["sd"], sd(t, 32))
                v.tensor_single_scalar(mskb[:, s * 16:(s + 1) * 16],
                                       pnb[:, s * 16:(s + 1) * 16], smc(MX, s),
                                       ALU.is_ge).then_inc(sm["sd"], 1)
                v.wait_ge(sm["sd"], sd(t, 33))
                v.tensor_mul(c.coef_all[:, t * 16:(t + 1) * 16],
                             pnb[:, s * 16:(s + 1) * 16],
                             mskb[:, s * 16:(s + 1) * 16]).then_inc(sm["sd"], 1)
                v.tensor_copy(c.sg_all[:, t:t + 1],
                              lgt[:, s * 32 + 16:s * 32 + 17]
                              ).then_inc(sm["sd"], 1)


def emit_p5(c):
    nc = c.nc
    sm = mksems(c, ["x2l", "wb0", "wb1", "wb2", "wb3", "wh0", "wh1",
                    "gmm", "gev", "sact5", "dmm", "dev", "ost0", "ost1"])

    NE = 20
    NIC = NI_EXP
    HWIDTH = NIC * D

    def wsrc(it, ic):
        if it < 4:
            g_ap = c.sgupc[it * NI_EXP + ic].rearrange("p dk j -> p (dk j)")
            u_ap = c.sgupc[44 + it * NI_EXP + ic].rearrange(
                "p dk j -> p (dk j)")
        else:
            e = it - 4
            g_ap = c.w13c[e, ic].rearrange("p dk j -> p (dk j)")
            u_ap = c.w13c[e, NI_EXP + ic].rearrange("p dk j -> p (dk j)")
        return g_ap, u_ap

    def w2src(it, half):
        if it < 4:
            return c.sdh_in[it, :, :, half * 1024:(half + 1) * 1024]
        return c.w2h_in[it - 4, :, :, half * 1024:(half + 1) * 1024]

    def coefcol(it, tc):
        if it < 4:
            return c.sgv[:, tc:tc + 1]
        return c.coef_all[:, tc * 16 + (it - 4):tc * 16 + (it - 4) + 1]

    wbs = ["wb0", "wb1", "wb2", "wb3"]

    with (
        nc.sbuf_tensor("p5_x2f", [P, NCD * D], BF16) as x2f,
        nc.sbuf_tensor("p5_h", [P, HWIDTH], BF16) as hact,
        nc.sbuf_tensor("p5_w13", [P, 4 * D], BF16) as w13b,
        nc.sbuf_tensor("p5_w2", [P, 2 * NIC * 1024], BF16) as w2b,
        nc.sbuf_tensor("p5_sgb", [P, 3 * 512], BF16) as sgb,
        nc.sbuf_tensor("p5_stage", [P, 2 * 512], F32) as stage,
        nc.psum_tensor("p5_pg", [P, 3 * 512], F32) as pg,
        nc.psum_tensor("p5_pu", [P, 3 * 512], F32) as pu,
        nc.psum_tensor("p5_pd", [P, 2 * 512], F32) as pd,
        nc.Block() as blk,
    ):
        @blk.sync
        def _(sy):
            sy.dma_start(
                x2f[:].rearrange("p (ch j) -> p ch j", ch=NCD, j=D),
                c.x2_s.rearrange("ch p j -> p ch j"),
            ).then_inc(sm["x2l"], 16)
            sy.wait_ge(c.convsem, 16 * c.conv_count)
            for it in range(NE):
                for ic in range(NIC):
                    for gu in range(2):
                        li = it * 22 + ic * 2 + gu
                        bi = li % 4
                        bsm = sm[wbs[bi]]
                        if li >= 4:
                            sy.wait_ge(bsm, 16 * (li // 4))
                            sy.wait_ge(sm["gmm"], ((li - 4) // 2 + 1) * 4)
                        g_ap, u_ap = wsrc(it, ic)
                        sy.dma_start(w13b[:, bi * D:(bi + 1) * D],
                                     g_ap if gu == 0 else u_ap
                                     ).then_inc(bsm, 16)

        @blk.gpsimd
        def _(g_):
            def emit_pieces(itp, lo, hi):
                for pi in range(itp * 64 + lo, itp * 64 + hi):
                    hs = (pi - itp * 64) // 16
                    tc = pi % 16
                    osl = pi % 2
                    osm = sm["ost0"] if osl == 0 else sm["ost1"]
                    g_.wait_ge(sm["dev"], pi + 1)
                    g_.wait_ge(osm, 16 * (pi // 2))
                    g_.dma_start(
                        c.out[tc * P:(tc + 1) * P, hs * 512:(hs + 1) * 512],
                        stage[:, osl * 512:(osl + 1) * 512],
                        accum_op=ALU.add,
                    ).then_inc(osm, 16)

            for it in range(NE + 1):
                if it >= 1:
                    emit_pieces(it - 1, 0, 64)
                if it < NE:
                    for h in range(2):
                        hidx = it * 2 + h
                        hsm = sm["wh0"] if hidx % 2 == 0 else sm["wh1"]
                        if hidx >= 2:
                            g_.wait_ge(hsm, 16 * (hidx // 2))
                            g_.wait_ge(sm["dmm"], (it - 1) * 64 + (h + 1) * 32)
                        g_.dma_start(
                            w2b[:, (hidx % 2) * NIC * 1024:
                                (hidx % 2 + 1) * NIC * 1024].rearrange(
                                "p (ic j) -> p ic j", ic=NIC, j=1024),
                            w2src(it, h),
                        ).then_inc(hsm, 16)
            g_.wait_ge(sm["ost0"], 16 * NE * 32)
            g_.wait_ge(sm["ost1"], 16 * NE * 32)

        @blk.tensor
        def _(te):
            te.wait_ge(sm["x2l"], 16)
            for it in range(NE):
                for ic in range(NIC):
                    for gu in range(2):
                        li = it * 22 + ic * 2 + gu
                        bi = li % 4
                        te.wait_ge(sm[wbs[bi]], 16 * (li // 4 + 1))
                    for ts in range(4):
                        pi = (it * NIC + ic) * 4 + ts
                        sl = pi % 3
                        if pi >= 3:
                            te.wait_ge(sm["gev"], pi - 2)
                            te.wait_ge(sm["sact5"], pi - 2)
                        for dk in range(NCD):
                            ins = te.matmul(
                                pg[:, sl * 512:(sl + 1) * 512],
                                w13b[:, ((it * 22 + ic * 2) % 4) * D + dk * P:
                                     ((it * 22 + ic * 2) % 4) * D + (dk + 1) * P],
                                x2f[:, dk * D + ts * 512:dk * D + (ts + 1) * 512],
                                start=(dk == 0), stop=(dk == NCD - 1))
                        for dk in range(NCD):
                            ins = te.matmul(
                                pu[:, sl * 512:(sl + 1) * 512],
                                w13b[:, ((it * 22 + ic * 2 + 1) % 4) * D + dk * P:
                                     ((it * 22 + ic * 2 + 1) % 4) * D + (dk + 1) * P],
                                x2f[:, dk * D + ts * 512:dk * D + (ts + 1) * 512],
                                start=(dk == 0), stop=(dk == NCD - 1))
                        ins.then_inc(sm["gmm"], 1)
                for h in range(2):
                    hidx = it * 2 + h
                    hsm = sm["wh0"] if hidx % 2 == 0 else sm["wh1"]
                    te.wait_ge(hsm, 16 * (hidx // 2 + 1))
                    for hs2 in range(2):
                        hs = h * 2 + hs2
                        for tc in range(16):
                            pi = it * 64 + hs * 16 + tc
                            bank = pi % 2
                            if pi >= 2:
                                te.wait_ge(sm["dev"], pi - 1)
                            if hs == 0 and tc == 0:
                                te.wait_ge(sm["gev"], (it + 1) * 44)
                            for ic in range(NIC):
                                ins = te.matmul(
                                    pd[:, bank * 512:(bank + 1) * 512],
                                    hact[:, ic * D + tc * P:ic * D + (tc + 1) * P],
                                    w2b[:, (hidx % 2) * NIC * 1024 + ic * 1024 +
                                        hs2 * 512:
                                        (hidx % 2) * NIC * 1024 + ic * 1024 +
                                        (hs2 + 1) * 512],
                                    start=(ic == 0), stop=(ic == NIC - 1))
                            ins.then_inc(sm["dmm"], 1)

        @blk.scalar
        def _(a):
            for it in range(NE):
                for ic in range(NIC):
                    for ts in range(4):
                        pi = (it * NIC + ic) * 4 + ts
                        sl = pi % 3
                        a.wait_ge(sm["gmm"], pi + 1)
                        if pi >= 3:
                            a.wait_ge(sm["gev"], pi - 2)
                        a.activation(sgb[:, sl * 512:(sl + 1) * 512],
                                     pg[:, sl * 512:(sl + 1) * 512],
                                     AF.Silu).then_inc(sm["sact5"], 1)

        @blk.vector
        def _(v):
            for it in range(NE):
                for ic in range(NIC):
                    for ts in range(4):
                        pi = (it * NIC + ic) * 4 + ts
                        sl = pi % 3
                        v.wait_ge(sm["sact5"], pi + 1)
                        if ic == 0 and ts == 0 and it >= 1:
                            v.wait_ge(sm["dmm"], it * 64)
                        v.tensor_mul(hact[:, ic * D + ts * 512:
                                          ic * D + (ts + 1) * 512],
                                     sgb[:, sl * 512:(sl + 1) * 512],
                                     pu[:, sl * 512:(sl + 1) * 512]
                                     ).then_inc(sm["gev"], 1)
                for h in range(2):
                    for hs2 in range(2):
                        hs = h * 2 + hs2
                        for tc in range(16):
                            pi = it * 64 + hs * 16 + tc
                            bank = pi % 2
                            osl = pi % 2
                            osm = sm["ost0"] if osl == 0 else sm["ost1"]
                            v.wait_ge(sm["dmm"], pi + 1)
                            v.wait_ge(osm, 16 * (pi // 2))
                            v.tensor_single_scalar(
                                stage[:, osl * 512:(osl + 1) * 512],
                                pd[:, bank * 512:(bank + 1) * 512],
                                coefcol(it, tc), ALU.mult
                            ).then_inc(sm["dev"], 1)



def prep_inputs(L0_attn_x, L0_attn_qkv, L0_attn_o_proj, L0_attn_gamma,
                L0_attn_beta, L0_ffn_gate_w, L0_ffn_sgate_w, L0_ffn_w13,
                L0_ffn_w2, L0_ffn_shared_gate_up_w, L0_ffn_shared_down_w):
    A = np.ascontiguousarray
    x = np.asarray(L0_attn_x)
    qkv = A(np.asarray(L0_attn_qkv))
    wo = A(np.asarray(L0_attn_o_proj))
    gamma = A(np.asarray(L0_attn_gamma).reshape(1, D))
    beta = A(np.asarray(L0_attn_beta).reshape(1, D))
    gate_t = np.asarray(L0_ffn_gate_w).T.reshape(NCD, P, E)
    sgate_t = np.asarray(L0_ffn_sgate_w).T.reshape(NCD, P, 1)
    router = A(np.concatenate([gate_t, sgate_t], axis=2)
               .transpose(1, 0, 2).reshape(P, NCD * (E + 1)))
    w13t = np.asarray(L0_ffn_w13).transpose(0, 2, 1)
    w13h = A(w13t.reshape(E, NCD, P, 22, P).transpose(0, 3, 2, 1, 4))
    w2t = np.asarray(L0_ffn_w2).transpose(0, 2, 1)
    w2h = A(w2t.reshape(E, NI_EXP, P, D).transpose(0, 2, 1, 3))
    sgupt = np.asarray(L0_ffn_shared_gate_up_w).T
    sguph = A(sgupt.reshape(NCD, P, 88, P).transpose(2, 1, 0, 3))
    sdownt = np.asarray(L0_ffn_shared_down_w).T
    sdh = A(sdownt.reshape(4, NI_EXP, P, D).transpose(0, 2, 1, 3))
    qkvh = A(qkv[:, :2 * D].reshape(NCD, P, 32, P).transpose(2, 1, 0, 3))

    in_maps = []
    for core in range(8):
        b = core // 2
        par = core % 2
        csel = np.zeros((1, 2), dtype=np.float32)
        csel[0, par] = 1.0
        in_maps.append({
            "x": A(x[b]),
            "qkv": qkv, "wo": wo, "gamma": gamma, "beta": beta,
            "router": router,
            "qkvh": qkvh, "w13h": w13h, "w2h": w2h, "sguph": sguph,
            "sdh": sdh,
            "csel": csel,
        })
    return in_maps


def assemble_output(results):
    out = np.empty((4, SEQ, D), dtype=np.float32)
    for core in range(8):
        b = core // 2
        par = core % 2
        oc = results[core]["out"]
        for k in range(NT_OWN):
            out[b, 256 * k + P * par:256 * k + P * par + P, :] = \
                oc[k * P:(k + 1) * P, :]
    return out


def kernel(**inputs):
    in_maps = prep_inputs(**inputs)
    nc = build_nc()
    res = run_bass_kernel_spmd(nc, in_maps, core_ids=list(range(8)))
    return assemble_output(res.results)


# version 09 (final)
# speedup vs baseline: 1.0109x; 1.0109x over previous
import numpy as np
from contextlib import ExitStack

import concourse.bass as bass
import concourse.mybir as mybir
from concourse.bass_utils import run_bass_kernel_spmd

F32 = mybir.dt.float32
BF16 = mybir.dt.bfloat16
AF = mybir.ActivationFunctionType
ALU = mybir.AluOpType
AX = mybir.AxisListType

P = 128
D = 2048
NCD = D // P
SEQ = 4096
NT_FULL = SEQ // P
TOWN = 2048
NT_OWN = TOWN // P
EPS = 1e-5
SM_SCALE = 1.0 / float(np.sqrt(2048.0))
NEG = -1e30
INV_D = 1.0 / D
E = 16
I_EXP = 1408
NI_EXP = I_EXP // P
I_SH = 5632
NI_SH = I_SH // P

ALL_PHASES = ("p0", "p1", "p2", "p3", "p5")


class Ctx:
    pass


def build_nc(phases=ALL_PHASES, dbg=()):
    nc = bass.Bass("TRN2", target_bir_lowering=False)
    c = Ctx()
    c.nc = nc

    def dram(name, shape, dtype, kind="Internal"):
        k = "ExternalOutput" if (kind == "Internal" and name in dbg) else kind
        return nc.dram_tensor(name, shape, dtype, kind=k)

    c.x_in = dram("x", [SEQ, D], F32, kind="ExternalInput")
    c.qkv_in = dram("qkv", [D, 3 * D], F32, kind="ExternalInput")
    c.qkvh_in = dram("qkvh", [32, P, NCD, P], F32, kind="ExternalInput")
    c.wo_in = dram("wo", [D, D], F32, kind="ExternalInput")
    c.gamma_in = dram("gamma", [1, D], F32, kind="ExternalInput")
    c.beta_in = dram("beta", [1, D], F32, kind="ExternalInput")
    c.router_in = dram("router", [P, NCD * (E + 1)], F32, kind="ExternalInput")
    c.w13h_in = dram("w13h", [E, 22, P, NCD, P], F32, kind="ExternalInput")
    c.w2h_in = dram("w2h", [E, P, NI_EXP, D], F32, kind="ExternalInput")
    c.sguph_in = dram("sguph", [88, P, NCD, P], F32, kind="ExternalInput")
    c.sdh_in = dram("sdh", [4, P, NI_EXP, D], F32, kind="ExternalInput")
    c.csel_in = dram("csel", [1, 2], F32, kind="ExternalInput")

    c.out = dram("out", [TOWN, D], F32, kind="ExternalOutput")

    c.xn_s = dram("xn_s", [NCD, P, SEQ], BF16)
    c.r_s = dram("r_s", [TOWN, D], F32)
    c.q_s = dram("q_s", [NCD, P, SEQ], BF16)
    c.k_s = dram("k_s", [NCD, P, SEQ], BF16)
    c.v_s = dram("v_s", [SEQ, D], BF16)
    c.ao_s = dram("ao_s", [NCD, P, TOWN], BF16)
    c.x2_s = dram("x2_s", [NCD, P, TOWN], BF16)
    c.w13c = dram("w13c", [E, 22, P, NCD, P], BF16)
    c.sgupc = dram("sgupc", [88, P, NCD, P], BF16)

    st = ExitStack()
    c.st = st
    with st:
        c.ident_bf = st.enter_context(nc.sbuf_tensor("ident_bf", [P, P], BF16))
        c.ident_f32 = st.enter_context(nc.sbuf_tensor("ident_f32", [P, P], F32))
        c.cvec = st.enter_context(nc.sbuf_tensor("cvec", [P, 2], F32))
        c.mask_m = st.enter_context(nc.sbuf_tensor("mask_m", [P, 1024], F32))
        c.ones_row = st.enter_context(nc.sbuf_tensor("ones_row", [1, P], F32))
        c.epsv = st.enter_context(nc.sbuf_tensor("epsv", [P, 1], F32))

        c._semn = 0
        c.conv_count = 0
        c.convsem = st.enter_context(nc.semaphore("convsem"))

        c.coef_all = st.enter_context(nc.sbuf_tensor("coef_all", [P, 16 * E], F32))
        c.sg_all = st.enter_context(nc.sbuf_tensor("sg_all", [P, 16], F32))
        c.sgv = st.enter_context(nc.sbuf_tensor("sgv", [P, 16], F32))

        st01 = ExitStack()
        c.gamma_b = st01.enter_context(nc.sbuf_tensor("gamma_b", [P, D], F32))
        c.beta_b = st01.enter_context(nc.sbuf_tensor("beta_b", [P, D], F32))
        c.gb_row = st01.enter_context(nc.sbuf_tensor("gb_row", [1, 2 * D + 2], F32))
        emit_preamble(c)
        if "p0" in phases:
            emit_p0(c)
        st01.close()
        if "p1" in phases:
            emit_p1(c)
        if "p2" in phases:
            emit_p2(c)
        if "p3" in phases:
            emit_p3(c)
        if "p5" in phases:
            emit_p5(c)

    return nc


def emit_conv(c, g_, n):
    for _ in range(n):
        u = c.conv_count
        if u >= 20:
            return
        if u > 0:
            g_.wait_ge(c.convsem, 16 * u)
        if u < 16:
            srcv = c.w13h_in[u].rearrange("a p c j -> (a p) (c j)")
            dstv = c.w13c[u].rearrange("a p c j -> (a p) (c j)")
        else:
            q = u - 16
            srcv = c.sguph_in[q * 22:(q + 1) * 22].rearrange(
                "a p c j -> (a p) (c j)")
            dstv = c.sgupc[q * 22:(q + 1) * 22].rearrange(
                "a p c j -> (a p) (c j)")
        g_.dma_start(dstv, srcv).then_inc(c.convsem, 16)
        c.conv_count += 1


def mksems(c, names):
    d = {}
    for n in names:
        c._semn += 1
        d[n] = c.st.enter_context(c.nc.semaphore(f"{n}_{c._semn}"))
    return d


def emit_preamble(c):
    nc = c.nc
    sm = mksems(c, ["pm", "pld", "ppe", "pdv"])
    with (
        nc.psum_tensor("ps_bc", [P, D], F32) as ps_bc,
        nc.psum_tensor("ps_bc2", [P, 2], F32) as ps_bc2,
        nc.Block() as blk,
    ):
        @blk.gpsimd
        def _(g):
            g.memset(c.ident_bf[:], 0.0).then_inc(sm["pm"], 1)
            g.wait_ge(sm["pm"], 1)
            g.affine_select(out=c.ident_bf[:], in_=c.ident_bf[:],
                            compare_op=ALU.not_equal, fill=1.0, base=0,
                            pattern=[[-1, P]], channel_multiplier=1)
            g.memset(c.ident_f32[:], 0.0).then_inc(sm["pm"], 1)
            g.wait_ge(sm["pm"], 2)
            g.affine_select(out=c.ident_f32[:], in_=c.ident_f32[:],
                            compare_op=ALU.not_equal, fill=1.0, base=0,
                            pattern=[[-1, P]], channel_multiplier=1)
            g.memset(c.mask_m[:], 0.0).then_inc(sm["pm"], 1)
            g.wait_ge(sm["pm"], 3)
            g.affine_select(out=c.mask_m[:], in_=c.mask_m[:],
                            compare_op=ALU.is_ge, fill=NEG, base=512,
                            pattern=[[-1, 1024]], channel_multiplier=1)
            g.memset(c.ones_row[:], 1.0)
            g.memset(c.epsv[:], EPS).then_inc(sm["pm"], 1)

        @blk.sync
        def _(sy):
            sy.dma_start(c.gb_row[:, 0:D], c.gamma_in[:]).then_inc(sm["pld"], 16)
            sy.wait_ge(sm["pld"], 16)
            sy.dma_start(c.gb_row[:, D:2 * D], c.beta_in[:]).then_inc(sm["pld"], 16)
            sy.wait_ge(sm["pld"], 32)
            sy.dma_start(c.gb_row[:, 2 * D:2 * D + 2],
                         c.csel_in[:]).then_inc(sm["pld"], 16)

        @blk.tensor
        def _(te):
            te.wait_ge(sm["pld"], 48)
            te.wait_ge(sm["pm"], 4)
            for j in range(4):
                ins = te.matmul(ps_bc[:, j * 512:(j + 1) * 512], c.ones_row[:],
                                c.gb_row[:, j * 512:(j + 1) * 512],
                                start=True, stop=True)
            ins.then_inc(sm["ppe"], 1)
            te.wait_ge(sm["pdv"], 1)
            for j in range(4):
                te.matmul(ps_bc[:, j * 512:(j + 1) * 512], c.ones_row[:],
                          c.gb_row[:, D + j * 512:D + (j + 1) * 512],
                          start=True, stop=True)
            te.matmul(ps_bc2[:], c.ones_row[:],
                      c.gb_row[:, 2 * D:2 * D + 2],
                      start=True, stop=True).then_inc(sm["ppe"], 1)

        @blk.vector
        def _(v):
            v.wait_ge(sm["ppe"], 1)
            v.tensor_copy(c.gamma_b[:], ps_bc[:]).then_inc(sm["pdv"], 1)
            v.wait_ge(sm["ppe"], 2)
            v.tensor_copy(c.beta_b[:], ps_bc[:]).then_inc(sm["pdv"], 1)
            v.tensor_copy(c.cvec[:], ps_bc2[:]).then_inc(sm["pdv"], 1)


def emit_p0(c):
    nc = c.nc
    sm = mksems(c, ["ld0", "ld1", "rs0", "rs1", "xs0", "xs1",
                    "sa", "sd", "sp"])
    NDV = 16
    NACT = 6

    def dv(m, j):
        return m * NDV + j

    def act(m, j):
        return m * NACT + j

    S1, SSQ, MU, MUSQ, VAR, LNV, RSTD = 0, 1, 2, 3, 4, 5, 6
    S1B, SSQB, MU2, MUSQ2, VAR2, LNV2, RSTD2 = 7, 8, 9, 10, 11, 12, 13

    with (
        nc.sbuf_tensor("p0_xt", [P, 2 * D], F32) as xt,
        nc.sbuf_tensor("p0_scrap", [P, D], F32) as scrap,
        nc.sbuf_tensor("p0_sm", [P, 32], F32) as smv,
        nc.sbuf_tensor("p0_ln1", [P, 2 * D], F32) as ln1,
        nc.sbuf_tensor("p0_rb", [P, 2 * D], F32) as rb,
        nc.sbuf_tensor("p0_xnf", [P, 2 * D], F32) as xnf,
        nc.sbuf_tensor("p0_xnb", [P, 2 * D], BF16) as xnb,
        nc.sbuf_tensor("p0_stage", [P, 2 * 2048], BF16) as stage,
        nc.psum_tensor("p0_psT", [P, 2 * D], BF16) as psT,
        nc.Block() as blk,
    ):
        def smc(idx, s):
            return smv[:, 2 * idx + s:2 * idx + s + 1]

        @blk.sync
        def _(sy):
            for m in range(NT_FULL):
                s = m % 2
                ldm = sm["ld0"] if s == 0 else sm["ld1"]
                if m >= 2:
                    sy.wait_ge(sm["sd"], dv(m - 2, 7))
                    sy.wait_ge(sm["sa"], act(m - 2, 1))
                    sy.wait_ge(ldm, 16 * (m // 2))
                sy.dma_start(xt[:, s * D:(s + 1) * D],
                             c.x_in[m * P:(m + 1) * P, :]).then_inc(ldm, 16)

        @blk.scalar
        def _(a):
            for m in range(NT_FULL):
                s = m % 2
                ldm = sm["ld0"] if s == 0 else sm["ld1"]
                a.wait_ge(ldm, 16 * (m // 2 + 1))
                if m >= 2:
                    a.wait_ge(sm["sd"], dv(m - 2, 4))
                a.activation(scrap[:], xt[:, s * D:(s + 1) * D], AF.Square,
                             accum_out=smc(SSQ, s)).then_inc(sm["sa"], 1)
                a.wait_ge(sm["sd"], dv(m, 4))
                a.activation(smc(LNV, s), smc(VAR, s), AF.Ln,
                             bias=c.epsv[:]).then_inc(sm["sa"], 1)
                a.wait_ge(sm["sa"], act(m, 2))
                a.activation(smc(RSTD, s), smc(LNV, s), AF.Exp,
                             scale=-0.5).then_inc(sm["sa"], 1)
                a.wait_ge(sm["sd"], dv(m, 5))
                a.activation(scrap[:], ln1[:, s * D:(s + 1) * D], AF.Square,
                             accum_out=smc(SSQB, s)).then_inc(sm["sa"], 1)
                a.wait_ge(sm["sd"], dv(m, 11))
                a.activation(smc(LNV2, s), smc(VAR2, s), AF.Ln,
                             bias=c.epsv[:]).then_inc(sm["sa"], 1)
                a.wait_ge(sm["sa"], act(m, 5))
                a.activation(smc(RSTD2, s), smc(LNV2, s), AF.Exp,
                             scale=-0.5).then_inc(sm["sa"], 1)

        @blk.vector
        def _(v):
            for m in range(NT_FULL):
                s = m % 2
                ldm = sm["ld0"] if s == 0 else sm["ld1"]
                rsm = sm["rs0"] if s == 0 else sm["rs1"]
                xts = xt[:, s * D:(s + 1) * D]
                ln1s = ln1[:, s * D:(s + 1) * D]
                v.wait_ge(ldm, 16 * (m // 2 + 1))
                v.reduce_sum(smc(S1, s), xts, axis=AX.X).then_inc(sm["sd"], 1)
                v.wait_ge(sm["sd"], dv(m, 1))
                v.tensor_single_scalar(smc(MU, s), smc(S1, s), INV_D,
                                       ALU.mult).then_inc(sm["sd"], 1)
                v.wait_ge(sm["sd"], dv(m, 2))
                v.tensor_mul(smc(MUSQ, s), smc(MU, s),
                             smc(MU, s)).then_inc(sm["sd"], 1)
                v.wait_ge(sm["sa"], act(m, 1))
                v.wait_ge(sm["sd"], dv(m, 3))
                if m >= 2:
                    v.wait_ge(sm["sa"], act(m - 2, 2))
                v.tensor_scalar(smc(VAR, s), smc(SSQ, s), INV_D, smc(MUSQ, s),
                                ALU.mult, ALU.subtract).then_inc(sm["sd"], 1)
                v.wait_ge(sm["sa"], act(m, 3))
                if m >= 2:
                    v.wait_ge(sm["sa"], act(m - 2, 4))
                v.tensor_scalar(ln1s, xts, smc(MU, s), smc(RSTD, s),
                                ALU.subtract, ALU.mult).then_inc(sm["sd"], 1)
                v.wait_ge(sm["sd"], dv(m, 5))
                if m >= 2:
                    v.wait_ge(rsm, 16 * (m // 2))
                v.tensor_add(rb[:, s * D:(s + 1) * D], xts,
                             ln1s).then_inc(sm["sd"], 1)
                v.wait_ge(sm["sd"], dv(m, 6))
                v.tensor_scalar_mul(rb[:, s * D:(s + 1) * D],
                                    rb[:, s * D:(s + 1) * D],
                                    c.cvec[:, s:s + 1]).then_inc(sm["sd"], 1)
                v.reduce_sum(smc(S1B, s), ln1s, axis=AX.X).then_inc(sm["sd"], 1)
                v.wait_ge(sm["sd"], dv(m, 8))
                v.tensor_single_scalar(smc(MU2, s), smc(S1B, s), INV_D,
                                       ALU.mult).then_inc(sm["sd"], 1)
                v.wait_ge(sm["sd"], dv(m, 9))
                v.tensor_mul(smc(MUSQ2, s), smc(MU2, s),
                             smc(MU2, s)).then_inc(sm["sd"], 1)
                v.wait_ge(sm["sa"], act(m, 4))
                v.wait_ge(sm["sd"], dv(m, 10))
                if m >= 2:
                    v.wait_ge(sm["sa"], act(m - 2, 5))
                v.tensor_scalar(smc(VAR2, s), smc(SSQB, s), INV_D,
                                smc(MUSQ2, s), ALU.mult,
                                ALU.subtract).then_inc(sm["sd"], 1)
                v.wait_ge(sm["sa"], act(m, 6))
                v.tensor_scalar(xnf[:, s * D:(s + 1) * D], ln1s, smc(MU2, s),
                                smc(RSTD2, s), ALU.subtract,
                                ALU.mult).then_inc(sm["sd"], 1)
                v.wait_ge(sm["sd"], dv(m, 12))
                v.tensor_mul(xnf[:, s * D:(s + 1) * D],
                             xnf[:, s * D:(s + 1) * D],
                             c.gamma_b[:]).then_inc(sm["sd"], 1)
                v.wait_ge(sm["sd"], dv(m, 13))
                if m >= 2:
                    v.wait_ge(sm["sp"], (m - 2) * 2 + 2)
                v.tensor_add(xnb[:, s * D:(s + 1) * D],
                             xnf[:, s * D:(s + 1) * D],
                             c.beta_b[:]).then_inc(sm["sd"], 1)
                sg = m % 2
                xsm = sm["xs0"] if sg == 0 else sm["xs1"]
                col0 = sg * 2048
                if m >= 2:
                    v.wait_ge(xsm, 16 * (m // 2))
                v.wait_ge(sm["sp"], m * 2 + 1)
                v.tensor_copy(stage[:, col0:col0 + 1024],
                              psT[:, (m % 2) * D:(m % 2) * D + 1024]
                              ).then_inc(sm["sd"], 1)
                v.wait_ge(sm["sp"], m * 2 + 2)
                v.tensor_copy(stage[:, col0 + 1024:col0 + 2048],
                              psT[:, (m % 2) * D + 1024:(m % 2) * D + 2048]
                              ).then_inc(sm["sd"], 1)

        @blk.tensor
        def _(te):
            for m in range(NT_FULL):
                s = m % 2
                te.wait_ge(sm["sd"], dv(m, 14))
                if m >= 2:
                    te.wait_ge(sm["sd"], dv(m - 2, 16))
                for j in range(16):
                    ins = te.transpose(
                        psT[:, (m % 2) * D + j * P:(m % 2) * D + (j + 1) * P],
                        xnb[:, s * D + j * P:s * D + (j + 1) * P],
                        c.ident_bf[:])
                    if j == 7 or j == 15:
                        ins.then_inc(sm["sp"], 1)

        @blk.gpsimd
        def _(g_):
            for m in range(NT_FULL):
                if m % 4 == 3:
                    emit_conv(c, g_, 1)
                s = m % 2
                rsm = sm["rs0"] if s == 0 else sm["rs1"]
                g_.wait_ge(sm["sd"], dv(m, 7))
                g_.wait_ge(rsm, 16 * (m // 2))
                if s == 1:
                    g_.wait_ge(sm["rs0"], 16 * (m // 2 + 1))
                row0 = (m // 2) * P
                if s == 0:
                    g_.dma_start(c.r_s[row0:row0 + P, :],
                                 rb[:, 0:D]).then_inc(rsm, 16)
                else:
                    g_.dma_start(c.r_s[row0:row0 + P, :], rb[:, D:2 * D],
                                 accum_op=ALU.add).then_inc(rsm, 16)
                sg = m % 2
                xsm = sm["xs0"] if sg == 0 else sm["xs1"]
                g_.wait_ge(sm["sd"], dv(m, 16))
                g_.wait_ge(xsm, 16 * (m // 2))
                g_.dma_start(
                    c.xn_s[:, :, m * P:(m + 1) * P].rearrange("ch p j -> p ch j"),
                    stage[:, sg * 2048:(sg + 1) * 2048].rearrange(
                        "p (ch j) -> p ch j", ch=16, j=P),
                ).then_inc(xsm, 16)
            g_.wait_ge(sm["rs0"], 16 * 16)
            g_.wait_ge(sm["rs1"], 16 * 16)
            g_.wait_ge(sm["xs0"], 16 * 16)
            g_.wait_ge(sm["xs1"], 16 * 16)


def emit_p1(c):
    nc = c.nc

    sm = mksems(c, ["aw00", "aw01", "aw10", "aw11", "aw20", "aw21",
                    "aw30", "aw31", "ax0", "ax1", "ax2", "amm", "aev",
                    "ast0", "ast1"])

    def wsem(cc):
        return sm[f"aw{cc % 4}{(cc // 4) % 2}"]

    NT = 8

    with (
        nc.sbuf_tensor("p1_wqk", [P, 32 * D], BF16) as wqk,
        nc.sbuf_tensor("p1_xns", [P, 3 * 8192], BF16) as xns,
        nc.sbuf_tensor("p1_kst", [P, 2 * 2048], BF16) as kst,
        nc.psum_tensor("p1_ps", [P, 8 * 512], F32) as ps,
        nc.Block() as blk,
    ):
        @blk.gpsimd
        def _(g):
            for cc in range(32):
                if cc % 6 == 5:
                    emit_conv(c, g, 1)
                if cc >= 8:
                    g.wait_ge(wsem(cc), 16 * (cc // 8))
                g.dma_start(
                    wqk[:, cc * D:(cc + 1) * D],
                    c.qkvh_in[cc].rearrange("p dk j -> p (dk j)"),
                ).then_inc(wsem(cc), 16)
            emit_conv(c, g, 2)

        @blk.sync
        def _(sy):
            for t in range(NT):
                xs = t % 3
                xsm = sm[f"ax{xs}"]
                if t >= 3:
                    sy.wait_ge(xsm, 16 * (t // 3))
                    sy.wait_ge(sm["amm"], 32 * (t - 2))
                sy.dma_start(
                    xns[:, xs * 8192:(xs + 1) * 8192]
                    .rearrange("p (ch j) -> p ch j", ch=NCD, j=512),
                    c.xn_s[:, :, t * 512:(t + 1) * 512]
                    .rearrange("ch p j -> p ch j"),
                ).then_inc(xsm, 16)

        @blk.tensor
        def _(te):
            for t in range(NT):
                xs = t % 3
                xsm = sm[f"ax{xs}"]
                te.wait_ge(xsm, 16 * (t // 3 + 1))
                for cc in range(32):
                    if t == 0:
                        te.wait_ge(wsem(cc), 16 * (cc // 8 + 1))
                    i = t * 32 + cc
                    b = i % 8
                    if i >= 8:
                        te.wait_ge(sm["aev"], i - 7)
                    for dk in range(NCD):
                        ins = te.matmul(
                            ps[:, b * 512:(b + 1) * 512],
                            wqk[:, cc * D + dk * P:cc * D + (dk + 1) * P],
                            xns[:, xs * 8192 + dk * 512:
                                xs * 8192 + (dk + 1) * 512],
                            start=(dk == 0), stop=(dk == NCD - 1))
                    ins.then_inc(sm["amm"], 1)

        @blk.vector
        def _(v):
            for i in range(NT * 32):
                grp = i // 4
                si = grp % 2
                stm = sm["ast0"] if si == 0 else sm["ast1"]
                v.wait_ge(sm["amm"], i + 1)
                if i % 4 == 0 and grp >= 2:
                    v.wait_ge(stm, 16 * (grp // 2))
                b = i % 8
                v.tensor_copy(kst[:, si * 2048 + (i % 4) * 512:
                                  si * 2048 + (i % 4 + 1) * 512],
                              ps[:, b * 512:(b + 1) * 512]).then_inc(sm["aev"], 1)

        @blk.scalar
        def _(a):
            for t in range(NT):
                for cc4 in range(8):
                    grp = t * 8 + cc4
                    si = grp % 2
                    stm = sm["ast0"] if si == 0 else sm["ast1"]
                    a.wait_ge(sm["aev"], grp * 4 + 4)
                    a.wait_ge(stm, 16 * (grp // 2))
                    cc0 = cc4 * 4
                    dst = (c.q_s if cc0 < NCD else c.k_s)[
                        (cc0 % NCD):(cc0 % NCD) + 4, :,
                        t * 512:(t + 1) * 512].rearrange("ch p j -> p ch j")
                    a.dma_start(dst,
                                kst[:, si * 2048:(si + 1) * 2048].rearrange(
                                    "p (ch j) -> p ch j", ch=4, j=512)
                                ).then_inc(stm, 16)
            a.wait_ge(sm["ast0"], 16 * 32)
            a.wait_ge(sm["ast1"], 16 * 32)

    sm2 = mksems(c, ["vw", "vx0", "vx1", "vmm", "vev", "vst0", "vst1"])
    with (
        nc.sbuf_tensor("p1_wv", [P, NCD * D], BF16) as wv,
        nc.sbuf_tensor("p1_xnsb", [P, 2 * 8192], BF16) as xns,
        nc.sbuf_tensor("p1_vst", [P, 2 * D], BF16) as vst,
        nc.psum_tensor("p1_pv", [P, 2 * D], F32) as pv,
        nc.Block() as blk,
    ):
        @blk.gpsimd
        def _(g):
            g.dma_start(
                wv[:].rearrange("p (dk j) -> p dk j", dk=NCD, j=D),
                c.qkv_in[:, 2 * D:3 * D]
                .rearrange("(dk p) j -> p dk j", dk=NCD, p=P),
            ).then_inc(sm2["vw"], 16)
            emit_conv(c, g, 2)

        @blk.sync
        def _(sy):
            for t in range(NT):
                xs = t % 2
                xsm = sm2["vx0"] if xs == 0 else sm2["vx1"]
                if t >= 2:
                    sy.wait_ge(xsm, 16 * (t // 2))
                    sy.wait_ge(sm2["vmm"], 4 * (t - 1))
                sy.dma_start(
                    xns[:, xs * 8192:(xs + 1) * 8192]
                    .rearrange("p (ch j) -> p ch j", ch=NCD, j=512),
                    c.xn_s[:, :, t * 512:(t + 1) * 512]
                    .rearrange("ch p j -> p ch j"),
                ).then_inc(xsm, 16)

        @blk.tensor
        def _(te):
            te.wait_ge(sm2["vw"], 16)
            for t in range(NT):
                xs = t % 2
                xsm = sm2["vx0"] if xs == 0 else sm2["vx1"]
                te.wait_ge(xsm, 16 * (t // 2 + 1))
                for tc in range(4):
                    i = t * 4 + tc
                    pslot = i % 2
                    if i >= 2:
                        te.wait_ge(sm2["vev"], i - 1)
                    for n in range(4):
                        for dk in range(NCD):
                            ins = te.matmul(
                                pv[:, pslot * D + n * 512:pslot * D + (n + 1) * 512],
                                xns[:, xs * 8192 + dk * 512 + tc * P:
                                    xs * 8192 + dk * 512 + (tc + 1) * P],
                                wv[:, dk * D + n * 512:dk * D + (n + 1) * 512],
                                start=(dk == 0), stop=(dk == NCD - 1))
                    ins.then_inc(sm2["vmm"], 1)

        @blk.vector
        def _(v):
            for i in range(NT * 4):
                pslot = i % 2
                v.wait_ge(sm2["vmm"], i + 1)
                stm = sm2["vst0"] if pslot == 0 else sm2["vst1"]
                if i >= 2:
                    v.wait_ge(stm, 16 * (i // 2))
                v.tensor_copy(vst[:, pslot * D:(pslot + 1) * D],
                              pv[:, pslot * D:(pslot + 1) * D]
                              ).then_inc(sm2["vev"], 1)

        @blk.scalar
        def _(a):
            for i in range(NT * 4):
                pslot = i % 2
                stm = sm2["vst0"] if pslot == 0 else sm2["vst1"]
                a.wait_ge(sm2["vev"], i + 1)
                a.wait_ge(stm, 16 * (i // 2))
                a.dma_start(c.v_s[i * P:(i + 1) * P, :],
                            vst[:, pslot * D:(pslot + 1) * D]
                            ).then_inc(stm, 16)
            a.wait_ge(sm2["vst0"], 16 * 16)
            a.wait_ge(sm2["vst1"], 16 * 16)


def emit_p2(c):
    nc = c.nc
    sm = mksems(c, ["qg0", "qg1", "kb0", "kb1", "kb2", "vb0", "vb1", "vb2",
                    "amm", "aev", "ssm", "sact", "spk", "pev",
                    "bmm", "bev", "ao0", "ao1"])

    NG = 16

    def nblk(g):
        return (g + 2) // 2

    cumA = [0] * (NG + 1)
    cumB = [0] * (NG + 1)
    for g in range(NG):
        cumA[g + 1] = cumA[g] + nblk(g) * 2
        cumB[g + 1] = cumB[g] + nblk(g) * 8
    klist = [(g, j) for g in range(NG) for j in range(nblk(g))]
    NKL = len(klist)

    pk_of_tile = {}
    aopk_of_tile = {}
    pk = 0
    for g in range(NG):
        for q in range(2):
            m = 2 * g + q
            npk = (m + 1 + 7) // 8
            pk_of_tile[m] = (pk, npk)
            pk += npk
        for q in range(2):
            m = 2 * g + q
            aopk_of_tile[m] = pk
            pk += 2

    def ammv(g, j, q):
        return cumA[g] + j * 2 + q + 1

    def bmmv(g, jb, q, dvN):
        return cumB[g] + jb * 8 + q * 4 + dvN + 1

    def ssm_max(m):
        return 8 * (m // 2) + 2 * (m % 2) + 1

    def ssm_bias(m):
        return 8 * (m // 2) + 2 * (m % 2) + 2

    def ssm_recip(m):
        return 8 * (m // 2) + 4 + (m % 2) + 1

    def ssm_aosc(m):
        return 8 * (m // 2) + 6 + (m % 2) + 1

    with (
        nc.sbuf_tensor("p2_qg", [P, 2 * NCD * 256], BF16) as qg,
        nc.sbuf_tensor("p2_kb", [P, 3 * 8192], BF16) as kb,
        nc.sbuf_tensor("p2_vb", [P, 3 * 8192], BF16) as vb,
        nc.sbuf_tensor("p2_strip", [P, 2 * SEQ], F32) as strip,
        nc.sbuf_tensor("p2_pb", [P, SEQ], BF16) as pb,
        nc.sbuf_tensor("p2_pt", [P, 2 * SEQ], BF16) as pt,
        nc.sbuf_tensor("p2_aoacc", [P, 2 * D], F32) as aoacc,
        nc.sbuf_tensor("p2_aotm", [P, 2 * D], BF16) as aotm,
        nc.sbuf_tensor("p2_aost", [P, 2 * D], BF16) as aost,
        nc.sbuf_tensor("p2_linv", [P, 4], F32) as linv,
        nc.sbuf_tensor("p2_smm", [P, 8], F32) as smm,
        nc.psum_tensor("p2_psA", [P, 6 * 512], F32) as psA,
        nc.psum_tensor("p2_psT", [P, 2 * 1024], BF16) as psT,
        nc.Block() as blk,
    ):
        @blk.sync
        def _(sy):
            kcnt = 0
            vcnt = 0
            for g in range(NG):
                qs = g % 2
                qsm = sm["qg0"] if qs == 0 else sm["qg1"]
                if g >= 2:
                    sy.wait_ge(qsm, 16 * (g // 2))
                    sy.wait_ge(sm["amm"], cumA[g - 1])
                sy.dma_start(
                    qg[:, qs * NCD * 256:(qs + 1) * NCD * 256]
                    .rearrange("p (ch j) -> p ch j", ch=NCD, j=256),
                    c.q_s[:, :, g * 256:(g + 1) * 256]
                    .rearrange("ch p j -> p ch j"),
                ).then_inc(qsm, 16)
                for j in range(nblk(g)):
                    ks = kcnt % 3
                    ksm = sm[f"kb{ks}"]
                    if kcnt >= 3:
                        sy.wait_ge(ksm, 16 * (kcnt // 3))
                        gp, jp = klist[kcnt - 3]
                        sy.wait_ge(sm["amm"], cumA[gp] + jp * 2 + 2)
                    sy.dma_start(
                        kb[:, ks * 8192:(ks + 1) * 8192]
                        .rearrange("p (ch j) -> p ch j", ch=NCD, j=512),
                        c.k_s[:, :, j * 512:(j + 1) * 512]
                        .rearrange("ch p j -> p ch j"),
                    ).then_inc(ksm, 16)
                    kcnt += 1
                for jb in range(nblk(g)):
                    vs = vcnt % 3
                    vsm = sm[f"vb{vs}"]
                    if vcnt >= 3:
                        sy.wait_ge(vsm, 16 * (vcnt // 3))
                        gp, jp = klist[vcnt - 3]
                        sy.wait_ge(sm["bmm"], cumB[gp] + jp * 8 + 8)
                    sy.dma_start(
                        vb[:, vs * 8192:(vs + 1) * 8192]
                        .rearrange("p (tc d) -> p tc d", tc=4, d=D),
                        c.v_s[jb * 512:(jb + 1) * 512, :]
                        .rearrange("(tc p) d -> p tc d", tc=4, p=P),
                    ).then_inc(vsm, 16)
                    vcnt += 1

        @blk.gpsimd
        def _(g_):
            for g in range(NG):
                if g in (3, 6, 9, 12):
                    emit_conv(c, g_, 1)
                for q in range(2):
                    m = 2 * g + q
                    s = m % 2
                    aosm = sm["ao0"] if s == 0 else sm["ao1"]
                    g_.wait_ge(sm["pev"], aopk_of_tile[m] + 2)
                    g_.wait_ge(aosm, 16 * (m // 2))
                    if s == 1:
                        g_.wait_ge(sm["ao0"], 16 * (m // 2 + 1))
                    k = m // 2
                    dst = c.ao_s[:, :, k * P:(k + 1) * P].rearrange(
                        "ch p j -> p ch j")
                    srcv = aost[:, s * D:(s + 1) * D].rearrange(
                        "p (ch j) -> p ch j", ch=NCD, j=P)
                    if s == 0:
                        g_.dma_start(dst, srcv).then_inc(aosm, 16)
                    else:
                        g_.dma_start(dst, srcv,
                                     accum_op=ALU.add).then_inc(aosm, 16)
            g_.wait_ge(sm["ao0"], 16 * 16)
            g_.wait_ge(sm["ao1"], 16 * 16)

        @blk.tensor
        def _(te):
            kcnt = 0
            vcnt = 0
            for g in range(NG):
                qs = g % 2
                qsm = sm["qg0"] if qs == 0 else sm["qg1"]
                te.wait_ge(qsm, 16 * (g // 2 + 1))
                for j in range(nblk(g)):
                    ks = kcnt % 3
                    ksm = sm[f"kb{ks}"]
                    te.wait_ge(ksm, 16 * (kcnt // 3 + 1))
                    kcnt += 1
                    for q in range(2):
                        i = cumA[g] + j * 2 + q
                        b = i % 6
                        if i >= 6:
                            te.wait_ge(sm["aev"], i - 5)
                        for dk in range(NCD):
                            ins = te.matmul(
                                psA[:, b * 512:(b + 1) * 512],
                                qg[:, qs * NCD * 256 + dk * 256 + q * P:
                                   qs * NCD * 256 + dk * 256 + (q + 1) * P],
                                kb[:, ks * 8192 + dk * 512:
                                   ks * 8192 + (dk + 1) * 512],
                                start=(dk == 0), stop=(dk == NCD - 1))
                        ins.then_inc(sm["amm"], 1)
                for q in range(2):
                    m = 2 * g + q
                    pk0, npk = pk_of_tile[m]
                    te.wait_ge(sm["sact"], m + 1)
                    ntk = m + 1
                    for pkk in range(npk):
                        slot = (pk0 + pkk) % 2
                        if pk0 + pkk >= 2:
                            te.wait_ge(sm["pev"], pk0 + pkk - 1)
                        lo = pkk * 8
                        hi = min(lo + 8, ntk)
                        for tk in range(lo, hi):
                            ins = te.transpose(
                                psT[:, slot * 1024 + (tk - lo) * P:
                                    slot * 1024 + (tk - lo + 1) * P],
                                pb[:, tk * P:(tk + 1) * P],
                                c.ident_bf[:])
                        ins.then_inc(sm["spk"], 1)
                for jb in range(nblk(g)):
                    vs = vcnt % 3
                    vsm = sm[f"vb{vs}"]
                    te.wait_ge(vsm, 16 * (vcnt // 3 + 1))
                    vcnt += 1
                    for q in range(2):
                        m = 2 * g + q
                        pk0, npk = pk_of_tile[m]
                        te.wait_ge(sm["pev"], pk0 + npk)
                        for dvN in range(4):
                            i = cumB[g] + jb * 8 + q * 4 + dvN
                            b = i % 6
                            if i >= 6:
                                te.wait_ge(sm["bev"], i - 5)
                            tks = [tk for tk in range(jb * 4, jb * 4 + 4)
                                   if tk <= m]
                            for nn, tk in enumerate(tks):
                                ins = te.matmul(
                                    psA[:, b * 512:(b + 1) * 512],
                                    pt[:, q * SEQ + tk * P:
                                       q * SEQ + (tk + 1) * P],
                                    vb[:, vs * 8192 + (tk - jb * 4) * D +
                                       dvN * 512:
                                       vs * 8192 + (tk - jb * 4) * D +
                                       (dvN + 1) * 512],
                                    start=(nn == 0), stop=(nn == len(tks) - 1))
                            ins.then_inc(sm["bmm"], 1)
                for q in range(2):
                    m = 2 * g + q
                    te.wait_ge(sm["ssm"], ssm_aosc(m))
                    for half in range(2):
                        pkk = aopk_of_tile[m] + half
                        slot = pkk % 2
                        if pkk >= 2:
                            te.wait_ge(sm["pev"], pkk - 1)
                        for jj in range(8):
                            ch = half * 8 + jj
                            ins = te.transpose(
                                psT[:, slot * 1024 + jj * P:
                                    slot * 1024 + (jj + 1) * P],
                                aotm[:, (m % 2) * D + ch * P:
                                     (m % 2) * D + (ch + 1) * P],
                                c.ident_bf[:])
                        ins.then_inc(sm["spk"], 1)

        @blk.vector
        def _(v):
            for g in range(NG):
                for j in range(nblk(g)):
                    for q in range(2):
                        i = cumA[g] + j * 2 + q
                        b = i % 6
                        v.wait_ge(sm["amm"], i + 1)
                        m = 2 * g + q
                        if m >= 2:
                            v.wait_ge(sm["sact"], m - 2 + 1)
                        dst = strip[:, q * SEQ + j * 512:q * SEQ + (j + 1) * 512]
                        if j == nblk(g) - 1:
                            s0 = 512 - 128 * (m % 4)
                            v.tensor_add(dst, psA[:, b * 512:(b + 1) * 512],
                                         c.mask_m[:, s0:s0 + 512]
                                         ).then_inc(sm["aev"], 1)
                        else:
                            v.tensor_copy(dst, psA[:, b * 512:(b + 1) * 512]
                                          ).then_inc(sm["aev"], 1)
                for q in range(2):
                    m = 2 * g + q
                    L = 512 * nblk(g)
                    v.wait_ge(sm["aev"], cumA[g] + (nblk(g) - 1) * 2 + q + 1)
                    v.reduce_max(smm[:, q:q + 1], strip[:, q * SEQ:q * SEQ + L],
                                 axis=AX.X).then_inc(sm["ssm"], 1)
                    v.wait_ge(sm["ssm"], ssm_max(m))
                    if m >= 2:
                        v.wait_ge(sm["sact"], m - 2 + 1)
                    v.tensor_single_scalar(smm[:, 4 + q:5 + q], smm[:, q:q + 1],
                                           -SM_SCALE, ALU.mult
                                           ).then_inc(sm["ssm"], 1)
                for q in range(2):
                    m = 2 * g + q
                    pk0, npk = pk_of_tile[m]
                    ntk = m + 1
                    for pkk in range(npk):
                        gi = pk0 + pkk
                        slot = gi % 2
                        v.wait_ge(sm["spk"], gi + 1)
                        lo = pkk * 8
                        hi = min(lo + 8, ntk)
                        if m >= 2:
                            v.wait_ge(sm["bmm"],
                                      bmmv(g - 1, nblk(g - 1) - 1, q, 3))
                        v.tensor_copy(pt[:, q * SEQ + lo * P:q * SEQ + hi * P],
                                      psT[:, slot * 1024:
                                          slot * 1024 + (hi - lo) * P]
                                      ).then_inc(sm["pev"], 1)
                    v.wait_ge(sm["sact"], m + 1)
                    v.reciprocal(linv[:, (g % 2) * 2 + q:(g % 2) * 2 + q + 1],
                                 smm[:, 6 + q:7 + q]
                                 ).then_inc(sm["ssm"], 1)
                for jb in range(nblk(g)):
                    for q in range(2):
                        for dvN in range(4):
                            i = cumB[g] + jb * 8 + q * 4 + dvN
                            b = i % 6
                            v.wait_ge(sm["bmm"], i + 1)
                            dst = aoacc[:, q * D + dvN * 512:
                                        q * D + (dvN + 1) * 512]
                            if jb == 0:
                                v.tensor_copy(dst, psA[:, b * 512:(b + 1) * 512]
                                              ).then_inc(sm["bev"], 1)
                            else:
                                v.tensor_add(dst, dst,
                                             psA[:, b * 512:(b + 1) * 512]
                                             ).then_inc(sm["bev"], 1)
                for q in range(2):
                    m = 2 * g + q
                    v.wait_ge(sm["bev"], bmmv(g, nblk(g) - 1, q, 3))
                    if m >= 2:
                        v.wait_ge(sm["spk"], aopk_of_tile[m - 2] + 2)
                    v.tensor_scalar(aotm[:, (m % 2) * D:(m % 2 + 1) * D],
                                    aoacc[:, q * D:(q + 1) * D],
                                    linv[:, (g % 2) * 2 + q:(g % 2) * 2 + q + 1],
                                    c.cvec[:, m % 2:m % 2 + 1],
                                    ALU.mult, ALU.mult).then_inc(sm["ssm"], 1)
                for q in range(2):
                    m = 2 * g + q
                    s = m % 2
                    aosm = sm["ao0"] if s == 0 else sm["ao1"]
                    for half in range(2):
                        pkk = aopk_of_tile[m] + half
                        slot = pkk % 2
                        v.wait_ge(sm["spk"], pkk + 1)
                        if half == 0:
                            v.wait_ge(aosm, 16 * (m // 2))
                        v.tensor_copy(aost[:, s * D + half * 1024:
                                           s * D + (half + 1) * 1024],
                                      psT[:, slot * 1024:(slot + 1) * 1024]
                                      ).then_inc(sm["pev"], 1)

        @blk.scalar
        def _(a):
            for g in range(NG):
                for q in range(2):
                    m = 2 * g + q
                    L = 512 * nblk(g)
                    a.wait_ge(sm["ssm"], ssm_bias(m))
                    if m >= 2:
                        a.wait_ge(sm["ssm"], ssm_recip(m - 2))
                    if m >= 1:
                        pk0p, npkp = pk_of_tile[m - 1]
                        a.wait_ge(sm["spk"], pk0p + npkp)
                    a.activation(pb[:, 0:L],
                                 strip[:, q * SEQ:q * SEQ + L], AF.Exp,
                                 scale=SM_SCALE, bias=smm[:, 4 + q:5 + q],
                                 accum_out=smm[:, 6 + q:7 + q]
                                 ).then_inc(sm["sact"], 1)


def emit_p3(c):
    nc = c.nc
    sm = mksems(c, ["wo", "rt", "ao0", "ao1", "rr0", "rr1", "omm", "tp",
                    "rmm", "sa", "sd", "x10", "x11", "x20", "x21"])

    NDV3 = 35
    NACT3 = 4

    def sd(t, j):
        return t * NDV3 + j

    def sa_(t, j):
        return t * NACT3 + j

    (S1, MU, MUSQ, VAR, LNV, RSTD, SSQ, MX, NEGC, SE, RSE) = range(11)

    with (
        nc.sbuf_tensor("p3_wo", [P, NCD * D], BF16) as wo,
        nc.sbuf_tensor("p3_rsb", [P, NCD * 17], F32) as rsb,
        nc.sbuf_tensor("p3_aob", [P, 2 * D], BF16) as aob,
        nc.sbuf_tensor("p3_rb", [P, 2 * D], F32) as rb,
        nc.sbuf_tensor("p3_x1", [P, 2 * D], F32) as x1b,
        nc.sbuf_tensor("p3_xn2", [P, 2 * D], F32) as xn2,
        nc.sbuf_tensor("p3_xfm", [P, 2 * D], F32) as xfm,
        nc.sbuf_tensor("p3_x2st", [P, 2 * D], BF16) as x2st,
        nc.sbuf_tensor("p3_scrap", [P, D], F32) as scrap,
        nc.sbuf_tensor("p3_sm", [P, 32], F32) as smv,
        nc.sbuf_tensor("p3_lgt", [P, 2 * 32], F32) as lgt,
        nc.sbuf_tensor("p3_prob", [P, 2 * 16], F32) as prob,
        nc.sbuf_tensor("p3_pn", [P, 2 * 16], F32) as pnb,
        nc.sbuf_tensor("p3_w4", [P, 2 * 16], F32) as w4,
        nc.sbuf_tensor("p3_msk", [P, 2 * 16], F32) as mskb,
        nc.psum_tensor("p3_pso", [P, D], F32) as pso,
        nc.psum_tensor("p3_psT", [P, 2 * 512], F32) as psT3,
        nc.psum_tensor("p3_psr", [P, 17], F32) as psr,
        nc.Block() as blk,
    ):
        def smc(idx, s):
            return smv[:, 2 * idx + s:2 * idx + s + 1]

        @blk.gpsimd
        def _(g_):
            g_.dma_start(
                wo[:].rearrange("p (dk j) -> p dk j", dk=NCD, j=D),
                c.wo_in.rearrange("(dk p) j -> p dk j", dk=NCD, p=P),
            ).then_inc(sm["wo"], 16)
            for t in range(NT_OWN):
                s = t % 2
                x1m = sm["x10"] if s == 0 else sm["x11"]
                x2m = sm["x20"] if s == 0 else sm["x21"]
                g_.wait_ge(sm["sd"], sd(t, 1))
                g_.wait_ge(x1m, 16 * (t // 2))
                g_.dma_start(c.out[t * P:(t + 1) * P, :],
                             x1b[:, s * D:(s + 1) * D]).then_inc(x1m, 16)
                g_.wait_ge(sm["sd"], sd(t, 14))
                g_.wait_ge(x2m, 16 * (t // 2))
                g_.dma_start(
                    c.x2_s[:, :, t * P:(t + 1) * P].rearrange("ch p j -> p ch j"),
                    x2st[:, s * D:(s + 1) * D].rearrange(
                        "p (ch j) -> p ch j", ch=NCD, j=P),
                ).then_inc(x2m, 16)
            for n in ("x10", "x11", "x20", "x21"):
                g_.wait_ge(sm[n], 16 * 8)

        @blk.sync
        def _(sy):
            sy.dma_start(rsb[:], c.router_in[:]).then_inc(sm["rt"], 16)
            for t in range(NT_OWN):
                s = t % 2
                aom = sm["ao0"] if s == 0 else sm["ao1"]
                rrm = sm["rr0"] if s == 0 else sm["rr1"]
                if t >= 2:
                    sy.wait_ge(aom, 16 * (t // 2))
                    sy.wait_ge(sm["omm"], t - 1)
                sy.dma_start(
                    aob[:, s * D:(s + 1) * D].rearrange(
                        "p (ch j) -> p ch j", ch=NCD, j=P),
                    c.ao_s[:, :, t * P:(t + 1) * P].rearrange("ch p j -> p ch j"),
                ).then_inc(aom, 16)
                if t >= 2:
                    sy.wait_ge(rrm, 16 * (t // 2))
                    sy.wait_ge(sm["sd"], sd(t - 2, 1))
                sy.dma_start(rb[:, s * D:(s + 1) * D],
                             c.r_s[t * P:(t + 1) * P, :]).then_inc(rrm, 16)

        @blk.tensor
        def _(te):
            te.wait_ge(sm["wo"], 16)
            te.wait_ge(sm["rt"], 16)
            for t in range(NT_OWN):
                s = t % 2
                aom = sm["ao0"] if s == 0 else sm["ao1"]
                te.wait_ge(aom, 16 * (t // 2 + 1))
                if t >= 1:
                    te.wait_ge(sm["sd"], sd(t - 1, 1))
                for n in range(4):
                    for dk in range(NCD):
                        ins = te.matmul(
                            pso[:, n * 512:(n + 1) * 512],
                            aob[:, s * D + dk * P:s * D + (dk + 1) * P],
                            wo[:, dk * D + n * 512:dk * D + (n + 1) * 512],
                            start=(dk == 0), stop=(dk == NCD - 1))
                ins.then_inc(sm["omm"], 1)
                te.wait_ge(sm["sd"], sd(t, 6))
                for pkk in range(4):
                    gi = t * 4 + pkk
                    slot = gi % 2
                    if gi >= 2:
                        tp_, pp_ = divmod(gi - 2, 4)
                        te.wait_ge(sm["sd"], sd(tp_, 8 + 2 * pp_))
                    for jj in range(4):
                        ch = pkk * 4 + jj
                        ins = te.transpose(
                            psT3[:, slot * 512 + jj * P:slot * 512 + (jj + 1) * P],
                            xn2[:, s * D + ch * P:s * D + (ch + 1) * P],
                            c.ident_f32[:])
                    ins.then_inc(sm["tp"], 1)
                te.wait_ge(sm["sd"], sd(t, 14))
                if t >= 2:
                    te.wait_ge(sm["sd"], sd(t - 2, 15))
                for dk in range(NCD):
                    ins = te.matmul(psr[:], xfm[:, s * D + dk * P:
                                               s * D + (dk + 1) * P],
                                    rsb[:, dk * 17:(dk + 1) * 17],
                                    start=(dk == 0), stop=(dk == NCD - 1))
                ins.then_inc(sm["rmm"], 1)

        @blk.scalar
        def _(a):
            for t in range(NT_OWN):
                s = t % 2
                a.wait_ge(sm["sd"], sd(t, 1))
                if t >= 2:
                    a.wait_ge(sm["sd"], sd(t - 2, 5))
                a.activation(scrap[:], x1b[:, s * D:(s + 1) * D], AF.Square,
                             accum_out=smc(SSQ, s)).then_inc(sm["sa"], 1)
                a.wait_ge(sm["sd"], sd(t, 5))
                a.activation(smc(LNV, s), smc(VAR, s), AF.Ln,
                             bias=c.epsv[:]).then_inc(sm["sa"], 1)
                a.wait_ge(sm["sa"], sa_(t, 2))
                a.activation(smc(RSTD, s), smc(LNV, s), AF.Exp,
                             scale=-0.5).then_inc(sm["sa"], 1)
                a.wait_ge(sm["sd"], sd(t, 17))
                if t >= 2:
                    a.wait_ge(sm["sd"], sd(t - 2, 18))
                a.activation(prob[:, s * 16:(s + 1) * 16],
                             lgt[:, s * 32:s * 32 + 16], AF.Exp,
                             bias=smc(NEGC, s),
                             accum_out=smc(SE, s)).then_inc(sm["sa"], 1)
            a.wait_ge(sm["sd"], NT_OWN * NDV3)
            a.activation(c.sgv[:], c.sg_all[:], AF.Sigmoid).then_inc(sm["sa"], 1)

        @blk.vector
        def _(v):
            for t in range(NT_OWN):
                s = t % 2
                rrm = sm["rr0"] if s == 0 else sm["rr1"]
                x1m = sm["x10"] if s == 0 else sm["x11"]
                x2m = sm["x20"] if s == 0 else sm["x21"]
                v.wait_ge(sm["omm"], t + 1)
                v.wait_ge(rrm, 16 * (t // 2 + 1))
                if t >= 2:
                    v.wait_ge(sm["sa"], sa_(t - 2, 1))
                    v.wait_ge(x1m, 16 * (t // 2))
                v.tensor_add(x1b[:, s * D:(s + 1) * D], pso[:],
                             rb[:, s * D:(s + 1) * D]).then_inc(sm["sd"], 1)
                v.wait_ge(sm["sd"], sd(t, 1))
                v.reduce_sum(smc(S1, s), x1b[:, s * D:(s + 1) * D],
                             axis=AX.X).then_inc(sm["sd"], 1)
                v.wait_ge(sm["sd"], sd(t, 2))
                v.tensor_single_scalar(smc(MU, s), smc(S1, s), INV_D,
                                       ALU.mult).then_inc(sm["sd"], 1)
                v.wait_ge(sm["sd"], sd(t, 3))
                v.tensor_mul(smc(MUSQ, s), smc(MU, s),
                             smc(MU, s)).then_inc(sm["sd"], 1)
                v.wait_ge(sm["sa"], sa_(t, 1))
                v.wait_ge(sm["sd"], sd(t, 4))
                if t >= 2:
                    v.wait_ge(sm["sa"], sa_(t - 2, 2))
                v.tensor_scalar(smc(VAR, s), smc(SSQ, s), INV_D, smc(MUSQ, s),
                                ALU.mult, ALU.subtract).then_inc(sm["sd"], 1)
                v.wait_ge(sm["sa"], sa_(t, 3))
                if t >= 1:
                    v.wait_ge(sm["tp"], t * 4)
                v.tensor_scalar(xn2[:, s * D:(s + 1) * D],
                                x1b[:, s * D:(s + 1) * D], smc(MU, s),
                                smc(RSTD, s), ALU.subtract,
                                ALU.mult).then_inc(sm["sd"], 1)
                for pkk in range(4):
                    gi = t * 4 + pkk
                    slot = gi % 2
                    v.wait_ge(sm["tp"], gi + 1)
                    if pkk == 0:
                        if t >= 1:
                            v.wait_ge(sm["rmm"], t)
                        v.wait_ge(x2m, 16 * (t // 2))
                    v.tensor_copy(xfm[:, s * D + pkk * 512:
                                      s * D + (pkk + 1) * 512],
                                  psT3[:, slot * 512:(slot + 1) * 512]
                                  ).then_inc(sm["sd"], 1)
                    v.tensor_copy(x2st[:, s * D + pkk * 512:
                                       s * D + (pkk + 1) * 512],
                                  psT3[:, slot * 512:(slot + 1) * 512]
                                  ).then_inc(sm["sd"], 1)
                v.wait_ge(sm["rmm"], t + 1)
                v.tensor_copy(lgt[:, s * 32:s * 32 + 17],
                              psr[:]).then_inc(sm["sd"], 1)
                v.wait_ge(sm["sd"], sd(t, 15))
                v.reduce_max(smc(MX, s), lgt[:, s * 32:s * 32 + 16],
                             axis=AX.X).then_inc(sm["sd"], 1)
                v.wait_ge(sm["sd"], sd(t, 16))
                if t >= 2:
                    v.wait_ge(sm["sa"], sa_(t - 2, 4))
                v.tensor_single_scalar(smc(NEGC, s), smc(MX, s), -1.0,
                                       ALU.mult).then_inc(sm["sd"], 1)
                v.wait_ge(sm["sa"], sa_(t, 4))
                v.reciprocal(smc(RSE, s), smc(SE, s)).then_inc(sm["sd"], 1)
                v.wait_ge(sm["sd"], sd(t, 18))
                v.tensor_scalar_mul(pnb[:, s * 16:(s + 1) * 16],
                                    prob[:, s * 16:(s + 1) * 16],
                                    smc(RSE, s)).then_inc(sm["sd"], 1)
                v.wait_ge(sm["sd"], sd(t, 19))
                v.tensor_copy(w4[:, s * 16:(s + 1) * 16],
                              pnb[:, s * 16:(s + 1) * 16]).then_inc(sm["sd"], 1)
                for r4 in range(4):
                    v.wait_ge(sm["sd"], sd(t, 20 + r4 * 3))
                    v.reduce_max(smc(MX, s), w4[:, s * 16:(s + 1) * 16],
                                 axis=AX.X).then_inc(sm["sd"], 1)
                    v.wait_ge(sm["sd"], sd(t, 21 + r4 * 3))
                    v.tensor_single_scalar(mskb[:, s * 16:(s + 1) * 16],
                                           w4[:, s * 16:(s + 1) * 16],
                                           smc(MX, s),
                                           ALU.is_ge).then_inc(sm["sd"], 1)
                    v.wait_ge(sm["sd"], sd(t, 22 + r4 * 3))
                    v.scalar_tensor_tensor(w4[:, s * 16:(s + 1) * 16],
                                           mskb[:, s * 16:(s + 1) * 16], -10.0,
                                           w4[:, s * 16:(s + 1) * 16],
                                           ALU.mult, ALU.add
                                           ).then_inc(sm["sd"], 1)
                v.wait_ge(sm["sd"], sd(t, 32))
                v.tensor_single_scalar(mskb[:, s * 16:(s + 1) * 16],
                                       pnb[:, s * 16:(s + 1) * 16], smc(MX, s),
                                       ALU.is_ge).then_inc(sm["sd"], 1)
                v.wait_ge(sm["sd"], sd(t, 33))
                v.tensor_mul(c.coef_all[:, t * 16:(t + 1) * 16],
                             pnb[:, s * 16:(s + 1) * 16],
                             mskb[:, s * 16:(s + 1) * 16]).then_inc(sm["sd"], 1)
                v.tensor_copy(c.sg_all[:, t:t + 1],
                              lgt[:, s * 32 + 16:s * 32 + 17]
                              ).then_inc(sm["sd"], 1)


def emit_p5(c):
    nc = c.nc
    sm = mksems(c, ["x2l", "wb0", "wb1", "wb2", "wb3", "wh0", "wh1",
                    "gmm", "gev", "sact5", "dmm", "dev",
                    "ost0", "ost1", "ost2", "ost3"])
    NOS = 4

    NE = 20
    NIC = NI_EXP
    HWIDTH = NIC * D

    def wsrc(it, ic):
        if it < 4:
            g_ap = c.sgupc[it * NI_EXP + ic].rearrange("p dk j -> p (dk j)")
            u_ap = c.sgupc[44 + it * NI_EXP + ic].rearrange(
                "p dk j -> p (dk j)")
        else:
            e = it - 4
            g_ap = c.w13c[e, ic].rearrange("p dk j -> p (dk j)")
            u_ap = c.w13c[e, NI_EXP + ic].rearrange("p dk j -> p (dk j)")
        return g_ap, u_ap

    def w2src(it, half, ic0, ic1):
        if it < 4:
            return c.sdh_in[it, :, ic0:ic1, half * 1024:(half + 1) * 1024]
        return c.w2h_in[it - 4, :, ic0:ic1, half * 1024:(half + 1) * 1024]

    def coefcol(it, tc):
        if it < 4:
            return c.sgv[:, tc:tc + 1]
        return c.coef_all[:, tc * 16 + (it - 4):tc * 16 + (it - 4) + 1]

    wbs = ["wb0", "wb1", "wb2", "wb3"]

    with (
        nc.sbuf_tensor("p5_x2f", [P, NCD * D], BF16) as x2f,
        nc.sbuf_tensor("p5_h", [P, HWIDTH], BF16) as hact,
        nc.sbuf_tensor("p5_w13", [P, 4 * D], BF16) as w13b,
        nc.sbuf_tensor("p5_w2", [P, 2 * NIC * 1024], BF16) as w2b,
        nc.sbuf_tensor("p5_sgb", [P, 3 * 512], BF16) as sgb,
        nc.sbuf_tensor("p5_stage", [P, NOS * 512], F32) as stage,
        nc.psum_tensor("p5_pg", [P, 3 * 512], F32) as pg,
        nc.psum_tensor("p5_pu", [P, 3 * 512], F32) as pu,
        nc.psum_tensor("p5_pd", [P, 2 * 512], F32) as pd,
        nc.Block() as blk,
    ):
        @blk.sync
        def _(sy):
            sy.dma_start(
                x2f[:].rearrange("p (ch j) -> p ch j", ch=NCD, j=D),
                c.x2_s.rearrange("ch p j -> p ch j"),
            ).then_inc(sm["x2l"], 16)
            sy.wait_ge(c.convsem, 16 * c.conv_count)
            for it in range(NE):
                for ic in range(NIC):
                    for gu in range(2):
                        li = it * 22 + ic * 2 + gu
                        bi = li % 4
                        bsm = sm[wbs[bi]]
                        if li >= 4:
                            sy.wait_ge(bsm, 16 * (li // 4))
                            sy.wait_ge(sm["gmm"], ((li - 4) // 2 + 1) * 4)
                        g_ap, u_ap = wsrc(it, ic)
                        sy.dma_start(w13b[:, bi * D:(bi + 1) * D],
                                     g_ap if gu == 0 else u_ap
                                     ).then_inc(bsm, 16)

        @blk.gpsimd
        def _(g_):
            def emit_pieces(itp, lo, hi):
                for pi in range(itp * 64 + lo, itp * 64 + hi):
                    hs = (pi - itp * 64) // 16
                    tc = pi % 16
                    osl = pi % NOS
                    osm = sm[f"ost{osl}"]
                    g_.wait_ge(sm["dev"], pi + 1)
                    g_.wait_ge(osm, 16 * (pi // NOS))
                    g_.dma_start(
                        c.out[tc * P:(tc + 1) * P, hs * 512:(hs + 1) * 512],
                        stage[:, osl * 512:(osl + 1) * 512],
                        accum_op=ALU.add,
                    ).then_inc(osm, 16)

            for it in range(NE + 1):
                if it < NE:
                    if it >= 1:
                        emit_pieces(it - 1, 0, 32)
                    for h in range(2):
                        hidx = it * 2 + h
                        hsm = sm["wh0"] if hidx % 2 == 0 else sm["wh1"]
                        if h == 1 and it >= 1:
                            emit_pieces(it - 1, 56, 64)
                        if hidx >= 2:
                            g_.wait_ge(hsm, 64 * (hidx // 2))
                            g_.wait_ge(sm["dmm"],
                                       (it - 1) * 64 + (h + 1) * 32)
                        base = (hidx % 2) * NIC * 1024
                        for qq, (ic0, ic1) in enumerate(
                                [(0, 3), (3, 6), (6, 9), (9, 11)]):
                            g_.dma_start(
                                w2b[:, base + ic0 * 1024:base + ic1 * 1024]
                                .rearrange("p (ic j) -> p ic j",
                                           ic=ic1 - ic0, j=1024),
                                w2src(it, h, ic0, ic1),
                            ).then_inc(hsm, 16)
                            if h == 0 and it >= 1 and qq < 3:
                                emit_pieces(it - 1, 32 + qq * 8,
                                            40 + qq * 8)
                else:
                    emit_pieces(it - 1, 0, 64)
            for osl in range(NOS):
                g_.wait_ge(sm[f"ost{osl}"], 16 * (NE * 64 // NOS))

        @blk.tensor
        def _(te):
            te.wait_ge(sm["x2l"], 16)
            for it in range(NE):
                for ic in range(NIC):
                    for gu in range(2):
                        li = it * 22 + ic * 2 + gu
                        bi = li % 4
                        te.wait_ge(sm[wbs[bi]], 16 * (li // 4 + 1))
                    for ts in range(4):
                        pi = (it * NIC + ic) * 4 + ts
                        sl = pi % 3
                        if pi >= 3:
                            te.wait_ge(sm["gev"], pi - 2)
                            te.wait_ge(sm["sact5"], pi - 2)
                        for dk in range(NCD):
                            ins = te.matmul(
                                pg[:, sl * 512:(sl + 1) * 512],
                                w13b[:, ((it * 22 + ic * 2) % 4) * D + dk * P:
                                     ((it * 22 + ic * 2) % 4) * D + (dk + 1) * P],
                                x2f[:, dk * D + ts * 512:dk * D + (ts + 1) * 512],
                                start=(dk == 0), stop=(dk == NCD - 1))
                        for dk in range(NCD):
                            ins = te.matmul(
                                pu[:, sl * 512:(sl + 1) * 512],
                                w13b[:, ((it * 22 + ic * 2 + 1) % 4) * D + dk * P:
                                     ((it * 22 + ic * 2 + 1) % 4) * D + (dk + 1) * P],
                                x2f[:, dk * D + ts * 512:dk * D + (ts + 1) * 512],
                                start=(dk == 0), stop=(dk == NCD - 1))
                        ins.then_inc(sm["gmm"], 1)
                for h in range(2):
                    hidx = it * 2 + h
                    hsm = sm["wh0"] if hidx % 2 == 0 else sm["wh1"]
                    te.wait_ge(hsm, 64 * (hidx // 2 + 1))
                    for hs2 in range(2):
                        hs = h * 2 + hs2
                        for tc in range(16):
                            pi = it * 64 + hs * 16 + tc
                            bank = pi % 2
                            if pi >= 2:
                                te.wait_ge(sm["dev"], pi - 1)
                            if hs == 0 and tc == 0:
                                te.wait_ge(sm["gev"], (it + 1) * 44)
                            for ic in range(NIC):
                                ins = te.matmul(
                                    pd[:, bank * 512:(bank + 1) * 512],
                                    hact[:, ic * D + tc * P:ic * D + (tc + 1) * P],
                                    w2b[:, (hidx % 2) * NIC * 1024 + ic * 1024 +
                                        hs2 * 512:
                                        (hidx % 2) * NIC * 1024 + ic * 1024 +
                                        (hs2 + 1) * 512],
                                    start=(ic == 0), stop=(ic == NIC - 1))
                            ins.then_inc(sm["dmm"], 1)

        @blk.scalar
        def _(a):
            for it in range(NE):
                for ic in range(NIC):
                    for ts in range(4):
                        pi = (it * NIC + ic) * 4 + ts
                        sl = pi % 3
                        a.wait_ge(sm["gmm"], pi + 1)
                        if pi >= 3:
                            a.wait_ge(sm["gev"], pi - 2)
                        a.activation(sgb[:, sl * 512:(sl + 1) * 512],
                                     pg[:, sl * 512:(sl + 1) * 512],
                                     AF.Silu).then_inc(sm["sact5"], 1)

        @blk.vector
        def _(v):
            for it in range(NE):
                for ic in range(NIC):
                    for ts in range(4):
                        pi = (it * NIC + ic) * 4 + ts
                        sl = pi % 3
                        v.wait_ge(sm["sact5"], pi + 1)
                        if ic == 0 and ts == 0 and it >= 1:
                            v.wait_ge(sm["dmm"], it * 64)
                        v.tensor_mul(hact[:, ic * D + ts * 512:
                                          ic * D + (ts + 1) * 512],
                                     sgb[:, sl * 512:(sl + 1) * 512],
                                     pu[:, sl * 512:(sl + 1) * 512]
                                     ).then_inc(sm["gev"], 1)
                for h in range(2):
                    for hs2 in range(2):
                        hs = h * 2 + hs2
                        for tc in range(16):
                            pi = it * 64 + hs * 16 + tc
                            bank = pi % 2
                            osl = pi % NOS
                            osm = sm[f"ost{osl}"]
                            v.wait_ge(sm["dmm"], pi + 1)
                            v.wait_ge(osm, 16 * (pi // NOS))
                            v.tensor_single_scalar(
                                stage[:, osl * 512:(osl + 1) * 512],
                                pd[:, bank * 512:(bank + 1) * 512],
                                coefcol(it, tc), ALU.mult
                            ).then_inc(sm["dev"], 1)



def prep_inputs(L0_attn_x, L0_attn_qkv, L0_attn_o_proj, L0_attn_gamma,
                L0_attn_beta, L0_ffn_gate_w, L0_ffn_sgate_w, L0_ffn_w13,
                L0_ffn_w2, L0_ffn_shared_gate_up_w, L0_ffn_shared_down_w):
    A = np.ascontiguousarray
    x = np.asarray(L0_attn_x)
    qkv = A(np.asarray(L0_attn_qkv))
    wo = A(np.asarray(L0_attn_o_proj))
    gamma = A(np.asarray(L0_attn_gamma).reshape(1, D))
    beta = A(np.asarray(L0_attn_beta).reshape(1, D))
    gate_t = np.asarray(L0_ffn_gate_w).T.reshape(NCD, P, E)
    sgate_t = np.asarray(L0_ffn_sgate_w).T.reshape(NCD, P, 1)
    router = A(np.concatenate([gate_t, sgate_t], axis=2)
               .transpose(1, 0, 2).reshape(P, NCD * (E + 1)))
    w13t = np.asarray(L0_ffn_w13).transpose(0, 2, 1)
    w13h = A(w13t.reshape(E, NCD, P, 22, P).transpose(0, 3, 2, 1, 4))
    w2t = np.asarray(L0_ffn_w2).transpose(0, 2, 1)
    w2h = A(w2t.reshape(E, NI_EXP, P, D).transpose(0, 2, 1, 3))
    sgupt = np.asarray(L0_ffn_shared_gate_up_w).T
    sguph = A(sgupt.reshape(NCD, P, 88, P).transpose(2, 1, 0, 3))
    sdownt = np.asarray(L0_ffn_shared_down_w).T
    sdh = A(sdownt.reshape(4, NI_EXP, P, D).transpose(0, 2, 1, 3))
    qkvh = A(qkv[:, :2 * D].reshape(NCD, P, 32, P).transpose(2, 1, 0, 3))

    in_maps = []
    for core in range(8):
        b = core // 2
        par = core % 2
        csel = np.zeros((1, 2), dtype=np.float32)
        csel[0, par] = 1.0
        in_maps.append({
            "x": A(x[b]),
            "qkv": qkv, "wo": wo, "gamma": gamma, "beta": beta,
            "router": router,
            "qkvh": qkvh, "w13h": w13h, "w2h": w2h, "sguph": sguph,
            "sdh": sdh,
            "csel": csel,
        })
    return in_maps


def assemble_output(results):
    out = np.empty((4, SEQ, D), dtype=np.float32)
    for core in range(8):
        b = core // 2
        par = core % 2
        oc = results[core]["out"]
        for k in range(NT_OWN):
            out[b, 256 * k + P * par:256 * k + P * par + P, :] = \
                oc[k * P:(k + 1) * P, :]
    return out


def kernel(**inputs):
    in_maps = prep_inputs(**inputs)
    nc = build_nc()
    res = run_bass_kernel_spmd(nc, in_maps, core_ids=list(range(8)))
    return assemble_output(res.results)


# version 10
# speedup vs baseline: 1.1215x; 1.1094x over previous
import numpy as np
from contextlib import ExitStack

import concourse.bass as bass
import concourse.mybir as mybir
from concourse.bass_utils import run_bass_kernel_spmd

F32 = mybir.dt.float32
BF16 = mybir.dt.bfloat16
AF = mybir.ActivationFunctionType
ALU = mybir.AluOpType
AX = mybir.AxisListType

P = 128
D = 2048
NCD = D // P
SEQ = 4096
NT_FULL = SEQ // P
TOWN = 2048
NT_OWN = TOWN // P
EPS = 1e-5
SM_SCALE = 1.0 / float(np.sqrt(2048.0))
NEG = -1e30
INV_D = 1.0 / D
E = 16
I_EXP = 1408
NI_EXP = I_EXP // P
I_SH = 5632
NI_SH = I_SH // P

ALL_PHASES = ("p0", "p1", "p2", "p3", "p5")


class Ctx:
    pass


def build_nc(phases=ALL_PHASES, dbg=()):
    nc = bass.Bass("TRN2", target_bir_lowering=False)
    c = Ctx()
    c.nc = nc

    def dram(name, shape, dtype, kind="Internal"):
        k = "ExternalOutput" if (kind == "Internal" and name in dbg) else kind
        return nc.dram_tensor(name, shape, dtype, kind=k)

    c.x_in = dram("x", [SEQ, D], F32, kind="ExternalInput")
    c.qkv_in = dram("qkv", [D, 3 * D], F32, kind="ExternalInput")
    c.qkvh_in = dram("qkvh", [32, P, NCD, P], F32, kind="ExternalInput")
    c.wo_in = dram("wo", [D, D], F32, kind="ExternalInput")
    c.gamma_in = dram("gamma", [1, D], F32, kind="ExternalInput")
    c.beta_in = dram("beta", [1, D], F32, kind="ExternalInput")
    c.router_in = dram("router", [P, NCD * (E + 1)], F32, kind="ExternalInput")
    c.w13h_in = dram("w13h", [E, 22, P, NCD, P], F32, kind="ExternalInput")
    c.w2h_in = dram("w2h", [E, P, NI_EXP, D], F32, kind="ExternalInput")
    c.sguph_in = dram("sguph", [88, P, NCD, P], F32, kind="ExternalInput")
    c.sdh_in = dram("sdh", [4, P, NI_EXP, D], F32, kind="ExternalInput")
    c.csel_in = dram("csel", [1, 2], F32, kind="ExternalInput")

    c.out = dram("out", [TOWN, D], F32, kind="ExternalOutput")

    c.xn_s = dram("xn_s", [NCD, P, SEQ], BF16)
    c.r_s = dram("r_s", [TOWN, D], F32)
    c.q_s = dram("q_s", [NCD, P, SEQ], BF16)
    c.k_s = dram("k_s", [NCD, P, SEQ], BF16)
    c.v_s = dram("v_s", [SEQ, D], BF16)
    c.ao_s = dram("ao_s", [NCD, P, TOWN], BF16)
    c.x2_s = dram("x2_s", [NCD, P, TOWN], BF16)
    c.w13c = dram("w13c", [E, 22, P, NCD, P], BF16)
    c.sgupc = dram("sgupc", [88, P, NCD, P], BF16)

    st = ExitStack()
    c.st = st
    with st:
        c.ident_bf = st.enter_context(nc.sbuf_tensor("ident_bf", [P, P], BF16))
        c.ident_f32 = st.enter_context(nc.sbuf_tensor("ident_f32", [P, P], F32))
        c.cvec = st.enter_context(nc.sbuf_tensor("cvec", [P, 2], F32))
        c.mask_m = st.enter_context(nc.sbuf_tensor("mask_m", [P, 1024], F32))
        c.ones_row = st.enter_context(nc.sbuf_tensor("ones_row", [1, P], F32))
        c.epsv = st.enter_context(nc.sbuf_tensor("epsv", [P, 1], F32))

        c._semn = 0
        c.conv_count = 0
        c.convsem = st.enter_context(nc.semaphore("convsem"))

        c.coef_all = st.enter_context(nc.sbuf_tensor("coef_all", [P, 16 * E], F32))
        c.sg_all = st.enter_context(nc.sbuf_tensor("sg_all", [P, 16], F32))
        c.sgv = st.enter_context(nc.sbuf_tensor("sgv", [P, 16], F32))

        st01 = ExitStack()
        c.gamma_b = st01.enter_context(nc.sbuf_tensor("gamma_b", [P, D], F32))
        c.beta_b = st01.enter_context(nc.sbuf_tensor("beta_b", [P, D], F32))
        c.gb_row = st01.enter_context(nc.sbuf_tensor("gb_row", [1, 2 * D + 2], F32))
        emit_preamble(c)
        if "p0" in phases:
            emit_p0(c)
        st01.close()
        if "p1" in phases:
            emit_p1(c)
        if "p2" in phases:
            emit_p2(c)
        if "p3" in phases:
            emit_p3(c)
        if "p5" in phases:
            emit_p5(c)

    return nc


def emit_conv(c, g_, n):
    for _ in range(n):
        u = c.conv_count
        if u >= 20:
            return
        if u > 0:
            g_.wait_ge(c.convsem, 16 * u)
        if u < 16:
            srcv = c.w13h_in[u].rearrange("a p c j -> (a p) (c j)")
            dstv = c.w13c[u].rearrange("a p c j -> (a p) (c j)")
        else:
            q = u - 16
            srcv = c.sguph_in[q * 22:(q + 1) * 22].rearrange(
                "a p c j -> (a p) (c j)")
            dstv = c.sgupc[q * 22:(q + 1) * 22].rearrange(
                "a p c j -> (a p) (c j)")
        g_.dma_start(dstv, srcv).then_inc(c.convsem, 16)
        c.conv_count += 1


def mksems(c, names):
    d = {}
    for n in names:
        c._semn += 1
        d[n] = c.st.enter_context(c.nc.semaphore(f"{n}_{c._semn}"))
    return d


def emit_preamble(c):
    nc = c.nc
    sm = mksems(c, ["pm", "pld", "ppe", "pdv"])
    with (
        nc.psum_tensor("ps_bc", [P, D], F32) as ps_bc,
        nc.psum_tensor("ps_bc2", [P, 2], F32) as ps_bc2,
        nc.Block() as blk,
    ):
        @blk.gpsimd
        def _(g):
            g.memset(c.ident_bf[:], 0.0).then_inc(sm["pm"], 1)
            g.wait_ge(sm["pm"], 1)
            g.affine_select(out=c.ident_bf[:], in_=c.ident_bf[:],
                            compare_op=ALU.not_equal, fill=1.0, base=0,
                            pattern=[[-1, P]], channel_multiplier=1)
            g.memset(c.ident_f32[:], 0.0).then_inc(sm["pm"], 1)
            g.wait_ge(sm["pm"], 2)
            g.affine_select(out=c.ident_f32[:], in_=c.ident_f32[:],
                            compare_op=ALU.not_equal, fill=1.0, base=0,
                            pattern=[[-1, P]], channel_multiplier=1)
            g.memset(c.mask_m[:], 0.0).then_inc(sm["pm"], 1)
            g.wait_ge(sm["pm"], 3)
            g.affine_select(out=c.mask_m[:], in_=c.mask_m[:],
                            compare_op=ALU.is_ge, fill=NEG, base=512,
                            pattern=[[-1, 1024]], channel_multiplier=1)
            g.memset(c.ones_row[:], 1.0)
            g.memset(c.epsv[:], EPS).then_inc(sm["pm"], 1)

        @blk.sync
        def _(sy):
            sy.dma_start(c.gb_row[:, 0:D], c.gamma_in[:]).then_inc(sm["pld"], 16)
            sy.wait_ge(sm["pld"], 16)
            sy.dma_start(c.gb_row[:, D:2 * D], c.beta_in[:]).then_inc(sm["pld"], 16)
            sy.wait_ge(sm["pld"], 32)
            sy.dma_start(c.gb_row[:, 2 * D:2 * D + 2],
                         c.csel_in[:]).then_inc(sm["pld"], 16)

        @blk.tensor
        def _(te):
            te.wait_ge(sm["pld"], 48)
            te.wait_ge(sm["pm"], 4)
            for j in range(4):
                ins = te.matmul(ps_bc[:, j * 512:(j + 1) * 512], c.ones_row[:],
                                c.gb_row[:, j * 512:(j + 1) * 512],
                                start=True, stop=True)
            ins.then_inc(sm["ppe"], 1)
            te.wait_ge(sm["pdv"], 1)
            for j in range(4):
                te.matmul(ps_bc[:, j * 512:(j + 1) * 512], c.ones_row[:],
                          c.gb_row[:, D + j * 512:D + (j + 1) * 512],
                          start=True, stop=True)
            te.matmul(ps_bc2[:], c.ones_row[:],
                      c.gb_row[:, 2 * D:2 * D + 2],
                      start=True, stop=True).then_inc(sm["ppe"], 1)

        @blk.vector
        def _(v):
            v.wait_ge(sm["ppe"], 1)
            v.tensor_copy(c.gamma_b[:], ps_bc[:]).then_inc(sm["pdv"], 1)
            v.wait_ge(sm["ppe"], 2)
            v.tensor_copy(c.beta_b[:], ps_bc[:]).then_inc(sm["pdv"], 1)
            v.tensor_copy(c.cvec[:], ps_bc2[:]).then_inc(sm["pdv"], 1)


def emit_p0(c):
    nc = c.nc
    sm = mksems(c, ["ld0", "ld1", "rs0", "rs1", "xs0", "xs1",
                    "sa", "sd", "sp"])
    NDV = 16
    NACT = 6

    def dv(m, j):
        return m * NDV + j

    def act(m, j):
        return m * NACT + j

    S1, SSQ, MU, MUSQ, VAR, LNV, RSTD = 0, 1, 2, 3, 4, 5, 6
    S1B, SSQB, MU2, MUSQ2, VAR2, LNV2, RSTD2 = 7, 8, 9, 10, 11, 12, 13

    with (
        nc.sbuf_tensor("p0_xt", [P, 2 * D], F32) as xt,
        nc.sbuf_tensor("p0_scrap", [P, D], F32) as scrap,
        nc.sbuf_tensor("p0_sm", [P, 32], F32) as smv,
        nc.sbuf_tensor("p0_ln1", [P, 2 * D], F32) as ln1,
        nc.sbuf_tensor("p0_rb", [P, 2 * D], F32) as rb,
        nc.sbuf_tensor("p0_xnf", [P, 2 * D], F32) as xnf,
        nc.sbuf_tensor("p0_xnb", [P, 2 * D], BF16) as xnb,
        nc.sbuf_tensor("p0_stage", [P, 2 * 2048], BF16) as stage,
        nc.psum_tensor("p0_psT", [P, 2 * D], BF16) as psT,
        nc.Block() as blk,
    ):
        def smc(idx, s):
            return smv[:, 2 * idx + s:2 * idx + s + 1]

        @blk.sync
        def _(sy):
            for m in range(NT_FULL):
                s = m % 2
                ldm = sm["ld0"] if s == 0 else sm["ld1"]
                if m >= 2:
                    sy.wait_ge(sm["sd"], dv(m - 2, 7))
                    sy.wait_ge(sm["sa"], act(m - 2, 1))
                    sy.wait_ge(ldm, 16 * (m // 2))
                sy.dma_start(xt[:, s * D:(s + 1) * D],
                             c.x_in[m * P:(m + 1) * P, :]).then_inc(ldm, 16)

        @blk.scalar
        def _(a):
            for m in range(NT_FULL):
                s = m % 2
                ldm = sm["ld0"] if s == 0 else sm["ld1"]
                a.wait_ge(ldm, 16 * (m // 2 + 1))
                if m >= 2:
                    a.wait_ge(sm["sd"], dv(m - 2, 4))
                a.activation(scrap[:], xt[:, s * D:(s + 1) * D], AF.Square,
                             accum_out=smc(SSQ, s)).then_inc(sm["sa"], 1)
                a.wait_ge(sm["sd"], dv(m, 4))
                a.activation(smc(LNV, s), smc(VAR, s), AF.Ln,
                             bias=c.epsv[:]).then_inc(sm["sa"], 1)
                a.wait_ge(sm["sa"], act(m, 2))
                a.activation(smc(RSTD, s), smc(LNV, s), AF.Exp,
                             scale=-0.5).then_inc(sm["sa"], 1)
                a.wait_ge(sm["sd"], dv(m, 5))
                a.activation(scrap[:], ln1[:, s * D:(s + 1) * D], AF.Square,
                             accum_out=smc(SSQB, s)).then_inc(sm["sa"], 1)
                a.wait_ge(sm["sd"], dv(m, 11))
                a.activation(smc(LNV2, s), smc(VAR2, s), AF.Ln,
                             bias=c.epsv[:]).then_inc(sm["sa"], 1)
                a.wait_ge(sm["sa"], act(m, 5))
                a.activation(smc(RSTD2, s), smc(LNV2, s), AF.Exp,
                             scale=-0.5).then_inc(sm["sa"], 1)

        @blk.vector
        def _(v):
            for m in range(NT_FULL):
                s = m % 2
                ldm = sm["ld0"] if s == 0 else sm["ld1"]
                rsm = sm["rs0"] if s == 0 else sm["rs1"]
                xts = xt[:, s * D:(s + 1) * D]
                ln1s = ln1[:, s * D:(s + 1) * D]
                v.wait_ge(ldm, 16 * (m // 2 + 1))
                v.reduce_sum(smc(S1, s), xts, axis=AX.X).then_inc(sm["sd"], 1)
                v.wait_ge(sm["sd"], dv(m, 1))
                v.tensor_single_scalar(smc(MU, s), smc(S1, s), INV_D,
                                       ALU.mult).then_inc(sm["sd"], 1)
                v.wait_ge(sm["sd"], dv(m, 2))
                v.tensor_mul(smc(MUSQ, s), smc(MU, s),
                             smc(MU, s)).then_inc(sm["sd"], 1)
                v.wait_ge(sm["sa"], act(m, 1))
                v.wait_ge(sm["sd"], dv(m, 3))
                if m >= 2:
                    v.wait_ge(sm["sa"], act(m - 2, 2))
                v.tensor_scalar(smc(VAR, s), smc(SSQ, s), INV_D, smc(MUSQ, s),
                                ALU.mult, ALU.subtract).then_inc(sm["sd"], 1)
                v.wait_ge(sm["sa"], act(m, 3))
                if m >= 2:
                    v.wait_ge(sm["sa"], act(m - 2, 4))
                v.tensor_scalar(ln1s, xts, smc(MU, s), smc(RSTD, s),
                                ALU.subtract, ALU.mult).then_inc(sm["sd"], 1)
                v.wait_ge(sm["sd"], dv(m, 5))
                if m >= 2:
                    v.wait_ge(rsm, 16 * (m // 2))
                v.tensor_add(rb[:, s * D:(s + 1) * D], xts,
                             ln1s).then_inc(sm["sd"], 1)
                v.wait_ge(sm["sd"], dv(m, 6))
                v.tensor_scalar_mul(rb[:, s * D:(s + 1) * D],
                                    rb[:, s * D:(s + 1) * D],
                                    c.cvec[:, s:s + 1]).then_inc(sm["sd"], 1)
                v.reduce_sum(smc(S1B, s), ln1s, axis=AX.X).then_inc(sm["sd"], 1)
                v.wait_ge(sm["sd"], dv(m, 8))
                v.tensor_single_scalar(smc(MU2, s), smc(S1B, s), INV_D,
                                       ALU.mult).then_inc(sm["sd"], 1)
                v.wait_ge(sm["sd"], dv(m, 9))
                v.tensor_mul(smc(MUSQ2, s), smc(MU2, s),
                             smc(MU2, s)).then_inc(sm["sd"], 1)
                v.wait_ge(sm["sa"], act(m, 4))
                v.wait_ge(sm["sd"], dv(m, 10))
                if m >= 2:
                    v.wait_ge(sm["sa"], act(m - 2, 5))
                v.tensor_scalar(smc(VAR2, s), smc(SSQB, s), INV_D,
                                smc(MUSQ2, s), ALU.mult,
                                ALU.subtract).then_inc(sm["sd"], 1)
                v.wait_ge(sm["sa"], act(m, 6))
                v.tensor_scalar(xnf[:, s * D:(s + 1) * D], ln1s, smc(MU2, s),
                                smc(RSTD2, s), ALU.subtract,
                                ALU.mult).then_inc(sm["sd"], 1)
                v.wait_ge(sm["sd"], dv(m, 12))
                v.tensor_mul(xnf[:, s * D:(s + 1) * D],
                             xnf[:, s * D:(s + 1) * D],
                             c.gamma_b[:]).then_inc(sm["sd"], 1)
                v.wait_ge(sm["sd"], dv(m, 13))
                if m >= 2:
                    v.wait_ge(sm["sp"], (m - 2) * 2 + 2)
                v.tensor_add(xnb[:, s * D:(s + 1) * D],
                             xnf[:, s * D:(s + 1) * D],
                             c.beta_b[:]).then_inc(sm["sd"], 1)
                sg = m % 2
                xsm = sm["xs0"] if sg == 0 else sm["xs1"]
                col0 = sg * 2048
                if m >= 2:
                    v.wait_ge(xsm, 16 * (m // 2))
                v.wait_ge(sm["sp"], m * 2 + 1)
                v.tensor_copy(stage[:, col0:col0 + 1024],
                              psT[:, (m % 2) * D:(m % 2) * D + 1024]
                              ).then_inc(sm["sd"], 1)
                v.wait_ge(sm["sp"], m * 2 + 2)
                v.tensor_copy(stage[:, col0 + 1024:col0 + 2048],
                              psT[:, (m % 2) * D + 1024:(m % 2) * D + 2048]
                              ).then_inc(sm["sd"], 1)

        @blk.tensor
        def _(te):
            for m in range(NT_FULL):
                s = m % 2
                te.wait_ge(sm["sd"], dv(m, 14))
                if m >= 2:
                    te.wait_ge(sm["sd"], dv(m - 2, 16))
                for j in range(16):
                    ins = te.transpose(
                        psT[:, (m % 2) * D + j * P:(m % 2) * D + (j + 1) * P],
                        xnb[:, s * D + j * P:s * D + (j + 1) * P],
                        c.ident_bf[:])
                    if j == 7 or j == 15:
                        ins.then_inc(sm["sp"], 1)

        @blk.gpsimd
        def _(g_):
            for m in range(NT_FULL):
                if m % 4 == 3:
                    emit_conv(c, g_, 1)
                s = m % 2
                rsm = sm["rs0"] if s == 0 else sm["rs1"]
                g_.wait_ge(sm["sd"], dv(m, 7))
                g_.wait_ge(rsm, 16 * (m // 2))
                if s == 1:
                    g_.wait_ge(sm["rs0"], 16 * (m // 2 + 1))
                row0 = (m // 2) * P
                if s == 0:
                    g_.dma_start(c.r_s[row0:row0 + P, :],
                                 rb[:, 0:D]).then_inc(rsm, 16)
                else:
                    g_.dma_start(c.r_s[row0:row0 + P, :], rb[:, D:2 * D],
                                 accum_op=ALU.add).then_inc(rsm, 16)
                sg = m % 2
                xsm = sm["xs0"] if sg == 0 else sm["xs1"]
                g_.wait_ge(sm["sd"], dv(m, 16))
                g_.wait_ge(xsm, 16 * (m // 2))
                g_.dma_start(
                    c.xn_s[:, :, m * P:(m + 1) * P].rearrange("ch p j -> p ch j"),
                    stage[:, sg * 2048:(sg + 1) * 2048].rearrange(
                        "p (ch j) -> p ch j", ch=16, j=P),
                ).then_inc(xsm, 16)
            g_.wait_ge(sm["rs0"], 16 * 16)
            g_.wait_ge(sm["rs1"], 16 * 16)
            g_.wait_ge(sm["xs0"], 16 * 16)
            g_.wait_ge(sm["xs1"], 16 * 16)


def emit_p1(c):
    nc = c.nc

    sm = mksems(c, ["aw00", "aw01", "aw10", "aw11", "aw20", "aw21",
                    "aw30", "aw31", "ax0", "ax1", "ax2", "amm", "aev",
                    "ast0", "ast1"])

    def wsem(cc):
        return sm[f"aw{cc % 4}{(cc // 4) % 2}"]

    NT = 8

    with (
        nc.sbuf_tensor("p1_wqk", [P, 32 * D], BF16) as wqk,
        nc.sbuf_tensor("p1_xns", [P, 3 * 8192], BF16) as xns,
        nc.sbuf_tensor("p1_kst", [P, 2 * 2048], BF16) as kst,
        nc.psum_tensor("p1_ps", [P, 8 * 512], F32) as ps,
        nc.Block() as blk,
    ):
        @blk.gpsimd
        def _(g):
            for cc in range(32):
                if cc % 6 == 5:
                    emit_conv(c, g, 1)
                if cc >= 8:
                    g.wait_ge(wsem(cc), 16 * (cc // 8))
                g.dma_start(
                    wqk[:, cc * D:(cc + 1) * D],
                    c.qkvh_in[cc].rearrange("p dk j -> p (dk j)"),
                ).then_inc(wsem(cc), 16)
            emit_conv(c, g, 2)

        @blk.sync
        def _(sy):
            for t in range(NT):
                xs = t % 3
                xsm = sm[f"ax{xs}"]
                if t >= 3:
                    sy.wait_ge(xsm, 16 * (t // 3))
                    sy.wait_ge(sm["amm"], 32 * (t - 2))
                sy.dma_start(
                    xns[:, xs * 8192:(xs + 1) * 8192]
                    .rearrange("p (ch j) -> p ch j", ch=NCD, j=512),
                    c.xn_s[:, :, t * 512:(t + 1) * 512]
                    .rearrange("ch p j -> p ch j"),
                ).then_inc(xsm, 16)

        @blk.tensor
        def _(te):
            for t in range(NT):
                xs = t % 3
                xsm = sm[f"ax{xs}"]
                te.wait_ge(xsm, 16 * (t // 3 + 1))
                for cc in range(32):
                    if t == 0:
                        te.wait_ge(wsem(cc), 16 * (cc // 8 + 1))
                    i = t * 32 + cc
                    b = i % 8
                    if i >= 8:
                        te.wait_ge(sm["aev"], i - 7)
                    for dk in range(NCD):
                        ins = te.matmul(
                            ps[:, b * 512:(b + 1) * 512],
                            wqk[:, cc * D + dk * P:cc * D + (dk + 1) * P],
                            xns[:, xs * 8192 + dk * 512:
                                xs * 8192 + (dk + 1) * 512],
                            start=(dk == 0), stop=(dk == NCD - 1))
                    ins.then_inc(sm["amm"], 1)

        @blk.vector
        def _(v):
            for i in range(NT * 32):
                grp = i // 4
                si = grp % 2
                stm = sm["ast0"] if si == 0 else sm["ast1"]
                v.wait_ge(sm["amm"], i + 1)
                if i % 4 == 0 and grp >= 2:
                    v.wait_ge(stm, 16 * (grp // 2))
                b = i % 8
                v.tensor_copy(kst[:, si * 2048 + (i % 4) * 512:
                                  si * 2048 + (i % 4 + 1) * 512],
                              ps[:, b * 512:(b + 1) * 512]).then_inc(sm["aev"], 1)

        @blk.scalar
        def _(a):
            for t in range(NT):
                for cc4 in range(8):
                    grp = t * 8 + cc4
                    si = grp % 2
                    stm = sm["ast0"] if si == 0 else sm["ast1"]
                    a.wait_ge(sm["aev"], grp * 4 + 4)
                    a.wait_ge(stm, 16 * (grp // 2))
                    cc0 = cc4 * 4
                    dst = (c.q_s if cc0 < NCD else c.k_s)[
                        (cc0 % NCD):(cc0 % NCD) + 4, :,
                        t * 512:(t + 1) * 512].rearrange("ch p j -> p ch j")
                    a.dma_start(dst,
                                kst[:, si * 2048:(si + 1) * 2048].rearrange(
                                    "p (ch j) -> p ch j", ch=4, j=512)
                                ).then_inc(stm, 16)
            a.wait_ge(sm["ast0"], 16 * 32)
            a.wait_ge(sm["ast1"], 16 * 32)

    sm2 = mksems(c, ["vw", "vx0", "vx1", "vmm", "vev", "vst0", "vst1"])
    with (
        nc.sbuf_tensor("p1_wv", [P, NCD * D], BF16) as wv,
        nc.sbuf_tensor("p1_xnsb", [P, 2 * 8192], BF16) as xns,
        nc.sbuf_tensor("p1_vst", [P, 2 * D], BF16) as vst,
        nc.psum_tensor("p1_pv", [P, 2 * D], F32) as pv,
        nc.Block() as blk,
    ):
        @blk.gpsimd
        def _(g):
            g.dma_start(
                wv[:].rearrange("p (dk j) -> p dk j", dk=NCD, j=D),
                c.qkv_in[:, 2 * D:3 * D]
                .rearrange("(dk p) j -> p dk j", dk=NCD, p=P),
            ).then_inc(sm2["vw"], 16)
            emit_conv(c, g, 2)

        @blk.sync
        def _(sy):
            for t in range(NT):
                xs = t % 2
                xsm = sm2["vx0"] if xs == 0 else sm2["vx1"]
                if t >= 2:
                    sy.wait_ge(xsm, 16 * (t // 2))
                    sy.wait_ge(sm2["vmm"], 4 * (t - 1))
                sy.dma_start(
                    xns[:, xs * 8192:(xs + 1) * 8192]
                    .rearrange("p (ch j) -> p ch j", ch=NCD, j=512),
                    c.xn_s[:, :, t * 512:(t + 1) * 512]
                    .rearrange("ch p j -> p ch j"),
                ).then_inc(xsm, 16)

        @blk.tensor
        def _(te):
            te.wait_ge(sm2["vw"], 16)
            for t in range(NT):
                xs = t % 2
                xsm = sm2["vx0"] if xs == 0 else sm2["vx1"]
                te.wait_ge(xsm, 16 * (t // 2 + 1))
                for tc in range(4):
                    i = t * 4 + tc
                    pslot = i % 2
                    if i >= 2:
                        te.wait_ge(sm2["vev"], i - 1)
                    for n in range(4):
                        for dk in range(NCD):
                            ins = te.matmul(
                                pv[:, pslot * D + n * 512:pslot * D + (n + 1) * 512],
                                xns[:, xs * 8192 + dk * 512 + tc * P:
                                    xs * 8192 + dk * 512 + (tc + 1) * P],
                                wv[:, dk * D + n * 512:dk * D + (n + 1) * 512],
                                start=(dk == 0), stop=(dk == NCD - 1))
                    ins.then_inc(sm2["vmm"], 1)

        @blk.vector
        def _(v):
            for i in range(NT * 4):
                pslot = i % 2
                v.wait_ge(sm2["vmm"], i + 1)
                stm = sm2["vst0"] if pslot == 0 else sm2["vst1"]
                if i >= 2:
                    v.wait_ge(stm, 16 * (i // 2))
                v.tensor_copy(vst[:, pslot * D:(pslot + 1) * D],
                              pv[:, pslot * D:(pslot + 1) * D]
                              ).then_inc(sm2["vev"], 1)

        @blk.scalar
        def _(a):
            for i in range(NT * 4):
                pslot = i % 2
                stm = sm2["vst0"] if pslot == 0 else sm2["vst1"]
                a.wait_ge(sm2["vev"], i + 1)
                a.wait_ge(stm, 16 * (i // 2))
                a.dma_start(c.v_s[i * P:(i + 1) * P, :],
                            vst[:, pslot * D:(pslot + 1) * D]
                            ).then_inc(stm, 16)
            a.wait_ge(sm2["vst0"], 16 * 16)
            a.wait_ge(sm2["vst1"], 16 * 16)


def emit_p2(c):
    nc = c.nc
    sm = mksems(c, ["qg0", "qg1", "kb0", "kb1", "kb2", "vb0", "vb1", "vb2",
                    "amm", "aev", "ssm", "sact", "spk", "pev",
                    "bmm", "bev", "ao0", "ao1"])

    NG = 16

    def nblk(g):
        return (g + 2) // 2

    cumA = [0] * (NG + 1)
    cumB = [0] * (NG + 1)
    for g in range(NG):
        cumA[g + 1] = cumA[g] + nblk(g) * 2
        cumB[g + 1] = cumB[g] + nblk(g) * 8
    klist = [(g, j) for g in range(NG) for j in range(nblk(g))]
    NKL = len(klist)

    pk_of_tile = {}
    aopk_of_tile = {}
    pk = 0
    for g in range(NG):
        for q in range(2):
            m = 2 * g + q
            npk = (m + 1 + 7) // 8
            pk_of_tile[m] = (pk, npk)
            pk += npk
        for q in range(2):
            m = 2 * g + q
            aopk_of_tile[m] = pk
            pk += 2

    def ammv(g, j, q):
        return cumA[g] + j * 2 + q + 1

    def bmmv(g, jb, q, dvN):
        return cumB[g] + jb * 8 + q * 4 + dvN + 1

    def ssm_max(m):
        return 8 * (m // 2) + 2 * (m % 2) + 1

    def ssm_bias(m):
        return 8 * (m // 2) + 2 * (m % 2) + 2

    def ssm_recip(m):
        return 8 * (m // 2) + 4 + (m % 2) + 1

    def ssm_aosc(m):
        return 8 * (m // 2) + 6 + (m % 2) + 1

    with (
        nc.sbuf_tensor("p2_qg", [P, 2 * NCD * 256], BF16) as qg,
        nc.sbuf_tensor("p2_kb", [P, 3 * 8192], BF16) as kb,
        nc.sbuf_tensor("p2_vb", [P, 3 * 8192], BF16) as vb,
        nc.sbuf_tensor("p2_strip", [P, 2 * SEQ], F32) as strip,
        nc.sbuf_tensor("p2_pb", [P, SEQ], BF16) as pb,
        nc.sbuf_tensor("p2_pt", [P, 2 * SEQ], BF16) as pt,
        nc.sbuf_tensor("p2_aoacc", [P, 2 * D], F32) as aoacc,
        nc.sbuf_tensor("p2_aotm", [P, 2 * D], BF16) as aotm,
        nc.sbuf_tensor("p2_aost", [P, 2 * D], BF16) as aost,
        nc.sbuf_tensor("p2_linv", [P, 4], F32) as linv,
        nc.sbuf_tensor("p2_smm", [P, 8], F32) as smm,
        nc.psum_tensor("p2_psA", [P, 6 * 512], F32) as psA,
        nc.psum_tensor("p2_psT", [P, 2 * 1024], BF16) as psT,
        nc.Block() as blk,
    ):
        @blk.sync
        def _(sy):
            kcnt = 0
            vcnt = 0
            for g in range(NG):
                qs = g % 2
                qsm = sm["qg0"] if qs == 0 else sm["qg1"]
                if g >= 2:
                    sy.wait_ge(qsm, 16 * (g // 2))
                    sy.wait_ge(sm["amm"], cumA[g - 1])
                sy.dma_start(
                    qg[:, qs * NCD * 256:(qs + 1) * NCD * 256]
                    .rearrange("p (ch j) -> p ch j", ch=NCD, j=256),
                    c.q_s[:, :, g * 256:(g + 1) * 256]
                    .rearrange("ch p j -> p ch j"),
                ).then_inc(qsm, 16)
                for j in range(nblk(g)):
                    ks = kcnt % 3
                    ksm = sm[f"kb{ks}"]
                    if kcnt >= 3:
                        sy.wait_ge(ksm, 16 * (kcnt // 3))
                        gp, jp = klist[kcnt - 3]
                        sy.wait_ge(sm["amm"], cumA[gp] + jp * 2 + 2)
                    sy.dma_start(
                        kb[:, ks * 8192:(ks + 1) * 8192]
                        .rearrange("p (ch j) -> p ch j", ch=NCD, j=512),
                        c.k_s[:, :, j * 512:(j + 1) * 512]
                        .rearrange("ch p j -> p ch j"),
                    ).then_inc(ksm, 16)
                    kcnt += 1
                for jb in range(nblk(g)):
                    vs = vcnt % 3
                    vsm = sm[f"vb{vs}"]
                    if vcnt >= 3:
                        sy.wait_ge(vsm, 16 * (vcnt // 3))
                        gp, jp = klist[vcnt - 3]
                        sy.wait_ge(sm["bmm"], cumB[gp] + jp * 8 + 8)
                    sy.dma_start(
                        vb[:, vs * 8192:(vs + 1) * 8192]
                        .rearrange("p (tc d) -> p tc d", tc=4, d=D),
                        c.v_s[jb * 512:(jb + 1) * 512, :]
                        .rearrange("(tc p) d -> p tc d", tc=4, p=P),
                    ).then_inc(vsm, 16)
                    vcnt += 1

        @blk.gpsimd
        def _(g_):
            for g in range(NG):
                if g in (3, 6, 9, 12):
                    emit_conv(c, g_, 1)
                for q in range(2):
                    m = 2 * g + q
                    s = m % 2
                    aosm = sm["ao0"] if s == 0 else sm["ao1"]
                    g_.wait_ge(sm["pev"], aopk_of_tile[m] + 2)
                    g_.wait_ge(aosm, 16 * (m // 2))
                    if s == 1:
                        g_.wait_ge(sm["ao0"], 16 * (m // 2 + 1))
                    k = m // 2
                    dst = c.ao_s[:, :, k * P:(k + 1) * P].rearrange(
                        "ch p j -> p ch j")
                    srcv = aost[:, s * D:(s + 1) * D].rearrange(
                        "p (ch j) -> p ch j", ch=NCD, j=P)
                    if s == 0:
                        g_.dma_start(dst, srcv).then_inc(aosm, 16)
                    else:
                        g_.dma_start(dst, srcv,
                                     accum_op=ALU.add).then_inc(aosm, 16)
            g_.wait_ge(sm["ao0"], 16 * 16)
            g_.wait_ge(sm["ao1"], 16 * 16)

        @blk.tensor
        def _(te):
            kcnt = 0
            vcnt = 0
            for g in range(NG):
                qs = g % 2
                qsm = sm["qg0"] if qs == 0 else sm["qg1"]
                te.wait_ge(qsm, 16 * (g // 2 + 1))
                for j in range(nblk(g)):
                    ks = kcnt % 3
                    ksm = sm[f"kb{ks}"]
                    te.wait_ge(ksm, 16 * (kcnt // 3 + 1))
                    kcnt += 1
                    for q in range(2):
                        i = cumA[g] + j * 2 + q
                        b = i % 6
                        if i >= 6:
                            te.wait_ge(sm["aev"], i - 5)
                        for dk in range(NCD):
                            ins = te.matmul(
                                psA[:, b * 512:(b + 1) * 512],
                                qg[:, qs * NCD * 256 + dk * 256 + q * P:
                                   qs * NCD * 256 + dk * 256 + (q + 1) * P],
                                kb[:, ks * 8192 + dk * 512:
                                   ks * 8192 + (dk + 1) * 512],
                                start=(dk == 0), stop=(dk == NCD - 1))
                        ins.then_inc(sm["amm"], 1)
                for q in range(2):
                    m = 2 * g + q
                    pk0, npk = pk_of_tile[m]
                    te.wait_ge(sm["sact"], m + 1)
                    ntk = m + 1
                    for pkk in range(npk):
                        slot = (pk0 + pkk) % 2
                        if pk0 + pkk >= 2:
                            te.wait_ge(sm["pev"], pk0 + pkk - 1)
                        lo = pkk * 8
                        hi = min(lo + 8, ntk)
                        for tk in range(lo, hi):
                            ins = te.transpose(
                                psT[:, slot * 1024 + (tk - lo) * P:
                                    slot * 1024 + (tk - lo + 1) * P],
                                pb[:, tk * P:(tk + 1) * P],
                                c.ident_bf[:])
                        ins.then_inc(sm["spk"], 1)
                for jb in range(nblk(g)):
                    vs = vcnt % 3
                    vsm = sm[f"vb{vs}"]
                    te.wait_ge(vsm, 16 * (vcnt // 3 + 1))
                    vcnt += 1
                    for q in range(2):
                        m = 2 * g + q
                        pk0, npk = pk_of_tile[m]
                        te.wait_ge(sm["pev"], pk0 + npk)
                        for dvN in range(4):
                            i = cumB[g] + jb * 8 + q * 4 + dvN
                            b = i % 6
                            if i >= 6:
                                te.wait_ge(sm["bev"], i - 5)
                            tks = [tk for tk in range(jb * 4, jb * 4 + 4)
                                   if tk <= m]
                            for nn, tk in enumerate(tks):
                                ins = te.matmul(
                                    psA[:, b * 512:(b + 1) * 512],
                                    pt[:, q * SEQ + tk * P:
                                       q * SEQ + (tk + 1) * P],
                                    vb[:, vs * 8192 + (tk - jb * 4) * D +
                                       dvN * 512:
                                       vs * 8192 + (tk - jb * 4) * D +
                                       (dvN + 1) * 512],
                                    start=(nn == 0), stop=(nn == len(tks) - 1))
                            ins.then_inc(sm["bmm"], 1)
                for q in range(2):
                    m = 2 * g + q
                    te.wait_ge(sm["ssm"], ssm_aosc(m))
                    for half in range(2):
                        pkk = aopk_of_tile[m] + half
                        slot = pkk % 2
                        if pkk >= 2:
                            te.wait_ge(sm["pev"], pkk - 1)
                        for jj in range(8):
                            ch = half * 8 + jj
                            ins = te.transpose(
                                psT[:, slot * 1024 + jj * P:
                                    slot * 1024 + (jj + 1) * P],
                                aotm[:, (m % 2) * D + ch * P:
                                     (m % 2) * D + (ch + 1) * P],
                                c.ident_bf[:])
                        ins.then_inc(sm["spk"], 1)

        @blk.vector
        def _(v):
            for g in range(NG):
                for j in range(nblk(g)):
                    for q in range(2):
                        i = cumA[g] + j * 2 + q
                        b = i % 6
                        v.wait_ge(sm["amm"], i + 1)
                        m = 2 * g + q
                        if m >= 2:
                            v.wait_ge(sm["sact"], m - 2 + 1)
                        dst = strip[:, q * SEQ + j * 512:q * SEQ + (j + 1) * 512]
                        if j == nblk(g) - 1:
                            s0 = 512 - 128 * (m % 4)
                            v.tensor_add(dst, psA[:, b * 512:(b + 1) * 512],
                                         c.mask_m[:, s0:s0 + 512]
                                         ).then_inc(sm["aev"], 1)
                        else:
                            v.tensor_copy(dst, psA[:, b * 512:(b + 1) * 512]
                                          ).then_inc(sm["aev"], 1)
                for q in range(2):
                    m = 2 * g + q
                    L = 512 * nblk(g)
                    v.wait_ge(sm["aev"], cumA[g] + (nblk(g) - 1) * 2 + q + 1)
                    v.reduce_max(smm[:, q:q + 1], strip[:, q * SEQ:q * SEQ + L],
                                 axis=AX.X).then_inc(sm["ssm"], 1)
                    v.wait_ge(sm["ssm"], ssm_max(m))
                    if m >= 2:
                        v.wait_ge(sm["sact"], m - 2 + 1)
                    v.tensor_single_scalar(smm[:, 4 + q:5 + q], smm[:, q:q + 1],
                                           -SM_SCALE, ALU.mult
                                           ).then_inc(sm["ssm"], 1)
                for q in range(2):
                    m = 2 * g + q
                    pk0, npk = pk_of_tile[m]
                    ntk = m + 1
                    for pkk in range(npk):
                        gi = pk0 + pkk
                        slot = gi % 2
                        v.wait_ge(sm["spk"], gi + 1)
                        lo = pkk * 8
                        hi = min(lo + 8, ntk)
                        if m >= 2:
                            v.wait_ge(sm["bmm"],
                                      bmmv(g - 1, nblk(g - 1) - 1, q, 3))
                        v.tensor_copy(pt[:, q * SEQ + lo * P:q * SEQ + hi * P],
                                      psT[:, slot * 1024:
                                          slot * 1024 + (hi - lo) * P]
                                      ).then_inc(sm["pev"], 1)
                    v.wait_ge(sm["sact"], m + 1)
                    v.reciprocal(linv[:, (g % 2) * 2 + q:(g % 2) * 2 + q + 1],
                                 smm[:, 6 + q:7 + q]
                                 ).then_inc(sm["ssm"], 1)
                for jb in range(nblk(g)):
                    for q in range(2):
                        for dvN in range(4):
                            i = cumB[g] + jb * 8 + q * 4 + dvN
                            b = i % 6
                            v.wait_ge(sm["bmm"], i + 1)
                            dst = aoacc[:, q * D + dvN * 512:
                                        q * D + (dvN + 1) * 512]
                            if jb == 0:
                                v.tensor_copy(dst, psA[:, b * 512:(b + 1) * 512]
                                              ).then_inc(sm["bev"], 1)
                            else:
                                v.tensor_add(dst, dst,
                                             psA[:, b * 512:(b + 1) * 512]
                                             ).then_inc(sm["bev"], 1)
                for q in range(2):
                    m = 2 * g + q
                    v.wait_ge(sm["bev"], bmmv(g, nblk(g) - 1, q, 3))
                    if m >= 2:
                        v.wait_ge(sm["spk"], aopk_of_tile[m - 2] + 2)
                    v.tensor_scalar(aotm[:, (m % 2) * D:(m % 2 + 1) * D],
                                    aoacc[:, q * D:(q + 1) * D],
                                    linv[:, (g % 2) * 2 + q:(g % 2) * 2 + q + 1],
                                    c.cvec[:, m % 2:m % 2 + 1],
                                    ALU.mult, ALU.mult).then_inc(sm["ssm"], 1)
                for q in range(2):
                    m = 2 * g + q
                    s = m % 2
                    aosm = sm["ao0"] if s == 0 else sm["ao1"]
                    for half in range(2):
                        pkk = aopk_of_tile[m] + half
                        slot = pkk % 2
                        v.wait_ge(sm["spk"], pkk + 1)
                        if half == 0:
                            v.wait_ge(aosm, 16 * (m // 2))
                        v.tensor_copy(aost[:, s * D + half * 1024:
                                           s * D + (half + 1) * 1024],
                                      psT[:, slot * 1024:(slot + 1) * 1024]
                                      ).then_inc(sm["pev"], 1)

        @blk.scalar
        def _(a):
            for g in range(NG):
                for q in range(2):
                    m = 2 * g + q
                    L = 512 * nblk(g)
                    a.wait_ge(sm["ssm"], ssm_bias(m))
                    if m >= 2:
                        a.wait_ge(sm["ssm"], ssm_recip(m - 2))
                    if m >= 1:
                        pk0p, npkp = pk_of_tile[m - 1]
                        a.wait_ge(sm["spk"], pk0p + npkp)
                    a.activation(pb[:, 0:L],
                                 strip[:, q * SEQ:q * SEQ + L], AF.Exp,
                                 scale=SM_SCALE, bias=smm[:, 4 + q:5 + q],
                                 accum_out=smm[:, 6 + q:7 + q]
                                 ).then_inc(sm["sact"], 1)


def emit_p3(c):
    nc = c.nc
    sm = mksems(c, ["wo", "rt", "ao0", "ao1", "rr0", "rr1", "omm", "tp",
                    "rmm", "sa", "sd", "x10", "x11", "x20", "x21"])

    NDV3 = 35
    NACT3 = 4

    def sd(t, j):
        return t * NDV3 + j

    def sa_(t, j):
        return t * NACT3 + j

    (S1, MU, MUSQ, VAR, LNV, RSTD, SSQ, MX, NEGC, SE, RSE) = range(11)

    with (
        nc.sbuf_tensor("p3_wo", [P, NCD * D], BF16) as wo,
        nc.sbuf_tensor("p3_rsb", [P, NCD * 17], F32) as rsb,
        nc.sbuf_tensor("p3_aob", [P, 2 * D], BF16) as aob,
        nc.sbuf_tensor("p3_rb", [P, 2 * D], F32) as rb,
        nc.sbuf_tensor("p3_x1", [P, 2 * D], F32) as x1b,
        nc.sbuf_tensor("p3_xn2", [P, 2 * D], F32) as xn2,
        nc.sbuf_tensor("p3_xfm", [P, 2 * D], F32) as xfm,
        nc.sbuf_tensor("p3_x2st", [P, 2 * D], BF16) as x2st,
        nc.sbuf_tensor("p3_scrap", [P, D], F32) as scrap,
        nc.sbuf_tensor("p3_sm", [P, 32], F32) as smv,
        nc.sbuf_tensor("p3_lgt", [P, 2 * 32], F32) as lgt,
        nc.sbuf_tensor("p3_prob", [P, 2 * 16], F32) as prob,
        nc.sbuf_tensor("p3_pn", [P, 2 * 16], F32) as pnb,
        nc.sbuf_tensor("p3_w4", [P, 2 * 16], F32) as w4,
        nc.sbuf_tensor("p3_msk", [P, 2 * 16], F32) as mskb,
        nc.psum_tensor("p3_pso", [P, D], F32) as pso,
        nc.psum_tensor("p3_psT", [P, 2 * 512], F32) as psT3,
        nc.psum_tensor("p3_psr", [P, 17], F32) as psr,
        nc.Block() as blk,
    ):
        def smc(idx, s):
            return smv[:, 2 * idx + s:2 * idx + s + 1]

        @blk.gpsimd
        def _(g_):
            g_.dma_start(
                wo[:].rearrange("p (dk j) -> p dk j", dk=NCD, j=D),
                c.wo_in.rearrange("(dk p) j -> p dk j", dk=NCD, p=P),
            ).then_inc(sm["wo"], 16)
            for t in range(NT_OWN):
                s = t % 2
                x1m = sm["x10"] if s == 0 else sm["x11"]
                x2m = sm["x20"] if s == 0 else sm["x21"]
                g_.wait_ge(sm["sd"], sd(t, 1))
                g_.wait_ge(x1m, 16 * (t // 2))
                g_.dma_start(c.out[t * P:(t + 1) * P, :],
                             x1b[:, s * D:(s + 1) * D]).then_inc(x1m, 16)
                g_.wait_ge(sm["sd"], sd(t, 14))
                g_.wait_ge(x2m, 16 * (t // 2))
                g_.dma_start(
                    c.x2_s[:, :, t * P:(t + 1) * P].rearrange("ch p j -> p ch j"),
                    x2st[:, s * D:(s + 1) * D].rearrange(
                        "p (ch j) -> p ch j", ch=NCD, j=P),
                ).then_inc(x2m, 16)
            for n in ("x10", "x11", "x20", "x21"):
                g_.wait_ge(sm[n], 16 * 8)

        @blk.sync
        def _(sy):
            sy.dma_start(rsb[:], c.router_in[:]).then_inc(sm["rt"], 16)
            for t in range(NT_OWN):
                s = t % 2
                aom = sm["ao0"] if s == 0 else sm["ao1"]
                rrm = sm["rr0"] if s == 0 else sm["rr1"]
                if t >= 2:
                    sy.wait_ge(aom, 16 * (t // 2))
                    sy.wait_ge(sm["omm"], t - 1)
                sy.dma_start(
                    aob[:, s * D:(s + 1) * D].rearrange(
                        "p (ch j) -> p ch j", ch=NCD, j=P),
                    c.ao_s[:, :, t * P:(t + 1) * P].rearrange("ch p j -> p ch j"),
                ).then_inc(aom, 16)
                if t >= 2:
                    sy.wait_ge(rrm, 16 * (t // 2))
                    sy.wait_ge(sm["sd"], sd(t - 2, 1))
                sy.dma_start(rb[:, s * D:(s + 1) * D],
                             c.r_s[t * P:(t + 1) * P, :]).then_inc(rrm, 16)

        @blk.tensor
        def _(te):
            te.wait_ge(sm["wo"], 16)
            te.wait_ge(sm["rt"], 16)
            for t in range(NT_OWN):
                s = t % 2
                aom = sm["ao0"] if s == 0 else sm["ao1"]
                te.wait_ge(aom, 16 * (t // 2 + 1))
                if t >= 1:
                    te.wait_ge(sm["sd"], sd(t - 1, 1))
                for n in range(4):
                    for dk in range(NCD):
                        ins = te.matmul(
                            pso[:, n * 512:(n + 1) * 512],
                            aob[:, s * D + dk * P:s * D + (dk + 1) * P],
                            wo[:, dk * D + n * 512:dk * D + (n + 1) * 512],
                            start=(dk == 0), stop=(dk == NCD - 1))
                ins.then_inc(sm["omm"], 1)
                te.wait_ge(sm["sd"], sd(t, 6))
                for pkk in range(4):
                    gi = t * 4 + pkk
                    slot = gi % 2
                    if gi >= 2:
                        tp_, pp_ = divmod(gi - 2, 4)
                        te.wait_ge(sm["sd"], sd(tp_, 8 + 2 * pp_))
                    for jj in range(4):
                        ch = pkk * 4 + jj
                        ins = te.transpose(
                            psT3[:, slot * 512 + jj * P:slot * 512 + (jj + 1) * P],
                            xn2[:, s * D + ch * P:s * D + (ch + 1) * P],
                            c.ident_f32[:])
                    ins.then_inc(sm["tp"], 1)
                te.wait_ge(sm["sd"], sd(t, 14))
                if t >= 2:
                    te.wait_ge(sm["sd"], sd(t - 2, 15))
                for dk in range(NCD):
                    ins = te.matmul(psr[:], xfm[:, s * D + dk * P:
                                               s * D + (dk + 1) * P],
                                    rsb[:, dk * 17:(dk + 1) * 17],
                                    start=(dk == 0), stop=(dk == NCD - 1))
                ins.then_inc(sm["rmm"], 1)

        @blk.scalar
        def _(a):
            for t in range(NT_OWN):
                s = t % 2
                a.wait_ge(sm["sd"], sd(t, 1))
                if t >= 2:
                    a.wait_ge(sm["sd"], sd(t - 2, 5))
                a.activation(scrap[:], x1b[:, s * D:(s + 1) * D], AF.Square,
                             accum_out=smc(SSQ, s)).then_inc(sm["sa"], 1)
                a.wait_ge(sm["sd"], sd(t, 5))
                a.activation(smc(LNV, s), smc(VAR, s), AF.Ln,
                             bias=c.epsv[:]).then_inc(sm["sa"], 1)
                a.wait_ge(sm["sa"], sa_(t, 2))
                a.activation(smc(RSTD, s), smc(LNV, s), AF.Exp,
                             scale=-0.5).then_inc(sm["sa"], 1)
                a.wait_ge(sm["sd"], sd(t, 17))
                if t >= 2:
                    a.wait_ge(sm["sd"], sd(t - 2, 18))
                a.activation(prob[:, s * 16:(s + 1) * 16],
                             lgt[:, s * 32:s * 32 + 16], AF.Exp,
                             bias=smc(NEGC, s),
                             accum_out=smc(SE, s)).then_inc(sm["sa"], 1)
            a.wait_ge(sm["sd"], NT_OWN * NDV3)
            a.activation(c.sgv[:], c.sg_all[:], AF.Sigmoid).then_inc(sm["sa"], 1)

        @blk.vector
        def _(v):
            for t in range(NT_OWN):
                s = t % 2
                rrm = sm["rr0"] if s == 0 else sm["rr1"]
                x1m = sm["x10"] if s == 0 else sm["x11"]
                x2m = sm["x20"] if s == 0 else sm["x21"]
                v.wait_ge(sm["omm"], t + 1)
                v.wait_ge(rrm, 16 * (t // 2 + 1))
                if t >= 2:
                    v.wait_ge(sm["sa"], sa_(t - 2, 1))
                    v.wait_ge(x1m, 16 * (t // 2))
                v.tensor_add(x1b[:, s * D:(s + 1) * D], pso[:],
                             rb[:, s * D:(s + 1) * D]).then_inc(sm["sd"], 1)
                v.wait_ge(sm["sd"], sd(t, 1))
                v.reduce_sum(smc(S1, s), x1b[:, s * D:(s + 1) * D],
                             axis=AX.X).then_inc(sm["sd"], 1)
                v.wait_ge(sm["sd"], sd(t, 2))
                v.tensor_single_scalar(smc(MU, s), smc(S1, s), INV_D,
                                       ALU.mult).then_inc(sm["sd"], 1)
                v.wait_ge(sm["sd"], sd(t, 3))
                v.tensor_mul(smc(MUSQ, s), smc(MU, s),
                             smc(MU, s)).then_inc(sm["sd"], 1)
                v.wait_ge(sm["sa"], sa_(t, 1))
                v.wait_ge(sm["sd"], sd(t, 4))
                if t >= 2:
                    v.wait_ge(sm["sa"], sa_(t - 2, 2))
                v.tensor_scalar(smc(VAR, s), smc(SSQ, s), INV_D, smc(MUSQ, s),
                                ALU.mult, ALU.subtract).then_inc(sm["sd"], 1)
                v.wait_ge(sm["sa"], sa_(t, 3))
                if t >= 1:
                    v.wait_ge(sm["tp"], t * 4)
                v.tensor_scalar(xn2[:, s * D:(s + 1) * D],
                                x1b[:, s * D:(s + 1) * D], smc(MU, s),
                                smc(RSTD, s), ALU.subtract,
                                ALU.mult).then_inc(sm["sd"], 1)
                for pkk in range(4):
                    gi = t * 4 + pkk
                    slot = gi % 2
                    v.wait_ge(sm["tp"], gi + 1)
                    if pkk == 0:
                        if t >= 1:
                            v.wait_ge(sm["rmm"], t)
                        v.wait_ge(x2m, 16 * (t // 2))
                    v.tensor_copy(xfm[:, s * D + pkk * 512:
                                      s * D + (pkk + 1) * 512],
                                  psT3[:, slot * 512:(slot + 1) * 512]
                                  ).then_inc(sm["sd"], 1)
                    v.tensor_copy(x2st[:, s * D + pkk * 512:
                                       s * D + (pkk + 1) * 512],
                                  psT3[:, slot * 512:(slot + 1) * 512]
                                  ).then_inc(sm["sd"], 1)
                v.wait_ge(sm["rmm"], t + 1)
                v.tensor_copy(lgt[:, s * 32:s * 32 + 17],
                              psr[:]).then_inc(sm["sd"], 1)
                v.wait_ge(sm["sd"], sd(t, 15))
                v.reduce_max(smc(MX, s), lgt[:, s * 32:s * 32 + 16],
                             axis=AX.X).then_inc(sm["sd"], 1)
                v.wait_ge(sm["sd"], sd(t, 16))
                if t >= 2:
                    v.wait_ge(sm["sa"], sa_(t - 2, 4))
                v.tensor_single_scalar(smc(NEGC, s), smc(MX, s), -1.0,
                                       ALU.mult).then_inc(sm["sd"], 1)
                v.wait_ge(sm["sa"], sa_(t, 4))
                v.reciprocal(smc(RSE, s), smc(SE, s)).then_inc(sm["sd"], 1)
                v.wait_ge(sm["sd"], sd(t, 18))
                v.tensor_scalar_mul(pnb[:, s * 16:(s + 1) * 16],
                                    prob[:, s * 16:(s + 1) * 16],
                                    smc(RSE, s)).then_inc(sm["sd"], 1)
                v.wait_ge(sm["sd"], sd(t, 19))
                v.tensor_copy(w4[:, s * 16:(s + 1) * 16],
                              pnb[:, s * 16:(s + 1) * 16]).then_inc(sm["sd"], 1)
                for r4 in range(4):
                    v.wait_ge(sm["sd"], sd(t, 20 + r4 * 3))
                    v.reduce_max(smc(MX, s), w4[:, s * 16:(s + 1) * 16],
                                 axis=AX.X).then_inc(sm["sd"], 1)
                    v.wait_ge(sm["sd"], sd(t, 21 + r4 * 3))
                    v.tensor_single_scalar(mskb[:, s * 16:(s + 1) * 16],
                                           w4[:, s * 16:(s + 1) * 16],
                                           smc(MX, s),
                                           ALU.is_ge).then_inc(sm["sd"], 1)
                    v.wait_ge(sm["sd"], sd(t, 22 + r4 * 3))
                    v.scalar_tensor_tensor(w4[:, s * 16:(s + 1) * 16],
                                           mskb[:, s * 16:(s + 1) * 16], -10.0,
                                           w4[:, s * 16:(s + 1) * 16],
                                           ALU.mult, ALU.add
                                           ).then_inc(sm["sd"], 1)
                v.wait_ge(sm["sd"], sd(t, 32))
                v.tensor_single_scalar(mskb[:, s * 16:(s + 1) * 16],
                                       pnb[:, s * 16:(s + 1) * 16], smc(MX, s),
                                       ALU.is_ge).then_inc(sm["sd"], 1)
                v.wait_ge(sm["sd"], sd(t, 33))
                v.tensor_mul(c.coef_all[:, t * 16:(t + 1) * 16],
                             pnb[:, s * 16:(s + 1) * 16],
                             mskb[:, s * 16:(s + 1) * 16]).then_inc(sm["sd"], 1)
                v.tensor_copy(c.sg_all[:, t:t + 1],
                              lgt[:, s * 32 + 16:s * 32 + 17]
                              ).then_inc(sm["sd"], 1)


def emit_p5(c):
    nc = c.nc
    sm = mksems(c, ["x2l", "wb0", "wb1", "wb2", "wb3", "wh0", "wh1",
                    "gmm", "gev", "sact5", "dmm", "dev",
                    "ost0", "ost1", "ost2", "ost3", "ost4", "ost5"])
    NOS = 6

    NE = 20
    NIC = NI_EXP
    HWIDTH = NIC * D

    def wsrc(it, ic):
        if it < 4:
            g_ap = c.sgupc[it * NI_EXP + ic].rearrange("p dk j -> p (dk j)")
            u_ap = c.sgupc[44 + it * NI_EXP + ic].rearrange(
                "p dk j -> p (dk j)")
        else:
            e = it - 4
            g_ap = c.w13c[e, ic].rearrange("p dk j -> p (dk j)")
            u_ap = c.w13c[e, NI_EXP + ic].rearrange("p dk j -> p (dk j)")
        return g_ap, u_ap

    def w2src(it, half, ic0, ic1):
        if it < 4:
            return c.sdh_in[it, :, ic0:ic1, half * 1024:(half + 1) * 1024]
        return c.w2h_in[it - 4, :, ic0:ic1, half * 1024:(half + 1) * 1024]

    def coefcol(it, tc):
        if it < 4:
            return c.sgv[:, tc:tc + 1]
        return c.coef_all[:, tc * 16 + (it - 4):tc * 16 + (it - 4) + 1]

    wbs = ["wb0", "wb1", "wb2", "wb3"]

    with (
        nc.sbuf_tensor("p5_x2f", [P, NCD * D], BF16) as x2f,
        nc.sbuf_tensor("p5_h", [P, HWIDTH], BF16) as hact,
        nc.sbuf_tensor("p5_w13", [P, 4 * D], BF16) as w13b,
        nc.sbuf_tensor("p5_w2", [P, 2 * NIC * 1024], BF16) as w2b,
        nc.sbuf_tensor("p5_sgb", [P, 3 * 512], BF16) as sgb,
        nc.sbuf_tensor("p5_stage", [P, NOS * 512], F32) as stage,
        nc.psum_tensor("p5_pg", [P, 3 * 512], F32) as pg,
        nc.psum_tensor("p5_pu", [P, 3 * 512], F32) as pu,
        nc.psum_tensor("p5_pd", [P, 2 * 512], F32) as pd,
        nc.Block() as blk,
    ):
        @blk.sync
        def _(sy):
            sy.dma_start(
                x2f[:].rearrange("p (ch j) -> p ch j", ch=NCD, j=D),
                c.x2_s.rearrange("ch p j -> p ch j"),
            ).then_inc(sm["x2l"], 16)
            sy.wait_ge(c.convsem, 16 * c.conv_count)
            for it in range(NE):
                for ic in range(NIC):
                    for gu in range(2):
                        li = it * 22 + ic * 2 + gu
                        bi = li % 4
                        bsm = sm[wbs[bi]]
                        if li >= 4:
                            sy.wait_ge(bsm, 16 * (li // 4))
                            sy.wait_ge(sm["gmm"], ((li - 4) // 2 + 1) * 4)
                        g_ap, u_ap = wsrc(it, ic)
                        sy.dma_start(w13b[:, bi * D:(bi + 1) * D],
                                     g_ap if gu == 0 else u_ap
                                     ).then_inc(bsm, 16)

        @blk.gpsimd
        def _(g_):
            def emit_pieces(itp, lo, hi):
                for pi in range(itp * 64 + lo, itp * 64 + hi):
                    hs = (pi - itp * 64) // 16
                    tc = pi % 16
                    osl = pi % NOS
                    osm = sm[f"ost{osl}"]
                    g_.wait_ge(sm["dev"], pi + 1)
                    g_.wait_ge(osm, 16 * (pi // NOS))
                    g_.dma_start(
                        c.out[tc * P:(tc + 1) * P, hs * 512:(hs + 1) * 512],
                        stage[:, osl * 512:(osl + 1) * 512],
                        accum_op=ALU.add,
                    ).then_inc(osm, 16)

            for it in range(NE + 1):
                if it < NE:
                    if it >= 1:
                        emit_pieces(it - 1, 0, 32)
                    for h in range(2):
                        hidx = it * 2 + h
                        hsm = sm["wh0"] if hidx % 2 == 0 else sm["wh1"]
                        if h == 1 and it >= 1:
                            emit_pieces(it - 1, 56, 64)
                        if hidx >= 2:
                            g_.wait_ge(hsm, 64 * (hidx // 2))
                            g_.wait_ge(sm["dmm"],
                                       (it - 1) * 64 + (h + 1) * 32)
                        base = (hidx % 2) * NIC * 1024
                        for qq, (ic0, ic1) in enumerate(
                                [(0, 3), (3, 6), (6, 9), (9, 11)]):
                            g_.dma_start(
                                w2b[:, base + ic0 * 1024:base + ic1 * 1024]
                                .rearrange("p (ic j) -> p ic j",
                                           ic=ic1 - ic0, j=1024),
                                w2src(it, h, ic0, ic1),
                            ).then_inc(hsm, 16)
                            if h == 0 and it >= 1 and qq < 3:
                                emit_pieces(it - 1, 32 + qq * 8,
                                            40 + qq * 8)
                else:
                    emit_pieces(it - 1, 0, 64)
            for osl in range(NOS):
                g_.wait_ge(sm[f"ost{osl}"],
                           16 * len(range(osl, NE * 64, NOS)))

        @blk.tensor
        def _(te):
            te.wait_ge(sm["x2l"], 16)
            for it in range(NE):
                for ic in range(NIC):
                    for gu in range(2):
                        li = it * 22 + ic * 2 + gu
                        bi = li % 4
                        te.wait_ge(sm[wbs[bi]], 16 * (li // 4 + 1))
                    for ts in range(4):
                        pi = (it * NIC + ic) * 4 + ts
                        sl = pi % 3
                        if pi >= 2 and pi % 2 == 0:
                            te.wait_ge(sm["gev"], pi - 1)
                            te.wait_ge(sm["sact5"], pi - 1)
                        for dk in range(NCD):
                            ins = te.matmul(
                                pg[:, sl * 512:(sl + 1) * 512],
                                w13b[:, ((it * 22 + ic * 2) % 4) * D + dk * P:
                                     ((it * 22 + ic * 2) % 4) * D + (dk + 1) * P],
                                x2f[:, dk * D + ts * 512:dk * D + (ts + 1) * 512],
                                start=(dk == 0), stop=(dk == NCD - 1))
                        for dk in range(NCD):
                            ins = te.matmul(
                                pu[:, sl * 512:(sl + 1) * 512],
                                w13b[:, ((it * 22 + ic * 2 + 1) % 4) * D + dk * P:
                                     ((it * 22 + ic * 2 + 1) % 4) * D + (dk + 1) * P],
                                x2f[:, dk * D + ts * 512:dk * D + (ts + 1) * 512],
                                start=(dk == 0), stop=(dk == NCD - 1))
                        ins.then_inc(sm["gmm"], 1)
                for h in range(2):
                    hidx = it * 2 + h
                    hsm = sm["wh0"] if hidx % 2 == 0 else sm["wh1"]
                    te.wait_ge(hsm, 64 * (hidx // 2 + 1))
                    for hs2 in range(2):
                        hs = h * 2 + hs2
                        for tc in range(16):
                            pi = it * 64 + hs * 16 + tc
                            bank = pi % 2
                            if pi >= 2:
                                te.wait_ge(sm["dev"], pi - 1)
                            if hs == 0 and tc == 0:
                                te.wait_ge(sm["gev"], (it + 1) * 44)
                            for ic in range(NIC):
                                ins = te.matmul(
                                    pd[:, bank * 512:(bank + 1) * 512],
                                    hact[:, ic * D + tc * P:ic * D + (tc + 1) * P],
                                    w2b[:, (hidx % 2) * NIC * 1024 + ic * 1024 +
                                        hs2 * 512:
                                        (hidx % 2) * NIC * 1024 + ic * 1024 +
                                        (hs2 + 1) * 512],
                                    start=(ic == 0), stop=(ic == NIC - 1))
                            ins.then_inc(sm["dmm"], 1)

        @blk.scalar
        def _(a):
            for it in range(NE):
                for ic in range(NIC):
                    for ts in range(4):
                        pi = (it * NIC + ic) * 4 + ts
                        sl = pi % 3
                        a.wait_ge(sm["gmm"], pi + 1)
                        if pi >= 3:
                            a.wait_ge(sm["gev"], pi - 2)
                        a.activation(sgb[:, sl * 512:(sl + 1) * 512],
                                     pg[:, sl * 512:(sl + 1) * 512],
                                     AF.Silu).then_inc(sm["sact5"], 1)

        @blk.vector
        def _(v):
            for it in range(NE):
                for ic in range(NIC):
                    for ts in range(4):
                        pi = (it * NIC + ic) * 4 + ts
                        sl = pi % 3
                        v.wait_ge(sm["sact5"], pi + 1)
                        if ic == 0 and ts == 0 and it >= 1:
                            v.wait_ge(sm["dmm"], it * 64)
                        v.tensor_mul(hact[:, ic * D + ts * 512:
                                          ic * D + (ts + 1) * 512],
                                     sgb[:, sl * 512:(sl + 1) * 512],
                                     pu[:, sl * 512:(sl + 1) * 512]
                                     ).then_inc(sm["gev"], 1)
                for h in range(2):
                    for hs2 in range(2):
                        hs = h * 2 + hs2
                        for tc in range(16):
                            pi = it * 64 + hs * 16 + tc
                            bank = pi % 2
                            osl = pi % NOS
                            osm = sm[f"ost{osl}"]
                            v.wait_ge(sm["dmm"], pi + 1)
                            v.wait_ge(osm, 16 * (pi // NOS))
                            v.tensor_single_scalar(
                                stage[:, osl * 512:(osl + 1) * 512],
                                pd[:, bank * 512:(bank + 1) * 512],
                                coefcol(it, tc), ALU.mult
                            ).then_inc(sm["dev"], 1)



def prep_inputs(L0_attn_x, L0_attn_qkv, L0_attn_o_proj, L0_attn_gamma,
                L0_attn_beta, L0_ffn_gate_w, L0_ffn_sgate_w, L0_ffn_w13,
                L0_ffn_w2, L0_ffn_shared_gate_up_w, L0_ffn_shared_down_w):
    A = np.ascontiguousarray
    x = np.asarray(L0_attn_x)
    qkv = A(np.asarray(L0_attn_qkv))
    wo = A(np.asarray(L0_attn_o_proj))
    gamma = A(np.asarray(L0_attn_gamma).reshape(1, D))
    beta = A(np.asarray(L0_attn_beta).reshape(1, D))
    gate_t = np.asarray(L0_ffn_gate_w).T.reshape(NCD, P, E)
    sgate_t = np.asarray(L0_ffn_sgate_w).T.reshape(NCD, P, 1)
    router = A(np.concatenate([gate_t, sgate_t], axis=2)
               .transpose(1, 0, 2).reshape(P, NCD * (E + 1)))
    w13t = np.asarray(L0_ffn_w13).transpose(0, 2, 1)
    w13h = A(w13t.reshape(E, NCD, P, 22, P).transpose(0, 3, 2, 1, 4))
    w2t = np.asarray(L0_ffn_w2).transpose(0, 2, 1)
    w2h = A(w2t.reshape(E, NI_EXP, P, D).transpose(0, 2, 1, 3))
    sgupt = np.asarray(L0_ffn_shared_gate_up_w).T
    sguph = A(sgupt.reshape(NCD, P, 88, P).transpose(2, 1, 0, 3))
    sdownt = np.asarray(L0_ffn_shared_down_w).T
    sdh = A(sdownt.reshape(4, NI_EXP, P, D).transpose(0, 2, 1, 3))
    qkvh = A(qkv[:, :2 * D].reshape(NCD, P, 32, P).transpose(2, 1, 0, 3))

    in_maps = []
    for core in range(8):
        b = core // 2
        par = core % 2
        csel = np.zeros((1, 2), dtype=np.float32)
        csel[0, par] = 1.0
        in_maps.append({
            "x": A(x[b]),
            "qkv": qkv, "wo": wo, "gamma": gamma, "beta": beta,
            "router": router,
            "qkvh": qkvh, "w13h": w13h, "w2h": w2h, "sguph": sguph,
            "sdh": sdh,
            "csel": csel,
        })
    return in_maps


def assemble_output(results):
    out = np.empty((4, SEQ, D), dtype=np.float32)
    for core in range(8):
        b = core // 2
        par = core % 2
        oc = results[core]["out"]
        for k in range(NT_OWN):
            out[b, 256 * k + P * par:256 * k + P * par + P, :] = \
                oc[k * P:(k + 1) * P, :]
    return out


def kernel(**inputs):
    in_maps = prep_inputs(**inputs)
    nc = build_nc()
    res = run_bass_kernel_spmd(nc, in_maps, core_ids=list(range(8)))
    return assemble_output(res.results)
